# version 13
# baseline (speedup 1.0000x reference)
import numpy as np
import concourse.bass as bass
import concourse.mybir as mybir
from concourse.bass_utils import run_bass_kernel_spmd

F32 = mybir.dt.float32
BF16 = mybir.dt.bfloat16
I32 = mybir.dt.int32
AF = mybir.ActivationFunctionType
ALU = mybir.AluOpType
AX = mybir.AxisListType

RSQRT_MAGIC = 0x5F3759DF
RND_MAGIC = 12582912.0
DEBUG_DUMPS = False
NCORE = 8
B_PER_CORE = 4
L = 2048
D = 1024
NT = L // 128


class Prog:
    ENGS = ("sync", "tensor", "vector", "scalar", "gpsimd")

    def __init__(self, nc):
        self.nc = nc
        self.steps = {e: [] for e in self.ENGS}
        self.sems = {}
        self.counts = {}
        self.evts = {}

    def sem(self, name):
        if name not in self.sems:
            self.sems[name] = self.nc.alloc_semaphore(name)
            self.counts[name] = 0
        return self.sems[name]

    AUTO_VEC = True

    def _auto_vec_wait(self, eng):
        if eng == "vector" and self.AUTO_VEC:
            self.sem("dve")
            s, cnt = self.sems["dve"], self.counts["dve"]
            if cnt > 0:
                self.steps[eng].append(lambda e, s=s, cnt=cnt: e.wait_ge(s, cnt))

    def onp(self, fn, evt=None):
        self.sem("pool")
        s, cnt = self.sems["pool"], self.counts["pool"]
        if cnt > 0:
            self.steps["gpsimd"].append(lambda e, s=s, cnt=cnt: e.wait_ge(s, cnt))
        self.inc("gpsimd", fn, "pool", 1, evt=evt)

    def on(self, eng, fn):
        if eng == "vector" and self.AUTO_VEC:
            self.inc(eng, fn, "dve", 1)
            return
        self.steps[eng].append(fn)

    def inc(self, eng, fn, sem, amt=1, evt=None):
        self._auto_vec_wait(eng)
        self.sem(sem)
        self.counts[sem] += amt
        cnt = self.counts[sem]
        s = self.sems[sem]

        def wrapped(e, fn=fn, s=s, amt=amt):
            fn(e).then_inc(s, amt)

        self.steps[eng].append(wrapped)
        if evt is not None:
            self.evts[evt] = (sem, cnt)
        return cnt

    def evt_here(self, sem, evt):
        self.sem(sem)
        self.evts[evt] = (sem, self.counts[sem])

    def wait_evt(self, eng, evt):
        sem, cnt = self.evts[evt]
        s = self.sems[sem]
        self.steps[eng].append(lambda e, s=s, cnt=cnt: e.wait_ge(s, cnt))

    def wait_sem_now(self, eng, sem):
        self.sem(sem)
        s, cnt = self.sems[sem], self.counts[sem]
        if cnt > 0:
            self.steps[eng].append(lambda e, s=s, cnt=cnt: e.wait_ge(s, cnt))

    def self_wait(self, eng, sem):
        self.sem(sem)
        s, cnt = self.sems[sem], self.counts[sem]
        if cnt > 0:
            self.steps[eng].append(lambda e, s=s, cnt=cnt: e.wait_ge(s, cnt))

    def dump(self, name, ap, evt):
        t = self.nc.dram_tensor(f"dbg_{name}", list(ap.shape), ap.dtype, kind="ExternalOutput").ap()
        self.wait_evt("gpsimd", evt)
        self.inc("gpsimd", lambda e, t=t, ap=ap: e.dma_start(out=t, in_=ap), "dbgst", 16)

    def finish_dumps(self):
        self.wait_sem_now("gpsimd", "dbgst")

    def run(self):
        with self.nc.Block() as block:
            for name in self.ENGS:
                def runner(eng, name=name):
                    for f in self.steps[name]:
                        f(eng)
                getattr(block, name)(runner)


def quake_rsqrt(p, v_ap, out_ap, scr_i32_ap, scr_f32_ap, half_ap, magic_tile_ap, iters=3):
    p.on("vector", lambda v: v.tensor_single_scalar(scr_i32_ap, v_ap.bitcast(I32), 1,
                                                    ALU.arith_shift_right))
    p.on("vector", lambda v: v.tensor_sub(out_ap.bitcast(I32), magic_tile_ap, scr_i32_ap))
    p.on("vector", lambda v: v.tensor_scalar_mul(half_ap, v_ap, 0.5))
    for _ in range(iters):
        p.on("vector", lambda v: v.tensor_mul(scr_f32_ap, out_ap, out_ap))
        p.on("vector", lambda v: v.tensor_mul(scr_f32_ap, scr_f32_ap, half_ap))
        p.on("vector", lambda v: v.tensor_scalar(scr_f32_ap, scr_f32_ap, -1.0, 1.5, ALU.mult, ALU.add))
        p.on("vector", lambda v: v.tensor_mul(out_ap, out_ap, scr_f32_ap))


class LNPack:
    def __init__(self, nc, pfx=""):
        a = nc.alloc_sbuf_tensor
        self.lsum = a(pfx + "lsum", [128, 1], F32)
        self.lsq = a(pfx + "lsq", [128, 1], F32)
        self.lmean = a(pfx + "lmean", [128, 1], F32)
        self.lmz = a(pfx + "lmz", [128, 1], F32)
        self.lvar = a(pfx + "lvar", [128, 1], F32)
        self.lu = a(pfx + "lu", [128, 1], F32)
        self.lneg = a(pfx + "lneg", [128, 1], F32)
        self.linv = a(pfx + "linv", [128, 1], F32)
        self.prev_store = None

    def init(self, p):
        pass


def emit_ln_store(p, ln, s_ln_ap, junk_ap, store_list, s_evt, tag, n=1024, eps=1e-5,
                  war_evt="auto"):
    rn = 1.0 / n
    p.wait_evt("vector", s_evt)
    if war_evt == "auto":
        war_evt = ln.prev_store
    if war_evt is not None:
        p.wait_evt("vector", war_evt)
    p.inc("vector", lambda v: v.reduce_sum(ln.lsum[:], s_ln_ap, axis=AX.X), "dve", 1)
    p.inc("vector", lambda v: v.tensor_mul(junk_ap, s_ln_ap, s_ln_ap), "dve", 1)
    p.inc("vector", lambda v: v.reduce_sum(ln.lsq[:], junk_ap, axis=AX.X), "dve", 1)
    p.inc("vector", lambda v: v.tensor_scalar_mul(ln.lmean[:], ln.lsum[:], rn), "dve", 1)
    p.inc("vector", lambda v: v.tensor_mul(ln.lmz[:], ln.lmean[:], ln.lmean[:]), "dve", 1)
    p.inc("vector", lambda v: v.tensor_scalar(ln.lvar[:], ln.lsq[:], rn, eps,
                                              ALU.mult, ALU.add), "dve", 1)
    p.inc("vector", lambda v: v.tensor_sub(ln.lvar[:], ln.lvar[:], ln.lmz[:]), "dve", 1,
          evt=f"{tag}_LC")
    p.wait_evt("scalar", f"{tag}_LC")
    p.inc("scalar", lambda s: s.activation(ln.lu[:], ln.lvar[:], AF.Ln), "act", 1)
    p.self_wait("scalar", "act")
    p.inc("scalar", lambda s: s.activation(ln.linv[:], ln.lu[:], AF.Exp, scale=-0.5),
          "act", 1, evt=f"{tag}_LF")
    p.wait_evt("vector", f"{tag}_LF")
    p.inc("vector", lambda v: v.scalar_tensor_tensor(
        junk_ap, s_ln_ap, ln.lmean[:, 0:1],
        ln.linv[:, 0:1].broadcast_to([128, n]), ALU.subtract, ALU.mult),
        "dve", 1, evt=f"{tag}_norm")
    p.wait_evt("sync", f"{tag}_norm")
    for dst in store_list:
        p.inc("sync", lambda e, dst=dst: e.dma_start(out=dst, in_=junk_ap), "st", 16,
              evt=f"{tag}_store")
    ln.prev_store = f"{tag}_store"


def emit_k1(nc, p, x_in, x1_out, W, ident):
    a = nc.alloc_sbuf_tensor
    ap_ = nc.alloc_psum_tensor

    id_t = a("id_t", [128, 128], BF16)
    wT = {n: a(f"{n}Ts", [128, 8 * 1024], BF16) for n in ("Wq", "Wk", "Wv", "Wo")}
    brow = {n: a(f"{n}_row", [1, 1024], BF16) for n in ("bq", "bk", "bv", "bo")}
    onesl = a("onesl", [1, 128], BF16)
    ones_col_bf = a("ones_col_bf", [128, 1], BF16)
    xres = a("xres", [128, NT * 1024], F32)
    phiQT = a("phiQT", [128, NT * 1024], BF16)
    xb0 = a("xb0", [128, 1024], BF16)
    xb = [xb0, xb0]
    xT = [a(f"xT{i}", [128, 1024], BF16) for i in range(2)]
    phiK = [a(f"phiK{i}", [128, 1024], BF16) for i in range(2)]
    Vt = [a(f"Vt{i}", [128, 1024], BF16) for i in range(2)]
    ctx = [a(f"ctx{i}", [128, 1024], BF16) for i in range(2)]
    ctxT0 = a("ctxT0", [128, 1024], BF16)
    ctxT = [ctxT0, ctxT0]
    t0 = a("t0", [128, 1024], F32)
    scr2 = a("scr2", [128, 1024], F32)
    s_ln = a("s_ln", [128, 1024], F32)
    junk0 = a("junk", [128, 1024], F32)
    junkb = [junk0, t0]
    k1_jstore = [None, None]
    S_bd = a("S_bd", [128, 8 * 128], BF16)
    z2 = a("z2", [128, 16], BF16)
    dnr = a("dnr", [128, 16], F32)
    dnr2 = a("dnr2", [128, 16], F32)
    qs_f = a("qs_f", [128, 16], F32)
    qs_h = a("qs_h", [128, 16], F32)
    ln = LNPack(nc)

    P_tp = ap_("P_tp", [128, 512], F32)
    P_kq = ap_("P_kq", [128, 1024], F32)
    P_v = ap_("P_v", [128, 1024], F32)
    P_s = ap_("P_s", [64, 1024], F32)
    P_z = ap_("P_z", [128, 512], F32)
    tp_bf = P_tp.ap().bitcast(BF16)

    p.inc("sync", lambda e: e.dma_start(out=id_t[:], in_=ident), "init", 16, evt="id")
    ln.init(p)
    p.on("vector", lambda v: v.memset(onesl[:], 1.0))
    p.on("vector", lambda v: v.memset(ones_col_bf[:], 1.0))
    p.on("vector", lambda v: v.memset(S_bd[:], 0.0))
    p.inc("vector", lambda v: v.memset(z2[:], 0.0), "dve", 1, evt="init_dve")
    for n in ("bq", "bk", "bv", "bo"):
        p.inc("gpsimd", lambda e, n=n: e.dma_start(out=brow[n][0:1, :], in_=W[n][None, :]), "bias", 16)
    for n in ("Wq", "Wk", "Wv", "Wo"):
        p.inc("gpsimd", lambda e, n=n: e.dma_start(out=wT[n][:], in_=W[n + "T"]), "wst", 16)
    p.wait_sem_now("tensor", "bias")
    p.wait_sem_now("tensor", "wst")
    p.wait_evt("tensor", "init_dve")
    p.wait_evt("tensor", "id")
    p.evt_here("pe", "prep_done")

    def tp_transpose(src_bf_ap, dst_bf_ap, nch, tag, pre_evac_wait=None):
        for g in range((nch + 3) // 4):
            njj = min(4, nch - g * 4)
            for jj in range(njj):
                j = g * 4 + jj
                p.inc("tensor",
                      lambda t, j=j, jj=jj: t.matmul(
                          tp_bf[:, jj * 128:(jj + 1) * 128],
                          src_bf_ap[:, j * 128:(j + 1) * 128], id_t[:],
                          start=(jj == 0), stop=True, is_transpose=True,
                          skip_group_check=True),
                      "pe", 1, evt=f"{tag}_tp{g}")
            p.wait_evt("vector", f"{tag}_tp{g}")
            if pre_evac_wait is not None and g == 0:
                p.wait_evt("vector", pre_evac_wait)
            p.inc("vector",
                  lambda v, g=g, njj=njj: v.tensor_copy(
                      dst_bf_ap[:, g * 4 * 128:(g * 4 + njj) * 128],
                      tp_bf[:, 0:njj * 128]),
                  "dve", 1, evt=f"{tag}_ev")
            p.wait_evt("tensor", f"{tag}_ev")

    def proj_matmuls(ps_ap, xT_ap, wTn, brow_n, tag):
        for h in range(2):
            for j in range(8):
                p.inc("tensor",
                      lambda t, h=h, j=j: t.matmul(
                          ps_ap[:, h * 512:(h + 1) * 512],
                          xT_ap[:, j * 128:(j + 1) * 128],
                          wT[wTn][:, j * 1024 + h * 512: j * 1024 + (h + 1) * 512],
                          start=(j == 0), stop=False, skip_group_check=True),
                      "pe", 1)
            p.inc("tensor",
                  lambda t, h=h: t.matmul(
                      ps_ap[:, h * 512:(h + 1) * 512],
                      onesl[:],
                      brow[brow_n][:, h * 512:(h + 1) * 512],
                      start=False, stop=True, skip_group_check=True),
                  "pe", 1, evt=f"{tag}_h{h}")

    def elu1(ps_ap, out_bf_ap, tag):
        p.inc("vector", lambda v: v.tensor_single_scalar(scr2[:], ps_ap, 0.0, ALU.min),
              "dve", 1, evt=f"{tag}_min")
        p.inc("vector", lambda v: v.tensor_single_scalar(t0[:], ps_ap, 0.0, ALU.max),
              "dve", 1, evt=f"{tag}_max")
        p.wait_evt("scalar", f"{tag}_min")
        p.inc("scalar", lambda s: s.activation(scr2[:], scr2[:], AF.Exp), "act", 1, evt=f"{tag}_exp")
        p.wait_evt("vector", f"{tag}_exp")
        p.wait_evt("vector", f"{tag}_max")
        p.inc("vector", lambda v: v.tensor_add(out_bf_ap, scr2[:], t0[:]), "dve", 1, evt=f"{tag}_out")

    nb = x_in.shape[0]
    for b in range(nb):
        for i in range(NT):
            ib = i % 2
            if b > 0 and i == 0:
                p.wait_evt("sync", f"k1_b{b - 1}_done")
            p.inc("sync", lambda e, b=b, i=i: e.dma_start(
                out=xres[:, i * 1024:(i + 1) * 1024], in_=x_in[b, i * 128:(i + 1) * 128, :]),
                f"xch{i}", 16, evt=f"x_{b}_{i}")
            p.wait_evt("vector", f"x_{b}_{i}")
            if i > 0:
                p.wait_evt("vector", f"k1xT_{b}_{i - 1}_ev")
            p.inc("vector", lambda v, ib=ib, i=i: v.tensor_copy(xb[ib][:], xres[:, i * 1024:(i + 1) * 1024]),
                  "dve", 1, evt=f"xb_{b}_{i}")
            if b == 0 and i == 0:
                p.wait_evt("tensor", "prep_done")
            p.wait_evt("tensor", f"xb_{b}_{i}")
            tp_transpose(xb[ib][:], xT[ib][:], 8, f"k1xT_{b}_{i}",
                         pre_evac_wait=(f"k1Q_{b}_{i - 2}_h1" if i >= 2 else None))
            if i > 0:
                p.wait_evt("tensor", f"k1Q_{b}_{i - 1}_min")
                p.wait_evt("tensor", f"k1Q_{b}_{i - 1}_max")
            proj_matmuls(P_kq.ap(), xT[ib][:], "Wk", "bk", f"k1K_{b}_{i}")
            p.wait_evt("vector", f"k1K_{b}_{i}_h1")
            if i >= 2:
                p.wait_evt("vector", f"k1SZ_{b}_{i - 2}")
            elu1(P_kq.ap(), phiK[ib][:], f"k1K_{b}_{i}")
            if i > 0:
                p.wait_evt("tensor", f"k1V_{b}_{i - 1}_out")
            proj_matmuls(P_v.ap(), xT[ib][:], "Wv", "bv", f"k1V_{b}_{i}")
            p.wait_evt("vector", f"k1V_{b}_{i}_h1")
            p.inc("vector", lambda v, ib=ib: v.tensor_copy(Vt[ib][:], P_v.ap()[:]),
                  "dve", 1, evt=f"k1V_{b}_{i}_out")
            p.wait_evt("tensor", f"k1K_{b}_{i}_out")
            p.wait_evt("tensor", f"k1V_{b}_{i}_out")
            if b > 0 and i == 0:
                p.wait_evt("tensor", f"k1Sevac_{b - 1}")
                p.wait_evt("tensor", f"k1ctx_{b - 1}_{NT - 1}")
            for h in range(16):
                p.inc("tensor", lambda t, h=h, i=i: t.matmul(
                    P_s.ap()[0:64, h * 64:(h + 1) * 64],
                    phiK[i % 2][:, h * 64:(h + 1) * 64],
                    Vt[i % 2][:, h * 64:(h + 1) * 64],
                    start=(i == 0 and h % 8 == 0), stop=(i == NT - 1),
                    skip_group_check=True),
                    "pe", 1)
            for j in range(8):
                p.inc("tensor", lambda t, j=j, i=i: t.matmul(
                    P_z.ap()[:, j:j + 1],
                    phiK[i % 2][:, j * 128:(j + 1) * 128],
                    ones_col_bf[:],
                    start=(i == 0 and j == 0), stop=(i == NT - 1),
                    skip_group_check=True),
                    "pe", 1, evt=(f"k1SZ_{b}_{i}" if j == 7 else None))
            p.wait_evt("tensor", f"k1K_{b}_{i}_min")
            p.wait_evt("tensor", f"k1K_{b}_{i}_max")
            proj_matmuls(P_kq.ap(), xT[ib][:], "Wq", "bq", f"k1Q_{b}_{i}")
            p.wait_evt("vector", f"k1Q_{b}_{i}_h1")
            elu1(P_kq.ap(), ctx[ib][:], f"k1Q_{b}_{i}")
            p.wait_evt("tensor", f"k1Q_{b}_{i}_out")
            if DEBUG_DUMPS and b == 0 and i == 0:
                p.dump("xT0", xT[0][:], f"k1xT_{b}_{i}_ev")
                p.dump("phiK0", phiK[0][:], f"k1K_{b}_{i}_out")
                p.dump("Vt0", Vt[0][:], f"k1V_{b}_{i}_out")
                p.dump("phiQ0", ctx[0][:], f"k1Q_{b}_{i}_out")
            tp_transpose(ctx[ib][:], phiQT.ap()[:, i * 1024:(i + 1) * 1024], 8, f"k1qT_{b}_{i}")
        p.wait_evt("vector", f"k1SZ_{b}_{NT - 1}")
        for hp in range(8):
            p.inc("vector", lambda v, hp=hp: v.tensor_copy(
                S_bd[0:64, hp * 128:hp * 128 + 64], P_s.ap()[0:64, (2 * hp) * 64:(2 * hp + 1) * 64]),
                "dve", 1)
            p.inc("vector", lambda v, hp=hp: v.tensor_copy(
                S_bd[64:128, hp * 128 + 64:hp * 128 + 128],
                P_s.ap()[0:64, (2 * hp + 1) * 64:(2 * hp + 2) * 64]),
                "dve", 1)
            p.inc("vector", lambda v, hp=hp: v.tensor_copy(
                z2[0:64, 2 * hp:2 * hp + 1], P_z.ap()[0:64, hp:hp + 1]), "dve", 1)
            p.inc("vector", lambda v, hp=hp: v.tensor_copy(
                z2[64:128, 2 * hp + 1:2 * hp + 2], P_z.ap()[64:128, hp:hp + 1]), "dve", 1,
                evt=(f"k1Sevac_{b}v" if hp == 7 else None))
        p.wait_evt("tensor", f"k1Sevac_{b}v")
        p.inc("tensor", lambda t: t.nop(), "pe", 1, evt=f"k1Sevac_{b}")
        if DEBUG_DUMPS and b == 0:
            p.dump("S_bd", S_bd[:], f"k1Sevac_{b}v")
            p.dump("z2", z2[:], f"k1Sevac_{b}v")
            p.dump("phiQT", phiQT[:], f"k1Sevac_{b}v")
        for i in range(NT):
            ib = i % 2
            if i > 0:
                p.wait_evt("tensor", f"k1ctx_{b}_{i - 1}")
            for hp in range(8):
                p.inc("tensor", lambda t, hp=hp, i=i: t.matmul(
                    P_kq.ap()[:, hp * 128:(hp + 1) * 128],
                    phiQT.ap()[:, i * 1024 + hp * 128: i * 1024 + (hp + 1) * 128],
                    S_bd[:, hp * 128:(hp + 1) * 128],
                    start=(hp % 4 == 0), stop=True, skip_group_check=True),
                    "pe", 1)
            for hp in range(8):
                p.inc("tensor", lambda t, hp=hp, i=i: t.matmul(
                    P_z.ap()[:, 16 + hp * 2:16 + (hp + 1) * 2],
                    phiQT.ap()[:, i * 1024 + hp * 128: i * 1024 + (hp + 1) * 128],
                    z2[:, hp * 2:(hp + 1) * 2],
                    start=(hp == 0), stop=True, skip_group_check=True),
                    "pe", 1, evt=(f"k1nd_{b}_{i}" if hp == 7 else None))
            p.wait_evt("vector", f"k1nd_{b}_{i}")
            p.inc("vector", lambda v: v.tensor_single_scalar(dnr[:], P_z.ap()[:, 16:32], 1e-6, ALU.add),
                  "dve", 1, evt=f"k1dn_{b}_{i}a")
            p.wait_evt("scalar", f"k1dn_{b}_{i}a")
            p.inc("scalar", lambda s: s.copy(qs_f[:], dnr[:]), "act", 1, evt=f"k1dn_{b}_{i}b")
            p.wait_evt("vector", f"k1dn_{b}_{i}b")
            p.inc("vector", lambda v: v.reciprocal(dnr2[:], qs_f[:]), "dve", 1, evt=f"k1dn_{b}_{i}c")
            p.wait_evt("scalar", f"k1dn_{b}_{i}c")
            p.inc("scalar", lambda s: s.copy(qs_h[:], dnr2[:]), "act", 1, evt=f"k1dn_{b}_{i}d")
            p.wait_evt("vector", f"k1dn_{b}_{i}d")
            p.inc("vector", lambda v, ib=ib: v.tensor_tensor(
                ctx[ib].ap().rearrange("q (h j) -> q h j", h=16),
                P_kq.ap().rearrange("q (h j) -> q h j", h=16),
                qs_h.ap().rearrange("q (h o) -> q h o", h=16).broadcast_to([128, 16, 64]),
                ALU.mult),
                "dve", 1, evt=f"k1ctx_{b}_{i}v")
            p.wait_evt("tensor", f"k1ctx_{b}_{i}v")
            p.inc("tensor", lambda t: t.nop(), "pe", 1, evt=f"k1ctx_{b}_{i}")
            if DEBUG_DUMPS and b == 0 and i == 0:
                p.dump("ctx0", ctx[0][:], f"k1ctx_{b}_{i}v")
                p.dump("dnr0", qs_h[:], f"k1ctx_{b}_{i}v")
            tp_transpose(ctx[ib][:], ctxT[ib][:], 8, f"k1cT_{b}_{i}",
                         pre_evac_wait=(f"k1o_{b}_{i - 1}_h1" if i >= 1 else None))
            if i > 0:
                p.wait_evt("tensor", f"k1s_{b}_{i - 1}")
            proj_matmuls(P_v.ap(), ctxT[ib][:], "Wo", "bo", f"k1o_{b}_{i}")
            p.wait_evt("vector", f"k1o_{b}_{i}_h1")
            p.inc("vector", lambda v, i=i: v.tensor_add(s_ln[:], P_v.ap()[:],
                                                        xres[:, i * 1024:(i + 1) * 1024]),
                  "dve", 1, evt=f"k1s_{b}_{i}v")
            p.wait_evt("tensor", f"k1s_{b}_{i}v")
            p.inc("tensor", lambda t: t.nop(), "pe", 1, evt=f"k1s_{b}_{i}")
            if DEBUG_DUMPS and b == 0 and i == 0:
                p.dump("sln0", s_ln[:], f"k1s_{b}_{i}v")
            emit_ln_store(p, ln, s_ln[:], junkb[i % 2][:],
                          [x1_out[b, i * 128:(i + 1) * 128, :]],
                          f"k1s_{b}_{i}v", f"k1ln_{b}_{i}", war_evt=k1_jstore[i % 2])
            k1_jstore[i % 2] = f"k1ln_{b}_{i}_store"
        p.evt_here("st", f"k1_b{b}_done")
    p.wait_sem_now("sync", "st")
    p.finish_dumps()


def build_k1():
    nc = bass.Bass("TRN2", target_bir_lowering=False)
    x_in = nc.dram_tensor("x0", [B_PER_CORE, L, D], F32, kind="ExternalInput").ap()
    x1_out = nc.dram_tensor("x1", [B_PER_CORE, L, D], F32, kind="ExternalOutput").ap()
    W = {}
    for n in ("Wq", "Wk", "Wv", "Wo"):
        W[n + "T"] = nc.dram_tensor(n + "T", [128, 8192], F32, kind="ExternalInput").ap()
    for n in ("bq", "bk", "bv", "bo"):
        W[n] = nc.dram_tensor(n, [1024], F32, kind="ExternalInput").ap()
    ident = nc.dram_tensor("ident", [128, 128], BF16, kind="ExternalInput").ap()
    p = Prog(nc)
    emit_k1(nc, p, x_in, x1_out, W, ident)
    p.run()
    return nc


def _ident_bf16():
    import ml_dtypes
    return np.eye(128, dtype=ml_dtypes.bfloat16)


def k1_inputs(inputs, core):
    x = np.ascontiguousarray(inputs["L0_attn_query"][core * B_PER_CORE:(core + 1) * B_PER_CORE])
    d = {"x0": x, "ident": _ident_bf16()}
    for n, key in (("Wq", "L0_attn_Wq"), ("Wk", "L0_attn_Wk"), ("Wv", "L0_attn_Wv"), ("Wo", "L0_attn_Wo")):
        w = np.asarray(inputs[key], dtype=np.float32)
        d[n + "T"] = np.ascontiguousarray(w.reshape(1024, 8, 128).transpose(2, 1, 0).reshape(128, 8192))
    for n, key in (("bq", "L0_attn_bq"), ("bk", "L0_attn_bk"), ("bv", "L0_attn_bv"), ("bo", "L0_attn_bo")):
        d[n] = np.ascontiguousarray(np.asarray(inputs[key], dtype=np.float32))
    return d




def emit_k2(nc, p, x1_in, x2_out, W2, ident):
    a = nc.alloc_sbuf_tensor
    ap_ = nc.alloc_psum_tensor
    NTT = B_PER_CORE * NT
    NP = NTT // 2

    id_t = a("k2_id", [128, 128], BF16)
    id_f = a("k2_idf", [128, 128], F32)
    keys_bf = a("k2_keys", [128, 16 * 8 * 256], BF16)
    vals_bf = a("k2_vals", [128, 16 * 2 * 1024], BF16)
    selT = a("k2_selT", [128, 128], F32)
    rs_col = a("k2_rs", [128, 1], F32)
    comb = a("k2_comb", [128, NTT * 16], F32)
    xres1 = [a(f"k2_xres{i}", [128, 2048], F32) for i in range(2)]
    xb = a("k2_xb", [128, 1024], BF16)
    xTf = a("k2_xTf", [128, 1024], F32)
    x1T = [a(f"k2_x1T{i}", [128, 8 * 256], BF16) for i in range(2)]
    scT = [a(f"k2_scT{i}", [128, 512], BF16) for i in range(2)]
    acc = [a(f"k2_acc{i}", [128, 2048], F32) for i in range(2)]
    s_ln = a("k2_sln", [128, 1024], F32)
    junk = a("k2_junk", [128, 1024], F32)
    junk2 = a("k2_junk2", [128, 1024], F32)
    junkb = [junk, junk2]
    k2_jstore = [None, None]
    probs = a("k2_probs", [128, 16], F32)
    mx = a("k2_mx", [128, 8], F32)
    mxb = a("k2_mxb", [128, 8], F32)
    wsel = a("k2_wsel", [128, 16], F32)
    wselb = a("k2_wselb", [128, 16], F32)
    grc2 = a("k2_grc2", [128, 1], F32)
    grcb2 = a("k2_grcb2", [128, 1], F32)
    gsum = a("k2_gsum", [128, 1], F32)
    gsumb = a("k2_gsumb", [128, 1], F32)
    grc = a("k2_grc", [128, 1], F32)
    grcb = a("k2_grcb", [128, 1], F32)
    ln = LNPack(nc, "k2_")

    P_tp = ap_("k2P_tp", [128, 512], F32)
    P_sc = [ap_("k2P_sc0", [128, 512], F32), ap_("k2P_sc1", [128, 512], F32)]
    P_vo = [ap_(f"k2P_vo{i}", [128, 1024], F32) for i in range(2)]
    P_sel = ap_("k2P_sel", [128, 512], F32)
    tp_bf = P_tp.ap().bitcast(BF16)

    p.inc("sync", lambda e: e.dma_start(out=id_t[:], in_=ident), "init", 16, evt="k2id")
    p.inc("sync", lambda e: e.dma_start(out=id_f[:], in_=W2["ident_f"]), "init2", 16, evt="k2idf")
    p.inc("gpsimd", lambda e: e.dma_start(out=rs_col[:], in_=W2["rs_col"]), "wrs", 16)
    p.inc("gpsimd", lambda e: e.dma_start(out=selT[:], in_=W2["selT"]), "wsel", 16,
          evt="k2seln")
    for ee in range(16):
        p.inc("gpsimd", lambda e, ee=ee: e.dma_start(
            out=keys_bf.ap()[:, ee * 8 * 256:(ee + 1) * 8 * 256].rearrange("q (j c) -> q j c", j=8),
            in_=W2["keys"][ee].rearrange("(j q) h -> q j h", q=128)), "wkeys", 16, evt="k2keys")
    p.inc("gpsimd", lambda e: e.dma_start(
        out=vals_bf.ap().rearrange("q (e c f) -> q e c f", e=16, c=2),
        in_=W2["values"].rearrange("e (c q) f -> q e c f", q=128)), "wvals", 16, evt="k2vals")
    p.wait_evt("tensor", "k2seln")
    p.wait_evt("tensor", "k2idf")
    p.wait_evt("tensor", "k2id")

    def tile_xy(ti):
        b, i = divmod(ti, NT)
        return x1_in[b, i * 128:(i + 1) * 128, :], x2_out[b, i * 128:(i + 1) * 128, :]

    p.wait_sem_now("scalar", "wrs")
    for ti in range(NTT):
        src, _ = tile_xy(ti)
        if ti > 0:
            p.wait_evt("sync", f"k2g_{ti - 1}_tp1")
        p.inc("sync", lambda e, src=src: e.dma_start(out=junk[:], in_=src), f"k2g{ti % 2}", 16,
              evt=f"k2g_{ti}_ld")
        for g in range(2):
            p.wait_evt("tensor", f"k2g_{ti}_ld")
            if ti > 0 and g == 0:
                p.wait_evt("tensor", f"k2g_{ti - 1}_xT")
            for jj in range(4):
                j = g * 4 + jj
                p.inc("tensor", lambda t, j=j, jj=jj: t.matmul(
                    P_tp.ap()[:, jj * 128:(jj + 1) * 128], junk[:, j * 128:(j + 1) * 128], id_f[:],
                    start=(jj == 0), stop=True, is_transpose=True, skip_group_check=True),
                    "pe", 1, evt=f"k2g_{ti}_tp{g}")
            p.wait_evt("vector", f"k2g_{ti}_tp{g}")
            if ti > 0 and g == 0:
                p.wait_evt("vector", f"k2g_{ti - 1}_mm")
            p.inc("vector", lambda v, g=g: v.tensor_copy(
                xTf[:, g * 512:(g + 1) * 512], P_tp.ap()[:]),
                "dve", 1, evt=f"k2g_{ti}_xT")
            p.wait_evt("tensor", f"k2g_{ti}_xT")
        if ti > 0:
            p.wait_evt("tensor", f"k2g_{ti - 1}_sig")
        for j in range(8):
            p.inc("tensor", lambda t, j=j: t.matmul(
                P_sel.ap()[:, 0:16],
                xTf[:, j * 128:(j + 1) * 128],
                selT[:, j * 16:(j + 1) * 16],
                start=(j == 0), stop=(j == 7), skip_group_check=True),
                "pe", 1, evt=f"k2g_{ti}_mm")
        p.wait_evt("scalar", f"k2g_{ti}_mm")
        p.inc("scalar", lambda s: s.activation(probs[:], P_sel.ap()[:, 0:16], AF.Sigmoid),
              "act", 1, evt=f"k2g_{ti}_sig")
        p.wait_evt("vector", f"k2g_{ti}_sig")
        p.inc("vector", lambda v: v.max(mx[:], probs[:]), "dve", 1, evt=f"k2g_{ti}_mx")
        p.wait_evt("scalar", f"k2g_{ti}_mx")
        p.inc("scalar", lambda s: s.copy(mxb[:], mx[:]), "act", 1, evt=f"k2g_{ti}_mxb")
        p.wait_evt("vector", f"k2g_{ti}_mxb")
        p.inc("vector", lambda v: v.scalar_tensor_tensor(
            wsel[:], probs[:], mxb[:, 3:4], probs[:], ALU.is_ge, ALU.mult),
            "dve", 1, evt=f"k2g_{ti}_ws")
        p.wait_evt("scalar", f"k2g_{ti}_ws")
        p.inc("scalar", lambda s: s.copy(wselb[:], wsel[:]), "act", 1, evt=f"k2g_{ti}_wsb")
        p.wait_evt("vector", f"k2g_{ti}_wsb")
        p.inc("vector", lambda v: v.reduce_sum(gsum[:], wselb[:], axis=AX.X), "dve", 1,
              evt=f"k2g_{ti}_gs")
        p.wait_evt("scalar", f"k2g_{ti}_gs")
        p.inc("scalar", lambda s: s.activation(gsumb[:], gsum[:], AF.Relu), "act", 1,
              evt=f"k2g_{ti}_gsb")
        p.wait_evt("vector", f"k2g_{ti}_gsb")
        p.inc("vector", lambda v: v.tensor_scalar_max(grc[:], gsumb[:], 1e-9), "dve", 1,
              evt=f"k2g_{ti}_cl")
        p.wait_evt("scalar", f"k2g_{ti}_cl")
        p.inc("scalar", lambda s: s.copy(grcb[:], grc[:]), "act", 1, evt=f"k2g_{ti}_clb")
        p.wait_evt("vector", f"k2g_{ti}_clb")
        p.inc("vector", lambda v: v.reciprocal(grc2[:], grcb[:]), "dve", 1, evt=f"k2g_{ti}_rc")
        p.wait_evt("scalar", f"k2g_{ti}_rc")
        p.inc("scalar", lambda s: s.mul(grcb2[:], grc2[:], rs_col[:, 0:1]), "act", 1,
              evt=f"k2g_{ti}_rcb")
        p.wait_evt("vector", f"k2g_{ti}_rcb")
        p.inc("vector", lambda v, ti=ti: v.tensor_tensor(
            comb[:, ti * 16:(ti + 1) * 16], wsel[:],
            grcb2[:, 0:1].broadcast_to([128, 16]), ALU.mult),
            "dve", 1, evt=f"k2g_{ti}_done")
        if DEBUG_DUMPS and ti == 0:
            p.dump("k2_probs0", probs[:], f"k2g_{ti}_sig")
            p.dump("k2_mx0", mx[:], f"k2g_{ti}_mx")
            p.dump("k2_wsel0", wsel[:], f"k2g_{ti}_gs")
    p.wait_sem_now("tensor", "wkeys")
    p.wait_sem_now("tensor", "wvals")
    p.evt_here("dve", "k2gates_done")
    if DEBUG_DUMPS:
        p.dump("k2_comb", comb[:, 0:NTT * 16], "k2gates_done")

    for pr in range(NP):
        pb = pr % 2
        tiles = (2 * pr, 2 * pr + 1)
        for k, ti in enumerate(tiles):
            src, _ = tile_xy(ti)
            if pr >= 2:
                p.wait_evt("sync", f"k2ln_{2 * (pr - 2) + k}_store")
            p.inc("sync", lambda e, src=src, pb=pb, k=k: e.dma_start(
                out=xres1[pb][:, k * 1024:(k + 1) * 1024], in_=src), f"k2x{pb}{k}", 16,
                evt=f"k2x_{ti}")
        for k, ti in enumerate(tiles):
            p.wait_evt("vector", f"k2x_{ti}")
            if pr >= 1:
                p.wait_evt("vector", f"k2scm_{pr - 1}_15")
            p.inc("vector", lambda v, pb=pb, k=k: v.tensor_copy(xb[:], xres1[pb][:, k * 1024:(k + 1) * 1024]),
                  "dve", 1, evt=f"k2cast_{ti}")
            if pr == 0 and k == 0:
                p.wait_evt("tensor", "k2gates_done")
            p.wait_evt("tensor", f"k2cast_{ti}")
            for j in range(8):
                p.inc("tensor", lambda t, j=j: t.matmul(
                    tp_bf[:, (j % 4) * 128:((j % 4) + 1) * 128],
                    xb[:, j * 128:(j + 1) * 128], id_t[:],
                    start=(j % 4 == 0), stop=True, is_transpose=True, skip_group_check=True),
                    "pe", 1, evt=f"k2tp_{ti}_{j}")
                if j % 4 == 3:
                    p.wait_evt("vector", f"k2tp_{ti}_{j}")
                    if pr >= 1 and j == 3 and k == 0:
                        p.wait_evt("vector", f"k2vo_{pr - 1}_done")
                    p.inc("vector", lambda v, pb=pb, k=k, j4=(j // 4) * 4: v.tensor_copy(
                        x1T[pb].ap().rearrange("q (j c) -> q j c", j=8)[:, j4:j4 + 4, k * 128:(k + 1) * 128],
                        tp_bf[:, 0:512].rearrange("q (j c) -> q j c", j=4)),
                        "dve", 1, evt=f"k2xT_{ti}_ev")
                    p.wait_evt("tensor", f"k2xT_{ti}_ev")
        for e in range(16):
            eb2 = e % 2
            esc = f"k2sc_{pr}_{e - 2}" if e > 1 else (f"k2sc_{pr - 1}_{14 + e}" if pr > 0 else None)
            if esc is not None:
                p.wait_evt("tensor", esc)
            for h in range(2):
                for j in range(8):
                    p.inc("tensor", lambda t, e=e, h=h, j=j, pb=pb, eb2=eb2: t.matmul(
                        P_sc[eb2].ap()[:, h * 256:(h + 1) * 256],
                        keys_bf[:, (e * 8 + j) * 256 + h * 128:(e * 8 + j) * 256 + (h + 1) * 128],
                        x1T[pb][:, j * 256:(j + 1) * 256],
                        start=(j == 0 and h == 0), stop=(j == 7), skip_group_check=True),
                        "pe", 1, evt=(f"k2scm_{pr}_{e}" if h == 1 and j == 7 else None))
            p.wait_evt("scalar", f"k2scm_{pr}_{e}")
            if esc is not None:
                p.wait_evt("scalar", f"k2vo_{pr}_{e - 2}_1" if e > 1 else f"k2vo_{pr - 1}_{14 + e}_1")
            p.inc("scalar", lambda s, eb2=eb2: s.activation(scT[eb2][:], P_sc[eb2].ap()[:], AF.Relu),
                  "act", 1, evt=f"k2sc_{pr}_{e}")
            p.wait_evt("tensor", f"k2sc_{pr}_{e}")
            if DEBUG_DUMPS and pr == 0 and e in (0, 1):
                p.dump(f"k2_scT{e}", scT[:], f"k2sc_{pr}_{e}")
                if e == 0:
                    p.dump("k2_x1T0", x1T[0][:], f"k2sc_{pr}_{e}")
                    p.dump("k2_keys0", keys_bf[:, 0:2048], f"k2sc_{pr}_{e}")
            for k, ti in enumerate(tiles):
                if e > 0:
                    p.wait_evt("tensor", f"k2acc_{pr}_{e - 1}_{k}")
                elif pr > 0:
                    p.wait_evt("tensor", f"k2acc_{pr - 1}_15_{k}")
                for c in range(2):
                    for nh in range(2):
                        p.inc("tensor", lambda t, e=e, k=k, c=c, nh=nh, eb2=eb2: t.matmul(
                            P_vo[k].ap()[:, nh * 512:(nh + 1) * 512],
                            scT[eb2][:, c * 256 + k * 128: c * 256 + (k + 1) * 128],
                            vals_bf[:, (e * 2 + c) * 1024 + nh * 512:(e * 2 + c) * 1024 + (nh + 1) * 512],
                            start=(c == 0), stop=(c == 1), skip_group_check=True),
                            "pe", 1, evt=(f"k2vo_{pr}_{e}_{k}" if c == 1 and nh == 1 else None))
                p.wait_evt("vector", f"k2vo_{pr}_{e}_{k}")
                ccol = comb[:, tiles[k] * 16 + e: tiles[k] * 16 + e + 1]
                if e == 0:
                    p.inc("vector", lambda v, k=k, pb=pb, ccol=ccol: v.tensor_scalar(
                        acc[pb][:, k * 1024:(k + 1) * 1024], P_vo[k].ap()[:], ccol, None, ALU.mult),
                        "dve", 1, evt=f"k2acc_{pr}_{e}_{k}")
                else:
                    p.inc("vector", lambda v, k=k, pb=pb, ccol=ccol: v.scalar_tensor_tensor(
                        acc[pb][:, k * 1024:(k + 1) * 1024], P_vo[k].ap()[:], ccol,
                        acc[pb][:, k * 1024:(k + 1) * 1024], ALU.mult, ALU.add),
                        "dve", 1, evt=f"k2acc_{pr}_{e}_{k}")
        p.evt_here("dve", f"k2vo_{pr}_done")
        if DEBUG_DUMPS and pr == 0:
            p.dump("k2_acc0", acc[0][:, 0:1024], f"k2acc_{pr}_15_0")
        for k, ti in enumerate(tiles):
            _, dst = tile_xy(ti)
            p.wait_evt("vector", f"k2acc_{pr}_15_{k}")
            p.inc("vector", lambda v, pb=pb, k=k: v.tensor_add(
                s_ln[:], acc[pb][:, k * 1024:(k + 1) * 1024], xres1[pb][:, k * 1024:(k + 1) * 1024]),
                "dve", 1, evt=f"k2s_{ti}")
            emit_ln_store(p, ln, s_ln[:], junkb[ti % 2][:], [dst], f"k2s_{ti}", f"k2ln_{ti}",
                          war_evt=k2_jstore[ti % 2])
            k2_jstore[ti % 2] = f"k2ln_{ti}_store"
    p.wait_sem_now("sync", "st")


def build_k2():
    nc = bass.Bass("TRN2", target_bir_lowering=False)
    x1_in = nc.dram_tensor("x1", [B_PER_CORE, L, D], F32, kind="ExternalInput").ap()
    x2_out = nc.dram_tensor("x2", [B_PER_CORE, L, D], F32, kind="ExternalOutput").ap()
    W2 = {
        "selT": nc.dram_tensor("selT", [128, 128], F32, kind="ExternalInput").ap(),
        "keys": nc.dram_tensor("keys", [16, 1024, 256], F32, kind="ExternalInput").ap(),
        "values": nc.dram_tensor("values", [16, 256, 1024], F32, kind="ExternalInput").ap(),
        "rs_col": nc.dram_tensor("rs_col", [128, 1], F32, kind="ExternalInput").ap(),
        "ident_f": nc.dram_tensor("ident_f", [128, 128], F32, kind="ExternalInput").ap(),
    }
    ident = nc.dram_tensor("ident", [128, 128], BF16, kind="ExternalInput").ap()
    p = Prog(nc)
    emit_k2(nc, p, x1_in, x2_out, W2, ident)
    p.run()
    return nc


def k2_inputs(inputs, core, x1):
    d = {
        "x1": (np.ascontiguousarray(x1[core * B_PER_CORE:(core + 1) * B_PER_CORE])
               if x1 is not None else None),
        "selT": np.ascontiguousarray(np.asarray(inputs["L0_ffn_expert_sel"], np.float32)
                                     .reshape(16, 8, 128).transpose(2, 1, 0).reshape(128, 128)),
        "keys": np.ascontiguousarray(inputs["L0_ffn_keys"]),
        "values": np.ascontiguousarray(inputs["L0_ffn_values"]),
        "rs_col": np.ascontiguousarray(np.broadcast_to(
            np.asarray(inputs["L0_ffn_route_scale"], np.float32).reshape(1, 1), (128, 1))),
        "ident": _ident_bf16(),
        "ident_f": np.eye(128, dtype=np.float32),
    }
    return d




TC = 256
NFT = 16
G3 = 8
NG3 = NFT // G3


def emit_k3(nc, p, x2_in, x3_out, W3):
    a = nc.alloc_sbuf_tensor
    ap_ = nc.alloc_psum_tensor
    NCH = L // TC
    TPC = TC // 128

    id_t = a("k3_id", [128, 128], BF16)
    wip = a("k3_wip", [128, 8 * 4128], BF16)
    wout = a("k3_wout", [128, 16 * 1024], BF16)
    Wbd = {n: a(f"k3_Wbd{n}", [128, 128], BF16) for n in ("re", "im")}
    Wstage = a("k3_Wstage", [64, 64], F32)
    ns_col = {n: a(f"k3_ns{n}", [64, 1], F32) for n in ("re", "im")}
    theta_cols = a("k3_theta", [128, 16], F32)
    patft = a("k3_patft", [32, 16 * 128], F32)
    bp16 = a("k3_bp16", [128, 16 * 32], BF16)
    distb = [a(f"k3_distb{i}", [32, TC], F32) for i in range(2)]
    sc_col = a("k3_sc", [32, 1], F32)
    convk_s = a("k3_cks", [32, 4], F32)
    convb_s = a("k3_cbs", [32, 1], F32)
    convk_cols = a("k3_ckc", [128, 4 * 32], F32)
    convb_cols = a("k3_cbc", [128, 32], F32)
    slopes = a("k3_slopes", [1, 32], F32)
    srow = [a(f"k3_srow{i}", [1, 32], F32) for i in range(3)]
    ones11 = a("k3_ones11", [1, 1], F32)
    sp_col = a("k3_spcol", [32, 1], F32)
    onesl = a("k3_onesl", [1, 128], F32)
    wcols = a("k3_wcols", [128, 4], F32)
    sqs = a("k3_sqs", [128, 1], F32)
    sqb2 = a("k3_sqb2", [128, 1], F32)
    sqa = a("k3_sqa", [128, 1], F32)
    sqt = a("k3_sqt", [128, 1], F32)
    sqt2 = a("k3_sqt2", [128, 1], F32)
    hp_col = a("k3_hp", [128, 1], F32)
    zeros1 = a("k3_zeros1", [128, 1], F32)
    zseg = a("k3_zseg", [128, G3, 3 + TC], F32)
    zcb = a("k3_zcb", [128, G3, TC], F32)
    Bt = a("k3_Bt", [128, G3, TC], F32)
    Ct = a("k3_Ct", [128, G3, TC], F32)
    St = a("k3_St", [128, G3, TC], F32)
    numbuf = [a(f"k3_numb{i}", [128, NFT * TC], BF16) for i in range(2)]
    numfb = [a(f"k3_numfb{i}", [128, G3 * TC], BF16) for i in range(2)]
    sqbig = numfb[0].ap()[:, 0:G3 * TC]
    ygbuf = a("k3_ygbuf", [128, NFT * TC], BF16)
    x2T = a("k3_x2T", [128, 8 * TC], BF16)
    xbst = a("k3_xbst", [128, 1024], BF16)
    hist = a("k3_hist", [128, 32 * 3], F32)
    carry = [a(f"k3_carry{i}", [128, NFT], F32) for i in range(2)]
    red = a("k3_red", [128, G3], F32)
    dcarry = a("k3_dcarry", [32, 1], F32)
    zs = a("k3_zs", [32, 3 + TC], F32)
    pw = a("k3_pw", [32, TC], F32)
    tw = a("k3_tw", [32, TC], F32)
    den = a("k3_den", [32, TC], F32)
    invd = a("k3_invd", [32, TC], F32)
    rsb = a("k3_rsb", [32, TC], F32)
    s_ln = a("k3_sln", [128, 1024], F32)
    x2res = [a(f"k3_x2res{i}", [128, 1024], F32) for i in range(2)]
    ln = LNPack(nc, "k3_")

    gateb = Ct.ap().rearrange("q g c -> q (g c)").bitcast(BF16)[:, 0:G3 * TC]
    Bf = Bt.ap().rearrange("q g c -> q (g c)")
    Cf = Ct.ap().rearrange("q g c -> q (g c)")
    Sf = St.ap().rearrange("q g c -> q (g c)")
    Zf = zcb.ap().rearrange("q g c -> q (g c)")
    hist3 = hist.ap().rearrange("q (ci t) -> q ci t", ci=32)
    ckv4 = convk_cols.ap().rearrange("q (j c o) -> q j c o", j=4, c=32)
    cbv = convb_cols.ap().rearrange("q (c o) -> q c o", c=32)
    thv = theta_cols.ap().rearrange("q (f o) -> q f o", f=16)

    P_z = [ap_(f"k3P_z{i}", [128, 512], F32) for i in range(2)]
    P_bc = [ap_(f"k3P_bc{i}", [128, 512], F32) for i in range(2)]
    P_sm = ap_("k3P_sm", [128, 512], F32)
    P_out = ap_("k3P_out", [128, 1024], F32)
    P_tp = ap_("k3P_tp", [128, 512], F32)
    tp_bf = P_tp.ap().bitcast(BF16)

    TWO_PI = float(2 * np.pi)
    INV2PI = float(1.0 / (2 * np.pi))
    HALF_PI = float(np.pi / 2)
    nb = x2_in.shape[0]

    p.inc("sync", lambda e: e.dma_start(out=id_t[:], in_=W3["ident"]), "init", 16, evt="k3id")
    p.inc("gpsimd", lambda e: e.dma_start(
        out=wip.ap().rearrange("q (dj c) -> q dj c", dj=8),
        in_=W3["in_proj"].rearrange("(dj q) c -> q dj c", q=128)), "k3wip", 16)
    p.inc("gpsimd", lambda e: e.dma_start(
        out=wout.ap().rearrange("q (ft e) -> q ft e", ft=16),
        in_=W3["out_proj"].rearrange("(ft q) e -> q ft e", q=128)), "k3wout", 16)
    smalls = {"theta_cols": theta_cols, "patft": patft, "bp16": bp16,
              "sc_col": sc_col, "convk_s": convk_s, "convb_s": convb_s,
              "convk_cols": convk_cols, "convb_cols": convb_cols}
    for key, t in smalls.items():
        p.inc("gpsimd", lambda e, t=t, key=key: e.dma_start(out=t[:], in_=W3[key]), "k3small", 16)
    p.inc("gpsimd", lambda e: e.dma_start(out=slopes[0:1, 0:28], in_=W3["decay"]), "k3small", 16)
    p.inc("gpsimd", lambda e: e.dma_start(out=slopes[0:1, 28:32], in_=W3["anchor"]), "k3small", 16)
    p.inc("gpsimd", lambda e: e.dma_start(out=srow[2][0:1, 0:3], in_=W3["deriv"]), "k3small", 16)
    p.inc("gpsimd", lambda e: e.dma_start(out=ns_col["re"][:], in_=W3["ns_re"]), "k3small", 16)
    p.inc("gpsimd", lambda e: e.dma_start(out=ns_col["im"][:], in_=W3["ns_im"]), "k3small", 16)
    p.on("vector", lambda v: v.memset(ones11[:], 1.0))
    p.on("vector", lambda v: v.memset(onesl[:], 1.0))
    p.on("vector", lambda v: v.memset(zeros1[:], 0.0))
    p.on("vector", lambda v: v.memset(hp_col[:], HALF_PI))
    p.wait_sem_now("vector", "k3small")
    p.on("vector", lambda v: v.tensor_scalar_mul(srow[0][:], slopes[:], -1.0))
    p.inc("vector", lambda v: v.tensor_max(srow[0][:], srow[0][:], slopes[:]),
          "dve", 1, evt="k3sp_a")
    p.wait_evt("scalar", "k3sp_a")
    p.inc("scalar", lambda s: s.activation(srow[1][:], srow[0][:], AF.Exp, scale=-1.0), "act", 1,
          evt="k3sp_b")
    p.self_wait("scalar", "act")
    p.inc("scalar", lambda s: s.activation(srow[1][:], srow[1][:], AF.Ln, bias=ones11[0:1, 0:1]),
          "act", 1, evt="k3sp_c")
    p.wait_evt("vector", "k3sp_c")
    p.inc("vector", lambda v: v.tensor_single_scalar(srow[0][:], slopes[:], 0.0, ALU.max), "dve", 1)
    p.self_wait("vector", "dve")
    p.inc("vector", lambda v: v.tensor_add(srow[0][:], srow[0][:], srow[1][:]), "dve", 1,
          evt="k3sp_d")
    p.wait_evt("tensor", "k3sp_d")
    p.wait_evt("tensor", "k3id")
    p.inc("tensor", lambda t: t.matmul(P_sm.ap()[0:32, 0:1], srow[0][0:1, 0:32],
                                       ones11[0:1, 0:1], start=True, stop=True,
                                       is_transpose=True, skip_group_check=True),
          "pe", 1, evt="k3sp_t")
    p.wait_evt("vector", "k3sp_t")
    p.inc("vector", lambda v: v.tensor_copy(sp_col[:], P_sm.ap()[0:32, 0:1]), "dve", 1,
          evt="k3sp_e")
    p.wait_evt("scalar", "k3sp_e")
    p.wait_sem_now("scalar", "k3small")
    p.evt_here("act", "k3tw")
    p.inc("vector", lambda v: v.reduce_max(srow[0][0:1, 0:1], srow[2][0:1, 0:3], axis=AX.X),
          "dve", 1, evt="k3dv_a")
    p.wait_evt("scalar", "k3dv_a")
    p.inc("scalar", lambda s: s.copy(srow[0][0:1, 1:2], srow[0][0:1, 0:1]), "act", 1, evt="k3dv_b")
    p.wait_evt("vector", "k3dv_b")
    p.inc("vector", lambda v: v.tensor_tensor(srow[1][0:1, 0:3], srow[2][0:1, 0:3],
                                              srow[0][0:1, 1:2].broadcast_to([1, 3]), ALU.subtract),
          "dve", 1, evt="k3dv_c")
    p.wait_evt("scalar", "k3dv_c")
    p.inc("scalar", lambda s: s.activation(srow[1][0:1, 0:3], srow[1][0:1, 0:3], AF.Exp), "act", 1,
          evt="k3dv_d")
    p.wait_evt("vector", "k3dv_d")
    p.inc("vector", lambda v: v.reduce_sum(srow[0][0:1, 2:3], srow[1][0:1, 0:3], axis=AX.X),
          "dve", 1, evt="k3dv_e")
    p.wait_evt("scalar", "k3dv_e")
    p.inc("scalar", lambda s: s.copy(srow[2][0:1, 1:2], srow[0][0:1, 2:3]), "act", 1, evt="k3dv_f")
    p.wait_evt("vector", "k3dv_f")
    p.inc("vector", lambda v: v.reciprocal(srow[2][0:1, 0:1], srow[2][0:1, 1:2]), "dve", 1,
          evt="k3dv_g")
    p.wait_evt("scalar", "k3dv_g")
    p.inc("scalar", lambda s: s.copy(srow[0][0:1, 0:1], srow[2][0:1, 0:1]), "act", 1, evt="k3dv_h")
    p.wait_evt("vector", "k3dv_h")
    p.inc("vector", lambda v: v.tensor_tensor(srow[1][0:1, 0:3], srow[1][0:1, 0:3],
                                              srow[0][0:1, 0:1].broadcast_to([1, 3]), ALU.mult),
          "dve", 1, evt="k3dv_i")
    p.wait_evt("tensor", "k3dv_i")
    p.wait_evt("tensor", "k3sp_e")
    p.inc("tensor", lambda t: t.matmul(P_sm.ap()[:, 0:3], onesl[:], srow[1][0:1, 0:3],
                                       start=True, stop=True, skip_group_check=True),
          "pe", 1, evt="k3dv_j")
    p.wait_evt("vector", "k3dv_j")
    p.inc("vector", lambda v: v.tensor_copy(wcols[:, 0:3], P_sm.ap()[:, 0:3]), "dve", 1,
          evt="k3wcols")
    p.wait_evt("scalar", "k3wcols")
    p.inc("scalar", lambda s: s.activation(sqs[:], wcols[:, 2:3], AF.Sqrt), "act", 1,
          evt="k3sq_a")
    p.wait_evt("vector", "k3sq_a")
    p.inc("vector", lambda v: v.tensor_scalar_mul(sqt[:], sqs[:], 2.0), "dve", 1)
    p.self_wait("vector", "dve")
    p.inc("vector", lambda v: v.reciprocal(sqt2[:], sqt[:]), "dve", 1)
    p.self_wait("vector", "dve")
    p.inc("vector", lambda v: v.tensor_scalar_mul(sqt2[:], sqt2[:], -1.0), "dve", 1)
    p.self_wait("vector", "dve")
    p.inc("vector", lambda v: v.tensor_scalar(sqb2[:], wcols[:, 1:2], sqt2[:, 0:1], None,
                                              ALU.mult), "dve", 1, evt="k3sq_b")
    p.wait_evt("scalar", "k3sq_b")
    p.inc("scalar", lambda s: s.activation(sqt[:], sqb2[:], AF.Square), "act", 1, evt="k3sq_c")
    p.wait_evt("vector", "k3sq_c")
    p.inc("vector", lambda v: v.tensor_add(sqa[:], wcols[:, 0:1], sqt[:]), "dve", 1,
          evt="k3sq_d")
    for nm in ("re", "im"):
        if nm == "im":
            p.wait_evt("gpsimd", "k3Wre_done")
        p.inc("gpsimd", lambda e, nm=nm: e.dma_start(out=Wstage[:], in_=W3[f"W_{nm}"]),
              f"k3w{nm}", 16, evt=f"k3W{nm}_ld")
        p.wait_evt("vector", f"k3W{nm}_ld")
        p.inc("vector", lambda v, nm=nm: v.tensor_scalar(
            Wstage[:], Wstage[:], ns_col[nm][0:64, 0:1], None, ALU.mult), "dve", 1)
        p.self_wait("vector", "dve")
        p.inc("vector", lambda v: v.tensor_scalar_mul(Wstage[:], Wstage[:], -1.0), "dve", 1)
        p.self_wait("vector", "dve")
        p.inc("vector", lambda v, nm=nm: v.tensor_copy(Wbd[nm][0:64, 0:64], Wstage[:]), "dve", 1)
        p.inc("vector", lambda v, nm=nm: v.memset(Wbd[nm][0:64, 64:128], 0.0), "dve", 1)
        p.inc("vector", lambda v, nm=nm: v.memset(Wbd[nm][64:128, 0:64], 0.0), "dve", 1)
        p.inc("vector", lambda v, nm=nm: v.tensor_copy(Wbd[nm][64:128, 64:128], Wstage[:]),
              "dve", 1, evt=f"k3W{nm}_done")
    p.wait_sem_now("tensor", "k3wip")
    p.wait_sem_now("tensor", "k3wout")
    p.wait_sem_now("tensor", "k3small")
    p.wait_evt("tensor", "k3Wim_done")
    p.wait_evt("tensor", "k3sq_d")
    p.wait_evt("tensor", "k3tw")
    p.evt_here("pe", "k3prep")
    p.evt_here("act", "k3prep_act")
    p.evt_here("dve", "k3prep_dve")

    st = {
        "zseg_rd": None,
        "xbst_rd": None,
        "x2T_rd": None,
        "x2T_wr": None,
        "pz_free": [None, None],
        "pbc_free": [None, None],
        "psm_free": None,
        "tw_rd": [None, None],
        "numbuf_sq": [[None, None], [None, None]],
        "numfb_rd": None,
        "yg_last": None,
        "out_rd": None,
        "pout_free": None,
        "ln_store2": [None, None],
        "pw_evt": None,
        "tot_evt": None,
        "silu": [None, None],
        "scan_done": None,
        "zsm_ok": None,
    }

    def build_x2T(b, c, tag):
        for tt in range(TPC):
            if st["xbst_rd"] is not None:
                p.wait_evt("gpsimd", st["xbst_rd"])
            p.inc("gpsimd", lambda e, b=b, c=c, tt=tt: e.dma_start(
                out=xbst[:], in_=x2_in[b, (c * TPC + tt) * 128:(c * TPC + tt + 1) * 128, :]),
                "k3xb", 16, evt=f"{tag}_ld{tt}")
            p.wait_evt("tensor", f"{tag}_ld{tt}")
            for g in range(2):
                if st["pbc_free"][g] is not None:
                    pass
                for jj in range(4):
                    dj = g * 4 + jj
                    p.inc("tensor", lambda t, dj=dj, jj=jj: t.matmul(
                        tp_bf[:, jj * 128:(jj + 1) * 128], xbst[:, dj * 128:(dj + 1) * 128],
                        id_t[:], start=(jj == 0), stop=True, is_transpose=True,
                        skip_group_check=True), "pe", 1, evt=f"{tag}_tp{tt}_{g}")
                p.wait_evt("vector", f"{tag}_tp{tt}_{g}")
                if tt == 0 and g == 0 and st["x2T_rd"] is not None:
                    p.wait_evt("vector", st["x2T_rd"])
                p.inc("vector", lambda v, g=g, tt=tt: v.tensor_copy(
                    x2T.ap().rearrange("q (dj t) -> q dj t", dj=8)[:, g * 4:(g + 1) * 4,
                                                                  tt * 128:(tt + 1) * 128],
                    tp_bf[:, 0:512].rearrange("q (dj t) -> q dj t", dj=4)),
                    "dve", 1, evt=f"{tag}_ev{tt}_{g}")
                p.wait_evt("tensor", f"{tag}_ev{tt}_{g}")
            st["xbst_rd"] = f"{tag}_tp{tt}_1"
        st["x2T_wr"] = f"{tag}_ev{TPC - 1}_1"

    def zmm(ci, pzb, tag):
        if st["pz_free"][pzb] is not None:
            p.wait_evt("tensor", st["pz_free"][pzb])
        for dj in range(8):
            p.inc("tensor", lambda t, ci=ci, dj=dj, pzb=pzb: t.matmul(
                P_z[pzb].ap()[:, 0:TC],
                wip[:, dj * 4128 + ci * 128: dj * 4128 + (ci + 1) * 128],
                x2T[:, dj * TC:(dj + 1) * TC],
                start=(dj == 0), stop=(dj == 7), skip_group_check=True),
                "pe", 1, evt=(tag if dj == 7 else None))

    for b in range(nb):
        if b > 0:
            p.wait_evt("vector", f"k3_b{b - 1}_done")
        p.inc("vector", lambda v: v.memset(carry[0][:], 0.0), "dve", 1)
        p.inc("vector", lambda v: v.memset(carry[1][:], 0.0), "dve", 1)
        p.inc("vector", lambda v: v.memset(dcarry[:], 0.0), "dve", 1)
        p.inc("vector", lambda v: v.memset(hist[:], 0.0), "dve", 1)
        p.inc("vector", lambda v: v.memset(zs[0:32, 0:3], 0.0), "dve", 1, evt=f"k3rst_{b}")
        if b == 0:
            p.wait_evt("tensor", "k3prep")
            p.wait_evt("vector", "k3prep_act")
            p.wait_evt("scalar", "k3prep_dve")
        build_x2T(b, 0, f"k3bx_{b}_0")
        for c in range(NCH):
            bc = f"{b}_{c}"
            par = c % 2
            if st["tw_rd"][par] is not None:
                p.wait_evt("sync", st["tw_rd"][par])
            p.inc("sync", lambda e, c=c, par=par: e.dma_start(
                out=distb[par][:], in_=W3["dist32n"][:, c * TC:(c + 1) * TC]),
                "k3dist", 16, evt=f"k3dl_{bc}")
            p.wait_evt("tensor", st["x2T_wr"])
            if st["psm_free"] is not None:
                p.wait_evt("tensor", st["psm_free"])
            for dj in range(8):
                p.inc("tensor", lambda t, dj=dj: t.matmul(
                    P_sm.ap()[0:32, 0:TC],
                    wip[:, dj * 4128 + 4096: dj * 4128 + 4128],
                    x2T[:, dj * TC:(dj + 1) * TC],
                    start=(dj == 0), stop=(dj == 7), skip_group_check=True),
                    "pe", 1, evt=(f"k3zsm_{bc}" if dj == 7 else None))
            p.wait_evt("scalar", f"k3zsm_{bc}")
            p.wait_evt("scalar", f"k3dl_{bc}")
            p.inc("scalar", lambda s, par=par: s.activation(tw[0:32, :], distb[par][:],
                                                            AF.Exp, scale=sp_col[0:32, 0:1]),
                  "act", 1, evt=f"k3twc_{bc}")
            st["tw_rd"][par] = f"k3twc_{bc}"
            p.wait_evt("vector", f"k3zsm_{bc}")
            if b == 0 and c == 0:
                p.wait_evt("vector", "k3rst_0")
            p.inc("vector", lambda v: v.tensor_copy(zs[0:32, 3:3 + TC], P_sm.ap()[0:32, 0:TC]),
                  "dve", 1, evt=f"k3zsev_{bc}")
            st["zsm_ok"] = f"k3zsev_{bc}"
            for j in range(4):
                if j == 0:
                    p.inc("vector", lambda v: v.tensor_scalar(
                        den[0:32, :], zs[0:32, 0:TC], convk_s[0:32, 0:1], None, ALU.mult), "dve", 1)
                else:
                    p.inc("vector", lambda v, j=j: v.scalar_tensor_tensor(
                        den[0:32, :], zs[0:32, j:j + TC], convk_s[0:32, j:j + 1], den[0:32, :],
                        ALU.mult, ALU.add), "dve", 1)
            p.inc("vector", lambda v: v.tensor_scalar(den[0:32, :], den[0:32, :],
                                                      convb_s[0:32, 0:1], None, ALU.add), "dve", 1)
            p.inc("vector", lambda v: v.tensor_copy(zs[0:32, 0:3], zs[0:32, TC:TC + 3]), "dve", 1)
            p.inc("vector", lambda v: v.tensor_scalar(den[0:32, :], den[0:32, :], sc_col[0:32, 0:1],
                                                      None, ALU.mult), "dve", 1)
            p.inc("vector", lambda v: v.tensor_single_scalar(den[0:32, :], den[0:32, :], 20.0,
                                                             ALU.min), "dve", 1)
            p.inc("vector", lambda v: v.tensor_single_scalar(den[0:32, :], den[0:32, :], -20.0,
                                                             ALU.max), "dve", 1, evt=f"k3p_{bc}a")
            p.wait_evt("scalar", f"k3p_{bc}a")
            p.inc("scalar", lambda s: s.activation(pw[0:32, :], den[0:32, :], AF.Exp), "act", 1,
                  evt=f"k3p_{bc}b")
            p.wait_evt("vector", f"k3p_{bc}b")
            p.wait_evt("vector", f"k3twc_{bc}")
            p.inc("vector", lambda v: v.tensor_mul(pw[0:32, :], pw[0:32, :], tw[0:32, :]),
                  "dve", 1, evt=f"k3pw_{bc}")
            st["pw_evt"] = f"k3pw_{bc}"
            p.inc("vector", lambda v: v.tensor_tensor_scan(
                den[0:32, :], pw[0:32, :], zeros1[0:32, 0:1].broadcast_to([32, TC]),
                dcarry[0:32, 0:1], ALU.add, ALU.add), "dve", 1)
            p.inc("vector", lambda v: v.tensor_copy(dcarry[:], den[0:32, TC - 1:TC]), "dve", 1)
            p.inc("vector", lambda v: v.tensor_single_scalar(den[0:32, :], den[0:32, :], 1e-4,
                                                             ALU.max), "dve", 1, evt=f"k3dn_{bc}a")
            p.wait_evt("scalar", f"k3dn_{bc}a")
            p.inc("scalar", lambda s: s.copy(tw[0:32, :], den[0:32, :]), "act", 1,
                  evt=f"k3dn_{bc}b")
            p.wait_evt("vector", f"k3dn_{bc}b")
            p.inc("vector", lambda v: v.reciprocal(invd[0:32, :], tw[0:32, :]), "dve", 1,
                  evt=f"k3invd_{bc}")

            ssq_pend = []
            for g in range(NG3):
                gk = f"{bc}_{g}"
                for fl in range(G3):
                    ft = g * G3 + fl
                    pzb = fl % 2
                    zmm(ft, pzb, f"k3z_{gk}_{fl}")
                    p.wait_evt("scalar", f"k3z_{gk}_{fl}")
                    if fl == 0 and st["zseg_rd"] is not None:
                        eng, evt = st["zseg_rd"]
                        p.wait_evt("scalar", evt)
                    p.inc("scalar", lambda s, fl=fl, pzb=pzb: s.copy(
                        zseg.ap()[:, fl, 3:3 + TC], P_z[pzb].ap()[:, 0:TC]), "act", 1,
                        evt=f"k3zev_{gk}_{fl}")
                    st["pz_free"][pzb] = f"k3zev_{gk}_{fl}"
                while ssq_pend:
                    ssq_pend.pop(0)()
                p.wait_evt("vector", f"k3zev_{gk}_{G3 - 1}")
                if b == 0 and c == 0 and g == 0:
                    p.wait_evt("vector", f"k3rst_{b}")
                p.inc("vector", lambda v, g=g: v.tensor_copy(
                    zseg.ap()[:, :, 0:3], hist3[:, g * G3:(g + 1) * G3, :]), "dve", 1)
                for j in range(4):
                    ckb = ckv4[:, j, g * G3:(g + 1) * G3, :].broadcast_to([128, G3, TC])
                    if j == 0:
                        p.inc("vector", lambda v, ckb=ckb: v.tensor_tensor(
                            zcb[:], zseg.ap()[:, :, 0:TC], ckb, ALU.mult), "dve", 1)
                    else:
                        p.inc("vector", lambda v, j=j, ckb=ckb: v.tensor_tensor(
                            St[:], zseg.ap()[:, :, j:j + TC], ckb, ALU.mult), "dve", 1)
                        p.inc("vector", lambda v: v.tensor_add(zcb[:], zcb[:], St[:]), "dve", 1)
                cbb = cbv[:, 16 * 0 + g * G3:16 * 0 + (g + 1) * G3, :].broadcast_to([128, G3, TC])
                p.inc("vector", lambda v, cbb=cbb: v.tensor_tensor(zcb[:], zcb[:], cbb, ALU.add),
                      "dve", 1)
                p.inc("vector", lambda v, g=g: v.tensor_copy(
                    hist3[:, g * G3:(g + 1) * G3, :], zseg.ap()[:, :, TC:TC + 3]), "dve", 1,
                    evt=f"k3cv_{gk}")
                st["zseg_rd"] = ("vector", f"k3cv_{gk}")
                thb = thv[:, g * G3:(g + 1) * G3, :].broadcast_to([128, G3, TC])
                p.inc("vector", lambda v, thb=thb: v.tensor_tensor(Ct[:], zcb[:], thb, ALU.mult),
                      "dve", 1, evt=f"k3ph_{gk}")
                p.inc("vector", lambda v: v.tensor_scalar(Bf, Cf, INV2PI, RND_MAGIC,
                                                          ALU.mult, ALU.add), "dve", 1)
                p.inc("vector", lambda v: v.tensor_scalar(Bf, Bf, RND_MAGIC, TWO_PI,
                                                          ALU.subtract, ALU.mult), "dve", 1)
                p.inc("vector", lambda v: v.tensor_sub(Sf, Cf, Bf), "dve", 1, evt=f"k3rs_{gk}")
                p.inc("vector", lambda v: v.tensor_scalar(Bf, Cf, INV2PI, 0.25,
                                                          ALU.mult, ALU.add), "dve", 1)
                p.inc("vector", lambda v: v.tensor_scalar(Bf, Bf, RND_MAGIC, RND_MAGIC,
                                                          ALU.add, ALU.subtract), "dve", 1)
                p.inc("vector", lambda v: v.tensor_scalar_mul(Bf, Bf, TWO_PI), "dve", 1)
                p.inc("vector", lambda v: v.tensor_sub(Cf, Cf, Bf), "dve", 1, evt=f"k3rc_{gk}")
                p.wait_evt("scalar", f"k3rs_{gk}")
                p.inc("scalar", lambda s: s.activation(Sf, Sf, AF.Sin), "act", 1,
                      evt=f"k3sin_{gk}")
                p.wait_evt("scalar", f"k3rc_{gk}")
                p.inc("scalar", lambda s: s.activation(Cf, Cf, AF.Sin, bias=hp_col[:, 0:1]),
                      "act", 1, evt=f"k3cos_{gk}")
                p.inc("scalar", lambda s: s.activation(Bf, Zf, AF.Square, scale=sqs[:, 0:1],
                                                       bias=sqb2[:, 0:1]), "act", 1,
                      evt=f"k3poly_{gk}")
                p.wait_evt("tensor", st["pw_evt"])
                p.wait_evt("tensor", f"k3poly_{gk}")
                for fl in range(G3):
                    ft = g * G3 + fl
                    bb = fl % 2
                    if st["pbc_free"][bb] is not None:
                        p.wait_evt("tensor", st["pbc_free"][bb])
                    p.inc("tensor", lambda t, ft=ft, bb=bb: t.matmul(
                        P_bc[bb].ap()[:, 0:TC], patft[:, ft * 128:(ft + 1) * 128], pw[0:32, 0:TC],
                        start=True, stop=True, skip_group_check=True), "pe", 1,
                        evt=f"k3pbm_{gk}_{fl}")
                    p.wait_evt("scalar", f"k3pbm_{gk}_{fl}")
                    if fl == 0:
                        p.wait_evt("scalar", f"k3ph_{gk}")
                    p.inc("scalar", lambda s, fl=fl, bb=bb: s.copy(
                        zcb.ap()[:, fl, :], P_bc[bb].ap()[:, 0:TC]), "act", 1,
                        evt=f"k3pbe_{gk}_{fl}")
                    st["pbc_free"][bb] = f"k3pbe_{gk}_{fl}"
                p.wait_evt("vector", f"k3poly_{gk}")
                p.wait_evt("vector", f"k3pbe_{gk}_{G3 - 1}")
                p.inc("vector", lambda v: v.scalar_tensor_tensor(
                    Bf, Bf, sqa[:, 0:1], Zf, ALU.subtract, ALU.mult), "dve", 1)
                p.wait_evt("vector", f"k3cos_{gk}")
                p.inc("vector", lambda v: v.tensor_mul(Cf, Cf, Bf), "dve", 1)
                p.wait_evt("vector", f"k3sin_{gk}")
                p.inc("vector", lambda v: v.tensor_mul(Sf, Sf, Bf), "dve", 1, evt=f"k3mod_{gk}")
                if st["numbuf_sq"][0][g] is not None:
                    p.wait_evt("vector", st["numbuf_sq"][0][g])
                    p.wait_evt("vector", st["numbuf_sq"][1][g])
                for ri, src in ((0, Ct), (1, St)):
                    for fl in range(G3):
                        ft = g * G3 + fl
                        p.inc("vector", lambda v, ri=ri, src=src, fl=fl, ft=ft: v.tensor_tensor_scan(
                            numbuf[ri][:, ft * TC:(ft + 1) * TC], src.ap()[:, fl, :],
                            zeros1[:, 0:1].broadcast_to([128, TC]),
                            carry[ri][:, ft:ft + 1], ALU.add, ALU.add), "dve", 1)
                for ri, src in ((0, Ct), (1, St)):
                    p.inc("vector", lambda v, src=src: v.reduce_sum(red[:], src[:], axis=AX.X),
                          "dve", 1)
                    p.inc("vector", lambda v, ri=ri, g=g: v.tensor_add(
                        carry[ri][:, g * G3:(g + 1) * G3], carry[ri][:, g * G3:(g + 1) * G3],
                        red[:]), "dve", 1, evt=(f"k3scan_{gk}" if ri == 1 else None))
                st["scan_done"] = f"k3scan_{gk}"
                def emit_sq(g=g, gk=gk):
                    for ri in range(2):
                        p.wait_evt("gpsimd", f"k3scan_{gk}")
                        if not (g == 0 and ri == 0):
                            p.wait_evt("gpsimd", f"k3ssq_{gk}_{ri - 1}" if ri == 1 else
                                       f"k3ssq_{bc}_{g - 1}_1")
                        p.onp(lambda e, ri=ri, g=g: e.tensor_mul(
                            sqbig, numbuf[ri][:, g * G3 * TC:(g + 1) * G3 * TC],
                            numbuf[ri][:, g * G3 * TC:(g + 1) * G3 * TC]),
                            evt=f"k3sq_{gk}_{ri}")
                        st["numbuf_sq"][ri][g] = f"k3sq_{gk}_{ri}"
                        p.wait_evt("tensor", f"k3sq_{gk}_{ri}")
                        if g == 0 and ri == 0:
                            p.wait_evt("tensor", st["zsm_ok"])
                        for fl in range(G3):
                            ft = g * G3 + fl
                            p.inc("tensor", lambda t, ft=ft, fl=fl, g=g, ri=ri: t.matmul(
                                P_sm.ap()[0:32, 0:TC], bp16[:, ft * 32:(ft + 1) * 32],
                                sqbig[:, fl * TC:(fl + 1) * TC],
                                start=(g == 0 and ri == 0 and fl == 0),
                                stop=(g == NG3 - 1 and ri == 1 and fl == G3 - 1),
                                skip_group_check=True), "pe", 1,
                                evt=(f"k3ssq_{gk}_{ri}" if fl == G3 - 1 else None))
                ssq_pend.append(emit_sq)

            while ssq_pend:
                ssq_pend.pop(0)()

            q1 = Bt.ap().rearrange("q g c -> q (g c)").bitcast(I32)[0:32, 0:TC]
            q2 = Cf[0:32, 0:TC]
            q3 = Sf[0:32, 0:TC]
            mgv = Zf.bitcast(I32)[0:32, 0:TC]
            m_ = tw[0:32, 0:TC]
            p.wait_evt("vector", f"k3ssq_{bc}_{NG3 - 1}_1")
            p.inc("vector", lambda v: v.tensor_scalar(m_, P_sm.ap()[0:32, 0:TC],
                                                      1.0 / 128.0, None, ALU.mult), "dve", 1)
            p.on("vector", lambda v: v.tensor_mul(m_, m_, invd[0:32, :]))
            p.on("vector", lambda v: v.tensor_mul(m_, m_, invd[0:32, :]))
            p.inc("vector", lambda v: v.tensor_single_scalar(m_, m_, 1e-5,
                                                             ALU.add), "dve", 1, evt=f"k3ms_{bc}v")
            p.wait_evt("tensor", f"k3ms_{bc}v")
            p.inc("tensor", lambda t: t.nop(), "pe", 1, evt=f"k3ms_{bc}")
            st["psm_free"] = f"k3ms_{bc}v"
            p.on("vector", lambda v: v.memset(mgv, RSQRT_MAGIC))
            p.on("vector", lambda v: v.tensor_single_scalar(
                q1, tw.ap().bitcast(I32)[0:32, 0:TC], 1, ALU.arith_shift_right))
            p.on("vector", lambda v: v.tensor_sub(rsb.ap().bitcast(I32)[0:32, 0:TC], mgv, q1))
            p.on("vector", lambda v: v.tensor_scalar_mul(q2, m_, 0.5))
            for _ in range(3):
                p.on("vector", lambda v: v.tensor_mul(q3, rsb[0:32, :], rsb[0:32, :]))
                p.on("vector", lambda v: v.tensor_mul(q3, q3, q2))
                p.on("vector", lambda v: v.tensor_scalar(q3, q3, -1.0, 1.5, ALU.mult, ALU.add))
                p.on("vector", lambda v: v.tensor_mul(rsb[0:32, :], rsb[0:32, :], q3))
            p.inc("vector", lambda v: v.tensor_mul(rsb[0:32, :], rsb[0:32, :], invd[0:32, :]),
                  "dve", 1, evt=f"k3tot_{bc}")
            st["tot_evt"] = f"k3tot_{bc}"

            for g in range(NG3):
                gk = f"{bc}_{g}"
                for fl in range(G3):
                    ci = 16 + g * G3 + fl
                    pzb = fl % 2
                    zmm(ci, pzb, f"k3zg_{gk}_{fl}")
                    p.wait_evt("scalar", f"k3zg_{gk}_{fl}")
                    if fl == 0:
                        eng, evt = st["zseg_rd"]
                        if eng == "vector":
                            p.wait_evt("scalar", evt)
                        else:
                            p.wait_evt("scalar", evt)
                    p.inc("scalar", lambda s, fl=fl, pzb=pzb: s.copy(
                        zseg.ap()[:, fl, 3:3 + TC], P_z[pzb].ap()[:, 0:TC]), "act", 1,
                        evt=f"k3zgev_{gk}_{fl}")
                    st["pz_free"][pzb] = f"k3zgev_{gk}_{fl}"
                p.wait_evt("vector", f"k3zgev_{gk}_{G3 - 1}")
                p.inc("vector", lambda v, g=g: v.tensor_copy(
                    zseg.ap()[:, :, 0:3], hist3[:, 16 + g * G3:16 + (g + 1) * G3, :]), "dve", 1)
                for j in range(4):
                    ckb = ckv4[:, j, 16 + g * G3:16 + (g + 1) * G3, :].broadcast_to([128, G3, TC])
                    if j == 0:
                        p.inc("vector", lambda v, ckb=ckb: v.tensor_tensor(
                            zcb[:], zseg.ap()[:, :, 0:TC], ckb, ALU.mult), "dve", 1)
                    else:
                        p.inc("vector", lambda v, j=j, ckb=ckb: v.tensor_tensor(
                            Bt[:], zseg.ap()[:, :, j:j + TC], ckb, ALU.mult), "dve", 1)
                        p.inc("vector", lambda v: v.tensor_add(zcb[:], zcb[:], Bt[:]), "dve", 1)
                cbb = cbv[:, 16 + g * G3:16 + (g + 1) * G3, :].broadcast_to([128, G3, TC])
                p.inc("vector", lambda v, cbb=cbb: v.tensor_tensor(zcb[:], zcb[:], cbb, ALU.add),
                      "dve", 1)
                p.inc("vector", lambda v, g=g: v.tensor_copy(
                    hist3[:, 16 + g * G3:16 + (g + 1) * G3, :], zseg.ap()[:, :, TC:TC + 3]),
                    "dve", 1, evt=f"k3cvg_{gk}")
                st["zseg_rd"] = ("vector", f"k3cvg_{gk}")
                p.wait_evt("scalar", f"k3cvg_{gk}")
                p.wait_evt("scalar", st["scan_done"])
                if st["yg_last"] is not None:
                    p.wait_evt("scalar", st["yg_last"])
                p.inc("scalar", lambda s: s.activation(gateb, Zf[:, 0:G3 * TC], AF.Silu),
                      "act", 1, evt=f"k3silu_{gk}")
                st["silu"][g] = f"k3silu_{gk}"
                p.wait_evt("tensor", st["tot_evt"])
                sf_rd = st.get("sf_rd")
                for fl in range(G3):
                    ft = g * G3 + fl
                    bb = fl % 2
                    if st["pbc_free"][bb] is not None:
                        p.wait_evt("tensor", st["pbc_free"][bb])
                    p.inc("tensor", lambda t, ft=ft, bb=bb: t.matmul(
                        P_bc[bb].ap()[:, 0:TC], patft[:, ft * 128:(ft + 1) * 128], rsb[0:32, 0:TC],
                        start=True, stop=True, skip_group_check=True), "pe", 1,
                        evt=f"k3tbm_{gk}_{fl}")
                    p.wait_evt("scalar", f"k3tbm_{gk}_{fl}")
                    if fl == 0:
                        p.wait_evt("scalar", st["scan_done"])
                        if sf_rd is not None:
                            p.wait_evt("scalar", sf_rd)
                    p.inc("scalar", lambda s, fl=fl, bb=bb: s.copy(
                        St.ap()[:, fl, :], P_bc[bb].ap()[:, 0:TC]), "act", 1,
                        evt=f"k3tbe_{gk}_{fl}")
                    st["pbc_free"][bb] = f"k3tbe_{gk}_{fl}"
                p.wait_evt("vector", f"k3tbe_{gk}_{G3 - 1}")
                if g == 0:
                    p.wait_evt("vector", f"k3ssq_{bc}_{NG3 - 1}_1")
                if st["numfb_rd"] is not None:
                    p.wait_evt("vector", st["numfb_rd"])
                for ri in range(2):
                    p.inc("vector", lambda v, ri=ri, g=g: v.tensor_tensor(
                        numfb[ri][:], numbuf[ri][:, g * G3 * TC:(g + 1) * G3 * TC],
                        Sf[:, 0:G3 * TC], ALU.mult), "dve", 1,
                        evt=(f"k3yop_{gk}" if ri == 1 else None))
                st["sf_rd"] = f"k3yop_{gk}"
                for fl in range(G3):
                    ft = g * G3 + fl
                    bb = fl % 2
                    if st["pbc_free"][bb] is not None:
                        p.wait_evt("tensor", st["pbc_free"][bb])
                    if fl == 0:
                        p.wait_evt("tensor", f"k3yop_{gk}")
                    for ri, nm in ((0, "re"), (1, "im")):
                        p.inc("tensor", lambda t, ri=ri, nm=nm, fl=fl, bb=bb: t.matmul(
                            P_bc[bb].ap()[:, 0:TC], Wbd[nm][:],
                            numfb[ri][:, fl * TC:(fl + 1) * TC],
                            start=(ri == 0), stop=(ri == 1), skip_group_check=True), "pe", 1,
                            evt=(f"k3y_{gk}_{fl}" if ri == 1 else None))
                    p.wait_evt("vector", f"k3y_{gk}_{fl}")
                    p.wait_evt("vector", f"k3silu_{gk}")
                    if g == 0 and fl == 0 and st["out_rd"] is not None:
                        p.wait_evt("vector", st["out_rd"])
                    p.inc("vector", lambda v, ft=ft, fl=fl, bb=bb: v.tensor_tensor(
                        ygbuf[:, ft * TC:(ft + 1) * TC], P_bc[bb].ap()[:, 0:TC],
                        gateb[:, fl * TC:(fl + 1) * TC], ALU.mult), "dve", 1,
                        evt=f"k3yg_{gk}_{fl}")
                    st["pbc_free"][bb] = f"k3yg_{gk}_{fl}"
                st["numfb_rd"] = f"k3y_{gk}_{G3 - 1}"
                st["yg_last"] = f"k3yg_{gk}_{G3 - 1}"

            st["x2T_rd"] = f"k3zg_{bc}_{NG3 - 1}_{G3 - 1}"
            if c + 1 < NCH:
                build_x2T(b, c + 1, f"k3bx_{b}_{c + 1}")
            elif b + 1 < nb:
                build_x2T(b + 1, 0, f"k3bx_{b + 1}_0")

            p.wait_evt("tensor", st["yg_last"])
            for tt in range(TPC):
                gtile = c * TPC + tt
                ir = gtile % 2
                if st["ln_store2"][ir] is not None:
                    p.wait_evt("sync", st["ln_store2"][ir])
                p.inc("sync", lambda e, b=b, gtile=gtile, ir=ir: e.dma_start(
                    out=x2res[ir][:], in_=x2_in[b, gtile * 128:(gtile + 1) * 128, :]),
                    "k3xr", 16, evt=f"k3xr_{b}_{gtile}")
                if st["pout_free"] is not None:
                    p.wait_evt("tensor", st["pout_free"])
                for ft in range(NFT):
                    for nh in range(2):
                        p.inc("tensor", lambda t, ft=ft, nh=nh, tt=tt: t.matmul(
                            P_out.ap()[:, nh * 512:(nh + 1) * 512],
                            ygbuf[:, ft * TC + tt * 128: ft * TC + (tt + 1) * 128],
                            wout[:, ft * 1024 + nh * 512: ft * 1024 + (nh + 1) * 512],
                            start=(ft == 0), stop=(ft == NFT - 1), skip_group_check=True),
                            "pe", 1, evt=(f"k3o_{b}_{gtile}" if ft == NFT - 1 and nh == 1 else None))
                p.wait_evt("vector", f"k3o_{b}_{gtile}")
                p.wait_evt("vector", f"k3xr_{b}_{gtile}")
                p.inc("vector", lambda v, ir=ir: v.tensor_add(s_ln[:], P_out.ap()[:], x2res[ir][:]),
                      "dve", 1, evt=f"k3s_{b}_{gtile}v")
                st["pout_free"] = f"k3s_{b}_{gtile}v"
                emit_ln_store(p, ln, s_ln[:], x2res[ir][:],
                              [x3_out[b, gtile * 128:(gtile + 1) * 128, :]],
                              f"k3s_{b}_{gtile}v", f"k3ln_{b}_{gtile}",
                              war_evt=st["ln_store2"][ir])
                st["ln_store2"][ir] = f"k3ln_{b}_{gtile}_store"
            st["out_rd"] = f"k3o_{b}_{c * TPC + TPC - 1}"
        p.evt_here("st", f"k3_b{b}_done")
    p.wait_sem_now("sync", "st")


def build_k3():
    nc = bass.Bass("TRN2", target_bir_lowering=False)
    x2_in = nc.dram_tensor("x2", [B_PER_CORE, L, D], F32, kind="ExternalInput").ap()
    x3_out = nc.dram_tensor("x3", [B_PER_CORE, L, D], F32, kind="ExternalOutput").ap()
    shapes = {
        "in_proj": [1024, 4128], "out_proj": [2048, 1024], "theta_cols": [128, 16],
        "dist32n": [32, 2048], "patft": [32, 16 * 128], "sc_col": [32, 1],
        "convk_s": [32, 4], "convb_s": [32, 1], "convk_cols": [128, 4 * 32],
        "convb_cols": [128, 32], "decay": [1, 28], "anchor": [1, 4], "deriv": [1, 3],
        "W_re": [64, 64], "W_im": [64, 64], "ns_re": [64, 1], "ns_im": [64, 1],
    }
    W3 = {k: nc.dram_tensor(k, v, F32, kind="ExternalInput").ap() for k, v in shapes.items()}
    W3["bp16"] = nc.dram_tensor("bp16", [128, 16 * 32], BF16, kind="ExternalInput").ap()
    W3["ident"] = nc.dram_tensor("ident", [128, 128], BF16, kind="ExternalInput").ap()
    p = Prog(nc)
    emit_k3(nc, p, x2_in, x3_out, W3)
    p.run()
    return nc


def k3_inputs(inputs, core, x2):
    import ml_dtypes
    f32 = lambda x: np.ascontiguousarray(np.asarray(x, dtype=np.float32))
    theta = f32(inputs["L1_attn_theta"]).reshape(2048)
    ns = f32(inputs["L1_attn_norm_scale"])
    patft = np.zeros((32, 16 * 128), np.float32)
    for ft in range(16):
        for m in range(128):
            patft[2 * ft + m // 64, ft * 128 + m] = 1.0
    bp16 = np.zeros((128, 16 * 32), np.float32)
    for ft in range(16):
        for pp in range(128):
            bp16[pp, ft * 32 + 2 * ft + pp // 64] = 1.0
    dist = np.zeros((32, 2048), np.float32)
    pos = np.arange(2048, dtype=np.float32)
    dist[:28, :] = -(2047.0 - pos)[None, :]
    dist[28:, :] = -pos[None, :]
    ck = f32(inputs["L1_attn_conv_k"])[:, 0, :]
    d = {
        "x2": (np.ascontiguousarray(x2[core * B_PER_CORE:(core + 1) * B_PER_CORE])
               if x2 is not None else None),
        "in_proj": f32(inputs["L1_attn_in_proj_w"]),
        "out_proj": f32(inputs["L1_attn_out_proj_w"]),
        "theta_cols": np.ascontiguousarray(theta.reshape(16, 128).T),
        "dist32n": dist,
        "patft": patft,
        "bp16": bp16.astype(ml_dtypes.bfloat16),
        "sc_col": f32(inputs["L1_attn_score_scale"]).reshape(32, 1),
        "convk_s": np.ascontiguousarray(ck[:, 4096:4128].T),
        "convb_s": f32(inputs["L1_attn_conv_b"])[4096:4128].reshape(32, 1),
        "convk_cols": np.ascontiguousarray(
            np.concatenate([ck[j, :4096].reshape(32, 128).T for j in range(4)], axis=1)),
        "convb_cols": np.ascontiguousarray(
            f32(inputs["L1_attn_conv_b"])[:4096].reshape(32, 128).T),
        "decay": f32(inputs["L1_attn_decay_slopes"]).reshape(1, 28),
        "anchor": f32(inputs["L1_attn_anchor_slopes"]).reshape(1, 4),
        "deriv": f32(inputs["L1_attn_deriv_logits"]).reshape(1, 3),
        "W_re": f32(inputs["L1_attn_W_re"]),
        "W_im": f32(inputs["L1_attn_W_im"]),
        "ns_re": ns[:64].reshape(64, 1),
        "ns_im": ns[64:].reshape(64, 1),
        "ident": np.eye(128, dtype=ml_dtypes.bfloat16),
    }
    return d




NE4 = 18


def emit_k4(nc, p, x3_in, x4_out, W4):
    a = nc.alloc_sbuf_tensor
    ap_ = nc.alloc_psum_tensor
    NTT = B_PER_CORE * NT
    wT_hbm = nc.dram_tensor("k4_wT", [NE4, 128, 12288], BF16, kind="Internal").ap()

    id_t = a("k4_id", [128, 128], BF16)
    id_f = a("k4_idf", [128, 128], F32)
    gateT = a("k4_gateT", [128, 128], F32)
    comb = a("k4_comb", [128, NTT * 16], F32)
    ones_col = a("k4_ones", [128, 1], F32)

    probs = a("k4_probs", [128, 16], F32)
    pb8 = a("k4_pb8", [128, 32], F32)
    mxb4 = a("k4_mxb4", [128, 32], F32)
    gsc8 = a("k4_gsc8", [128, 8], F32)
    gmx = a("k4_gmx", [128, 8], F32)
    gmxb = a("k4_gmxb", [128, 8], F32)
    alw = a("k4_alw", [128, 4], F32)
    msk = a("k4_msk", [128, 16], F32)
    mx4 = a("k4_mx4", [128, 8], F32)
    mx4b = a("k4_mx4b", [128, 8], F32)
    wsel = a("k4_wsel", [128, 16], F32)
    wselb = a("k4_wselb", [128, 16], F32)
    gsum = a("k4_gsum", [128, 1], F32)
    gsumb = a("k4_gsumb", [128, 1], F32)
    grc = a("k4_grc", [128, 1], F32)
    grc2 = a("k4_grc2", [128, 1], F32)
    NCH4 = L // 512
    x3T = a("k4_x3T", [128, 8 * L], BF16)
    xb = a("k4_xb", [128, 1024], BF16)
    wbuf = [a(f"k4_wbuf{i}", [128, 12288], BF16) for i in range(2)]
    sgT = [a(f"k4_sgT{i}", [128, 512], BF16) for i in range(2)]
    guT = [a(f"k4_guT{i}", [128, 4 * 512], BF16) for i in range(2)]
    acc = a("k4_acc", [128, NT * 1024], F32)
    x3res = [a(f"k4_x3res{i}", [128, 1024], F32) for i in range(2)]
    junkb = [a(f"k4_junk{i}", [128, 1024], F32) for i in range(2)]
    junk = junkb[0]
    xTf = acc.ap()[:, 0:1024]
    s_ln = a("k4_sln", [128, 1024], F32)
    ln = LNPack(nc, "k4_")

    P_gu = [ap_(f"k4P_gu{i}", [128, 512], F32) for i in range(4)]
    P_oo = [ap_(f"k4P_oo{i}", [128, 1024], F32) for i in range(2)]
    P_tp = P_gu[0]
    P_sel = P_gu[1]
    tp_bf = P_tp.ap().bitcast(BF16)

    nb = x3_in.shape[0]

    p.inc("sync", lambda e: e.dma_start(out=id_t[:], in_=W4["ident"]), "init", 16, evt="k4id")
    p.inc("sync", lambda e: e.dma_start(out=id_f[:], in_=W4["ident_f"]), "init2", 16, evt="k4idf")
    p.inc("gpsimd", lambda e: e.dma_start(out=gateT[:], in_=W4["gateT"]), "k4small", 16,
          evt="k4gnat")
    p.on("vector", lambda v: v.memset(ones_col[:], 1.0))
    p.on("vector", lambda v: v.memset(pb8[:], -1e30))
    p.inc("vector", lambda v: v.memset(gsc8[:], -1e30), "dve", 1, evt="k4init_v")
    p.wait_evt("tensor", "k4gnat")
    p.wait_evt("tensor", "k4idf")
    p.wait_evt("tensor", "k4id")
    for e in range(NE4):
        p.inc("gpsimd", lambda q, e=e: q.dma_start(out=wT_hbm[e], in_=W4["wTf"][e]),
              "k4wc", 16, evt=f"k4cast_{e}")
    p.evt_here("pe", "k4prep")

    for ti in range(NTT):
        xsrc = x3_in[ti // NT, (ti % NT) * 128:((ti % NT) + 1) * 128, :]
        if ti > 0:
            p.wait_evt("sync", f"k4g_{ti - 1}_tp1")
        p.inc("sync", lambda e, xsrc=xsrc: e.dma_start(out=junk[:], in_=xsrc), f"k4g{ti % 2}", 16,
              evt=f"k4g_{ti}_ld")
        for g in range(2):
            p.wait_evt("tensor", f"k4g_{ti}_ld")
            if ti > 0 and g == 0:
                p.wait_evt("tensor", f"k4g_{ti - 1}_xT")
            for jj in range(4):
                j = g * 4 + jj
                p.inc("tensor", lambda t, j=j, jj=jj: t.matmul(
                    P_tp.ap()[:, jj * 128:(jj + 1) * 128], junk[:, j * 128:(j + 1) * 128], id_f[:],
                    start=(jj == 0), stop=True, is_transpose=True, skip_group_check=True),
                    "pe", 1, evt=f"k4g_{ti}_tp{g}")
            p.wait_evt("vector", f"k4g_{ti}_tp{g}")
            if ti > 0 and g == 0:
                p.wait_evt("vector", f"k4g_{ti - 1}_mm")
            p.inc("vector", lambda v, g=g: v.tensor_copy(xTf[:, g * 512:(g + 1) * 512], P_tp.ap()[:, 0:512]),
                  "dve", 1, evt=f"k4g_{ti}_xT")
            p.wait_evt("tensor", f"k4g_{ti}_xT")
        if ti > 0:
            p.wait_evt("tensor", f"k4g_{ti - 1}_sig")
        for j in range(8):
            p.inc("tensor", lambda t, j=j: t.matmul(
                P_sel.ap()[:, 0:16], xTf[:, j * 128:(j + 1) * 128], gateT[:, j * 16:(j + 1) * 16],
                start=(j == 0), stop=(j == 7), skip_group_check=True),
                "pe", 1, evt=f"k4g_{ti}_mm")
        p.wait_evt("scalar", f"k4g_{ti}_mm")
        p.inc("scalar", lambda s: s.activation(probs[:], P_sel.ap()[:, 0:16], AF.Sigmoid),
              "act", 1, evt=f"k4g_{ti}_sig")
        p.wait_evt("vector", f"k4g_{ti}_sig")
        if ti == 0:
            p.wait_evt("vector", "k4init_v")
        p.inc("vector", lambda v: v.tensor_copy(
            pb8.ap().rearrange("q (g w) -> q g w", g=4)[:, :, 0:4],
            probs.ap().rearrange("q (g w) -> q g w", g=4)), "dve", 1)
        for g in range(4):
            p.inc("vector", lambda v, g=g: v.max(mxb4[:, g * 8:(g + 1) * 8],
                                                 pb8[:, g * 8:(g + 1) * 8]), "dve", 1,
                  evt=f"k4g_{ti}_m8")
        p.wait_evt("scalar", f"k4g_{ti}_m8")
        p.inc("scalar", lambda s: s.copy(mx4b[:], mxb4[:, 0:8]), "act", 1,
              evt=f"k4g_{ti}_m8b")
        p.wait_evt("vector", f"k4g_{ti}_m8b")
        p.inc("vector", lambda v: v.tensor_add(
            gsc8[:, 0:4],
            mxb4.ap().rearrange("q (g w) -> q g w", g=4)[:, :, 0:1],
            mxb4.ap().rearrange("q (g w) -> q g w", g=4)[:, :, 1:2]), "dve", 1,
            evt=f"k4g_{ti}_gsc")
        p.inc("vector", lambda v: v.max(gmx[:], gsc8[:, 0:8]), "dve", 1, evt=f"k4g_{ti}_gmx")
        p.wait_evt("scalar", f"k4g_{ti}_gmx")
        p.inc("scalar", lambda s: s.copy(gmxb[:], gmx[:]), "act", 1, evt=f"k4g_{ti}_gmxb")
        p.wait_evt("vector", f"k4g_{ti}_gmxb")
        p.inc("vector", lambda v: v.tensor_tensor(
            alw[:], gsc8[:, 0:4], gmxb[:, 1:2].broadcast_to([128, 4]), ALU.is_ge), "dve", 1)
        p.inc("vector", lambda v: v.tensor_scalar(alw[:], alw[:], 1.0, 1e9,
                                                  ALU.subtract, ALU.mult), "dve", 1)
        p.inc("vector", lambda v: v.tensor_tensor(
            msk.ap().rearrange("q (g w) -> q g w", g=4),
            probs.ap().rearrange("q (g w) -> q g w", g=4),
            alw.ap().rearrange("q (g w) -> q g w", g=4).broadcast_to([128, 4, 4]),
            ALU.add), "dve", 1)
        p.inc("vector", lambda v: v.max(mx4[:], msk[:]), "dve", 1, evt=f"k4g_{ti}_mx4")
        p.wait_evt("scalar", f"k4g_{ti}_mx4")
        p.inc("scalar", lambda s: s.copy(mx4b[:], mx4[:]), "act", 1, evt=f"k4g_{ti}_mx4b")
        p.wait_evt("vector", f"k4g_{ti}_mx4b")
        p.inc("vector", lambda v: v.scalar_tensor_tensor(
            wsel[:], msk[:], mx4b[:, 3:4], probs[:], ALU.is_ge, ALU.mult), "dve", 1,
            evt=f"k4g_{ti}_ws")
        p.wait_evt("scalar", f"k4g_{ti}_ws")
        p.inc("scalar", lambda s: s.copy(wselb[:], wsel[:]), "act", 1, evt=f"k4g_{ti}_wsb")
        p.wait_evt("vector", f"k4g_{ti}_wsb")
        p.inc("vector", lambda v: v.reduce_sum(gsum[:], wselb[:], axis=AX.X), "dve", 1,
              evt=f"k4g_{ti}_gs")
        p.wait_evt("scalar", f"k4g_{ti}_gs")
        p.inc("scalar", lambda s: s.activation(gsumb[:], gsum[:], AF.Relu), "act", 1,
              evt=f"k4g_{ti}_gsb")
        p.wait_evt("vector", f"k4g_{ti}_gsb")
        p.inc("vector", lambda v: v.tensor_single_scalar(grc[:], gsumb[:], 1e-6, ALU.add),
              "dve", 1, evt=f"k4g_{ti}_cl")
        p.wait_evt("scalar", f"k4g_{ti}_cl")
        p.inc("scalar", lambda s: s.copy(grc2[:], grc[:]), "act", 1, evt=f"k4g_{ti}_clb")
        p.wait_evt("vector", f"k4g_{ti}_clb")
        p.inc("vector", lambda v: v.reciprocal(grc[:], grc2[:]), "dve", 1, evt=f"k4g_{ti}_rc")
        p.wait_evt("scalar", f"k4g_{ti}_rc")
        p.inc("scalar", lambda s: s.copy(grc2[:], grc[:]), "act", 1, evt=f"k4g_{ti}_rcb")
        p.wait_evt("vector", f"k4g_{ti}_rcb")
        p.inc("vector", lambda v, ti=ti: v.tensor_tensor(
            comb[:, ti * 16:(ti + 1) * 16], wsel[:],
            grc2[:, 0:1].broadcast_to([128, 16]), ALU.mult), "dve", 1, evt=f"k4g_{ti}_done")
    p.evt_here("dve", "k4gates")

    k4_jstore = [None, None]
    fhist_sg = []
    fhist_gu = []
    ohist = []
    for b in range(nb):
        for i in range(NT):
            if b > 0 and i == 0:
                p.wait_evt("gpsimd", f"k4xb_{b - 1}_{NT - 1}tp")
            if i > 0:
                p.wait_evt("gpsimd", f"k4xb_{b}_{i - 1}tp")
            p.inc("gpsimd", lambda g, b=b, i=i: g.dma_start(
                out=xb[:], in_=x3_in[b, i * 128:(i + 1) * 128, :]), f"k4xb{i % 2}", 16,
                evt=f"k4xb_{b}_{i}")
            p.wait_evt("tensor", f"k4xb_{b}_{i}")
            if b == 0 and i == 0:
                p.wait_evt("tensor", "k4gates")
                p.wait_evt("tensor", "k4prep")
            if b > 0 and i == 0:
                p.wait_evt("tensor", fhist_sg[-1])
                p.wait_evt("tensor", fhist_sg[-2])
            for g in range(2):
                for jj in range(4):
                    j = g * 4 + jj
                    p.inc("tensor", lambda t, j=j, jj=jj: t.matmul(
                        tp_bf[:, jj * 128:(jj + 1) * 128], xb[:, j * 128:(j + 1) * 128], id_t[:],
                        start=(jj == 0), stop=True, is_transpose=True, skip_group_check=True),
                        "pe", 1, evt=f"k4tp_{b}_{i}_{g}")
                p.wait_evt("vector", f"k4tp_{b}_{i}_{g}")
                if b > 0 and i == 0 and g == 0:
                    p.wait_evt("vector", f"k4mm_{b - 1}_done")
                p.inc("vector", lambda v, g=g, i=i: v.tensor_copy(
                    x3T.ap().rearrange("q (hc t) -> q hc t", hc=8)[:, g * 4:(g + 1) * 4,
                                                                   i * 128:(i + 1) * 128],
                    tp_bf[:, 0:512].rearrange("q (hc t) -> q hc t", hc=4)),
                    "dve", 1, evt=f"k4x3T_{b}_{i}_{g}")
                p.wait_evt("tensor", f"k4x3T_{b}_{i}_{g}")
            p.evt_here("pe", f"k4xb_{b}_{i}tp")
        p.evt_here("dve", f"k4x3T_{b}")
        w2_pend = [None]
        for e in range(NE4):
            eb = e % 2
            if b == 0 and e < 2:
                p.wait_evt("gpsimd", f"k4cast_{e}")
            if e < 2:
                if b > 0:
                    p.wait_evt("gpsimd", f"k4w_{b - 1}_{NE4 - 2 + e}_used")
            else:
                p.wait_evt("gpsimd", f"k4w_{b}_{e - 2}_used")
            p.inc("gpsimd", lambda g, e=e, eb=eb: g.dma_start(
                out=wbuf[eb][:], in_=wT_hbm[e]), f"k4w{eb}", 16, evt=f"k4w_{b}_{e}")
            for tc in range(NCH4):
                for ftile in range(4):
                    fseq = len(fhist_sg)
                    gb = P_gu[(2 * fseq) % 4]
                    ub = P_gu[(2 * fseq + 1) % 4]
                    sg = sgT[fseq % 2]
                    if e == 0 and tc == 0 and ftile == 0:
                        p.wait_evt("tensor", f"k4x3T_{b}")
                    if tc == 0 and ftile == 0:
                        p.wait_evt("tensor", f"k4w_{b}_{e}")
                    if fseq >= 2:
                        p.wait_evt("tensor", fhist_sg[fseq - 2])
                    for hc in range(8):
                        p.inc("tensor", lambda t, eb=eb, ftile=ftile, hc=hc, tc=tc, gb=gb: t.matmul(
                            gb.ap()[:, 0:512],
                            wbuf[eb][:, hc * 512 + ftile * 128: hc * 512 + (ftile + 1) * 128],
                            x3T[:, hc * L + tc * 512: hc * L + (tc + 1) * 512],
                            start=(hc == 0), stop=(hc == 7), skip_group_check=True),
                            "pe", 1, evt=(f"k4gm_{b}_{e}_{tc}_{ftile}" if hc == 7 else None))
                    if fseq >= 2:
                        p.wait_evt("tensor", fhist_gu[fseq - 2])
                    for hc in range(8):
                        p.inc("tensor", lambda t, eb=eb, ftile=ftile, hc=hc, tc=tc, ub=ub: t.matmul(
                            ub.ap()[:, 0:512],
                            wbuf[eb][:, 4096 + hc * 512 + ftile * 128: 4096 + hc * 512 + (ftile + 1) * 128],
                            x3T[:, hc * L + tc * 512: hc * L + (tc + 1) * 512],
                            start=(hc == 0), stop=(hc == 7), skip_group_check=True),
                            "pe", 1, evt=(f"k4um_{b}_{e}_{tc}_{ftile}" if hc == 7 else None))
                    p.wait_evt("scalar", f"k4gm_{b}_{e}_{tc}_{ftile}")
                    if fseq >= 2:
                        p.wait_evt("scalar", fhist_gu[fseq - 2])
                    p.inc("scalar", lambda s, sg=sg, gb=gb: s.activation(sg[:], gb.ap()[:], AF.Silu),
                          "act", 1, evt=f"k4sg_{b}_{e}_{tc}_{ftile}")
                    fhist_sg.append(f"k4sg_{b}_{e}_{tc}_{ftile}")
                    p.wait_evt("vector", f"k4sg_{b}_{e}_{tc}_{ftile}")
                    p.wait_evt("vector", f"k4um_{b}_{e}_{tc}_{ftile}")
                    p.inc("vector", lambda v, ftile=ftile, sg=sg, ub=ub, gub=guT[tc % 2]:
                          v.tensor_tensor(
                        gub[:, ftile * 512:(ftile + 1) * 512], sg[:], ub.ap()[:, 0:512],
                        ALU.mult), "dve", 1, evt=f"k4gu_{b}_{e}_{tc}_{ftile}v")
                    fhist_gu.append(f"k4gu_{b}_{e}_{tc}_{ftile}v")

                def w2_emit(e=e, eb=eb, tc=tc, b=b, gub=guT[tc % 2]):
                    p.wait_evt("tensor", f"k4gu_{b}_{e}_{tc}_3v")
                    for tt in range(4):
                        gt = tc * 4 + tt
                        oseq = len(ohist)
                        ob = P_oo[oseq % 2]
                        if oseq >= 2:
                            p.wait_evt("tensor", ohist[oseq - 2])
                        for fc in range(4):
                            for nh in range(2):
                                p.inc("tensor", lambda t, fc=fc, nh=nh, tt=tt, ob=ob: t.matmul(
                                    ob.ap()[:, nh * 512:(nh + 1) * 512],
                                    gub[:, fc * 512 + tt * 128: fc * 512 + (tt + 1) * 128],
                                    wbuf[eb][:, 8192 + fc * 1024 + nh * 512: 8192 + fc * 1024 + (nh + 1) * 512],
                                    start=(fc == 0), stop=(fc == 3), skip_group_check=True),
                                    "pe", 1,
                                    evt=(f"k4o_{b}_{e}_{tc}_{tt}" if fc == 3 and nh == 1 else None))
                        p.wait_evt("vector", f"k4o_{b}_{e}_{tc}_{tt}")
                        ti = b * NT + tc * 4 + tt
                        if e < 16:
                            ccol = comb[:, ti * 16 + e: ti * 16 + e + 1]
                        else:
                            ccol = ones_col[:, 0:1]
                        if e == 0:
                            p.inc("vector", lambda v, gt=gt, ccol=ccol, ob=ob: v.tensor_scalar(
                                acc[:, gt * 1024:(gt + 1) * 1024], ob.ap()[:], ccol, None, ALU.mult),
                                "dve", 1, evt=f"k4acc_{b}_{e}_{tc}_{tt}v")
                        else:
                            p.inc("vector", lambda v, gt=gt, ccol=ccol, ob=ob: v.scalar_tensor_tensor(
                                acc[:, gt * 1024:(gt + 1) * 1024], ob.ap()[:], ccol,
                                acc[:, gt * 1024:(gt + 1) * 1024], ALU.mult, ALU.add),
                                "dve", 1, evt=f"k4acc_{b}_{e}_{tc}_{tt}v")
                        ohist.append(f"k4acc_{b}_{e}_{tc}_{tt}v")
                    if tc == NCH4 - 1:
                        p.evts[f"k4w_{b}_{e}_used"] = p.evts[f"k4o_{b}_{e}_{NCH4 - 1}_3"]
                if w2_pend[0] is not None:
                    w2_pend[0]()
                w2_pend[0] = w2_emit
        while w2_pend[0] is not None:
            fn, w2_pend[0] = w2_pend[0], None
            fn()
        p.evts[f"k4mm_{b}_done"] = p.evts[f"k4o_{b}_{NE4 - 1}_{NCH4 - 1}_3"]
        for i in range(NT):
            ir = i % 2
            prev2 = (b, i - 2) if i >= 2 else ((b - 1, NT - 2 + i) if b > 0 else None)
            if prev2 is not None:
                p.wait_evt("sync", f"k4sres_{prev2[0]}_{prev2[1]}")
            if b == 0 and i == 0:
                p.wait_evt("sync", "k4gates")
            p.inc("sync", lambda e, b=b, i=i, ir=ir: e.dma_start(
                out=x3res[ir][:], in_=x3_in[b, i * 128:(i + 1) * 128, :]), "k4xr", 16,
                evt=f"k4xr_{b}_{i}")
            tc, tt = divmod(i, 4)
            p.wait_evt("vector", f"k4acc_{b}_{NE4 - 1}_{tc}_{tt}v")
            p.wait_evt("vector", f"k4xr_{b}_{i}")
            p.inc("vector", lambda v, i=i, ir=ir: v.tensor_add(
                s_ln[:], acc[:, i * 1024:(i + 1) * 1024], x3res[ir][:]), "dve", 1,
                evt=f"k4s_{b}_{i}")
            p.evts[f"k4sres_{b}_{i}"] = p.evts[f"k4s_{b}_{i}"]
            emit_ln_store(p, ln, s_ln[:], junkb[ir][:],
                          [x4_out[b, i * 128:(i + 1) * 128, :]],
                          f"k4s_{b}_{i}", f"k4ln_{b}_{i}", war_evt=k4_jstore[ir])
            k4_jstore[ir] = f"k4ln_{b}_{i}_store"
        p.evts[f"k4_b{b}_done"] = p.evts[f"k4ln_{b}_{NT - 1}_store"]
    p.wait_sem_now("sync", "st")


def tile_xy_k4(ti, x3_in, x4_out):
    b, i = divmod(ti, NT)
    return [(x3_in[b, i * 128:(i + 1) * 128, :], x4_out[b, i * 128:(i + 1) * 128, :])]


def build_k4():
    nc = bass.Bass("TRN2", target_bir_lowering=False)
    x3_in = nc.dram_tensor("x3", [B_PER_CORE, L, D], F32, kind="ExternalInput").ap()
    x4_out = nc.dram_tensor("x4", [B_PER_CORE, L, D], F32, kind="ExternalOutput").ap()
    W4 = {
        "gateT": nc.dram_tensor("gateT_h", [128, 128], F32, kind="ExternalInput").ap(),
        "wTf": nc.dram_tensor("wTf", [18, 128, 12288], F32, kind="ExternalInput").ap(),
        "ident": nc.dram_tensor("ident", [128, 128], BF16, kind="ExternalInput").ap(),
        "ident_f": nc.dram_tensor("ident_f", [128, 128], F32, kind="ExternalInput").ap(),
    }
    p = Prog(nc)
    emit_k4(nc, p, x3_in, x4_out, W4)
    p.run()
    return nc


def _k4_wTf(inputs):
    f32 = lambda x: np.asarray(x, dtype=np.float32)

    def pA(w):
        R, C = w.shape
        return w.reshape(R, C // 128, 128).transpose(2, 1, 0).reshape(128, R * C // 128)

    w1_e, w3_e = f32(inputs["L1_ffn_w1_e"]), f32(inputs["L1_ffn_w3_e"])
    w2_e = f32(inputs["L1_ffn_w2_e"])
    w1_s, w3_s = f32(inputs["L1_ffn_w1_s"]), f32(inputs["L1_ffn_w3_s"])
    w2_s = f32(inputs["L1_ffn_w2_s"])
    out = np.empty((18, 128, 12288), np.float32)
    for e in range(16):
        out[e, :, 0:4096] = pA(w1_e[e])
        out[e, :, 4096:8192] = pA(w3_e[e])
        out[e, :, 8192:12288] = pA(w2_e[e])
    w2sA = pA(w2_s)
    for s in range(2):
        out[16 + s, :, 0:4096] = pA(w1_s[s * 512:(s + 1) * 512])
        out[16 + s, :, 4096:8192] = pA(w3_s[s * 512:(s + 1) * 512])
        out[16 + s, :, 8192:12288] = w2sA[:, s * 4096:(s + 1) * 4096]
    return out


def k4_inputs(inputs, core, x3):
    f32 = lambda x: np.ascontiguousarray(np.asarray(x, dtype=np.float32))
    return {
        "x3": (np.ascontiguousarray(x3[core * B_PER_CORE:(core + 1) * B_PER_CORE])
               if x3 is not None else None),
        "gateT_h": np.ascontiguousarray(
            f32(inputs["L1_ffn_w_gate"]).reshape(16, 8, 128).transpose(2, 1, 0).reshape(128, 128)),
        "wTf": _k4_wTf(inputs),
        "ident": _ident_bf16(),
        "ident_f": np.eye(128, dtype=np.float32),
    }


def all_engine_fence(p):
    for eng in Prog.ENGS:
        if eng == "tensor":
            p.inc(eng, lambda t: t.nop(), "fence", 1)
        elif eng == "vector":
            p.inc(eng, lambda v: v.nop(), "fence", 1)
        elif eng == "scalar":
            p.inc(eng, lambda s: s.nop(), "fence", 1)
        elif eng == "sync":
            p.inc(eng, lambda e: e.nop(), "fence", 1)
        else:
            p.inc(eng, lambda g: g.nop(), "fence", 1)
    for eng in Prog.ENGS:
        p.wait_sem_now(eng, "fence")


def build_fused():
    nc = bass.Bass("TRN2", target_bir_lowering=False)
    x0 = nc.dram_tensor("x0", [B_PER_CORE, L, D], F32, kind="ExternalInput").ap()
    x4 = nc.dram_tensor("x4", [B_PER_CORE, L, D], F32, kind="ExternalOutput").ap()
    x1 = nc.dram_tensor("x1i", [B_PER_CORE, L, D], F32, kind="Internal").ap()
    x2 = nc.dram_tensor("x2i", [B_PER_CORE, L, D], F32, kind="Internal").ap()
    x3 = nc.dram_tensor("x3i", [B_PER_CORE, L, D], F32, kind="Internal").ap()
    ident = nc.dram_tensor("ident", [128, 128], BF16, kind="ExternalInput").ap()
    ident_f = nc.dram_tensor("ident_f", [128, 128], F32, kind="ExternalInput").ap()
    W1 = {}
    for n in ("Wq", "Wk", "Wv", "Wo"):
        W1[n + "T"] = nc.dram_tensor(n + "T", [128, 8192], F32, kind="ExternalInput").ap()
    for n in ("bq", "bk", "bv", "bo"):
        W1[n] = nc.dram_tensor(n, [1024], F32, kind="ExternalInput").ap()
    W2 = {
        "selT": nc.dram_tensor("selT", [128, 128], F32, kind="ExternalInput").ap(),
        "keys": nc.dram_tensor("keys", [16, 1024, 256], F32, kind="ExternalInput").ap(),
        "values": nc.dram_tensor("values", [16, 256, 1024], F32, kind="ExternalInput").ap(),
        "rs_col": nc.dram_tensor("rs_col", [128, 1], F32, kind="ExternalInput").ap(),
        "ident_f": ident_f,
    }
    shapes3 = {
        "in_proj": [1024, 4128], "out_proj": [2048, 1024], "theta_cols": [128, 16],
        "dist32n": [32, 2048], "patft": [32, 16 * 128], "sc_col": [32, 1],
        "convk_s": [32, 4], "convb_s": [32, 1], "convk_cols": [128, 4 * 32],
        "convb_cols": [128, 32], "decay": [1, 28], "anchor": [1, 4], "deriv": [1, 3],
        "W_re": [64, 64], "W_im": [64, 64], "ns_re": [64, 1], "ns_im": [64, 1],
    }
    W3 = {k: nc.dram_tensor(k, v, F32, kind="ExternalInput").ap() for k, v in shapes3.items()}
    W3["bp16"] = nc.dram_tensor("bp16", [128, 16 * 32], BF16, kind="ExternalInput").ap()
    W3["ident"] = ident
    W4 = {
        "gateT": nc.dram_tensor("gateT_h", [128, 128], F32, kind="ExternalInput").ap(),
        "wTf": nc.dram_tensor("wTf", [18, 128, 12288], F32, kind="ExternalInput").ap(),
        "ident": ident,
        "ident_f": ident_f,
    }
    p = Prog(nc)
    sb0, ps0 = nc.sbuf_base, nc.psum_base
    emit_k1(nc, p, x0, x1, W1, ident)
    all_engine_fence(p)
    nc.sbuf_base, nc.psum_base = sb0, ps0
    emit_k2(nc, p, x1, x2, W2, ident)
    all_engine_fence(p)
    nc.sbuf_base, nc.psum_base = sb0, ps0
    emit_k3(nc, p, x2, x3, W3)
    all_engine_fence(p)
    nc.sbuf_base, nc.psum_base = sb0, ps0
    emit_k4(nc, p, x3, x4, W4)
    p.run()
    return nc


def fused_inputs(inputs, core):
    d = k1_inputs(inputs, core)
    d["x0"] = np.ascontiguousarray(inputs["L0_attn_query"][core * B_PER_CORE:(core + 1) * B_PER_CORE])
    d2 = k2_inputs(inputs, core, None)
    d2.pop("x1")
    d3 = k3_inputs(inputs, core, None)
    d3.pop("x2")
    d4 = k4_inputs(inputs, core, None)
    d4.pop("x3")
    d.update(d2)
    d.update(d3)
    d.update(d4)
    return d


def kernel(**inputs):
    cores = list(range(NCORE))
    nc = build_fused()
    res = run_bass_kernel_spmd(nc, [fused_inputs(inputs, c) for c in cores], core_ids=cores)
    return np.concatenate([r["x4"] for r in res.results], axis=0)


# version 14
# speedup vs baseline: 1.2112x; 1.2112x over previous
import numpy as np
import concourse.bass as bass
import concourse.mybir as mybir
from concourse.bass_utils import run_bass_kernel_spmd

F32 = mybir.dt.float32
BF16 = mybir.dt.bfloat16
I32 = mybir.dt.int32
AF = mybir.ActivationFunctionType
ALU = mybir.AluOpType
AX = mybir.AxisListType

RSQRT_MAGIC = 0x5F3759DF
RND_MAGIC = 12582912.0
DEBUG_DUMPS = False
NCORE = 8
B_PER_CORE = 4
L = 2048
D = 1024
NT = L // 128


class Prog:
    ENGS = ("sync", "tensor", "vector", "scalar", "gpsimd")

    def __init__(self, nc):
        self.nc = nc
        self.steps = {e: [] for e in self.ENGS}
        self.sems = {}
        self.counts = {}
        self.evts = {}

    def sem(self, name):
        if name not in self.sems:
            self.sems[name] = self.nc.alloc_semaphore(name)
            self.counts[name] = 0
        return self.sems[name]

    AUTO_VEC = True

    def _auto_vec_wait(self, eng):
        if eng == "vector" and self.AUTO_VEC:
            self.sem("dve")
            s, cnt = self.sems["dve"], self.counts["dve"]
            if cnt > 0:
                self.steps[eng].append(lambda e, s=s, cnt=cnt: e.wait_ge(s, cnt))

    def onp(self, fn, evt=None):
        self.sem("pool")
        s, cnt = self.sems["pool"], self.counts["pool"]
        if cnt > 0:
            self.steps["gpsimd"].append(lambda e, s=s, cnt=cnt: e.wait_ge(s, cnt))
        self.inc("gpsimd", fn, "pool", 1, evt=evt)

    def on(self, eng, fn):
        if eng == "vector" and self.AUTO_VEC:
            self.inc(eng, fn, "dve", 1)
            return
        self.steps[eng].append(fn)

    def inc(self, eng, fn, sem, amt=1, evt=None):
        self._auto_vec_wait(eng)
        self.sem(sem)
        self.counts[sem] += amt
        cnt = self.counts[sem]
        s = self.sems[sem]

        def wrapped(e, fn=fn, s=s, amt=amt):
            fn(e).then_inc(s, amt)

        self.steps[eng].append(wrapped)
        if evt is not None:
            self.evts[evt] = (sem, cnt)
        return cnt

    def evt_here(self, sem, evt):
        self.sem(sem)
        self.evts[evt] = (sem, self.counts[sem])

    def wait_evt(self, eng, evt):
        sem, cnt = self.evts[evt]
        s = self.sems[sem]
        self.steps[eng].append(lambda e, s=s, cnt=cnt: e.wait_ge(s, cnt))

    def wait_sem_now(self, eng, sem):
        self.sem(sem)
        s, cnt = self.sems[sem], self.counts[sem]
        if cnt > 0:
            self.steps[eng].append(lambda e, s=s, cnt=cnt: e.wait_ge(s, cnt))

    def self_wait(self, eng, sem):
        self.sem(sem)
        s, cnt = self.sems[sem], self.counts[sem]
        if cnt > 0:
            self.steps[eng].append(lambda e, s=s, cnt=cnt: e.wait_ge(s, cnt))

    def dump(self, name, ap, evt):
        t = self.nc.dram_tensor(f"dbg_{name}", list(ap.shape), ap.dtype, kind="ExternalOutput").ap()
        self.wait_evt("gpsimd", evt)
        self.inc("gpsimd", lambda e, t=t, ap=ap: e.dma_start(out=t, in_=ap), "dbgst", 16)

    def finish_dumps(self):
        self.wait_sem_now("gpsimd", "dbgst")

    def run(self):
        with self.nc.Block() as block:
            for name in self.ENGS:
                def runner(eng, name=name):
                    for f in self.steps[name]:
                        f(eng)
                getattr(block, name)(runner)


def quake_rsqrt(p, v_ap, out_ap, scr_i32_ap, scr_f32_ap, half_ap, magic_tile_ap, iters=3):
    p.on("vector", lambda v: v.tensor_single_scalar(scr_i32_ap, v_ap.bitcast(I32), 1,
                                                    ALU.arith_shift_right))
    p.on("vector", lambda v: v.tensor_sub(out_ap.bitcast(I32), magic_tile_ap, scr_i32_ap))
    p.on("vector", lambda v: v.tensor_scalar_mul(half_ap, v_ap, 0.5))
    for _ in range(iters):
        p.on("vector", lambda v: v.tensor_mul(scr_f32_ap, out_ap, out_ap))
        p.on("vector", lambda v: v.tensor_mul(scr_f32_ap, scr_f32_ap, half_ap))
        p.on("vector", lambda v: v.tensor_scalar(scr_f32_ap, scr_f32_ap, -1.0, 1.5, ALU.mult, ALU.add))
        p.on("vector", lambda v: v.tensor_mul(out_ap, out_ap, scr_f32_ap))


class LNPack:
    def __init__(self, nc, pfx=""):
        a = nc.alloc_sbuf_tensor
        self.lsum = a(pfx + "lsum", [128, 1], F32)
        self.lsq = a(pfx + "lsq", [128, 1], F32)
        self.lmean = a(pfx + "lmean", [128, 1], F32)
        self.lmz = a(pfx + "lmz", [128, 1], F32)
        self.lvar = a(pfx + "lvar", [128, 1], F32)
        self.lu = a(pfx + "lu", [128, 1], F32)
        self.lneg = a(pfx + "lneg", [128, 1], F32)
        self.linv = a(pfx + "linv", [128, 1], F32)
        self.prev_store = None

    def init(self, p):
        pass


def emit_ln_store(p, ln, s_ln_ap, junk_ap, store_list, s_evt, tag, n=1024, eps=1e-5,
                  war_evt="auto"):
    rn = 1.0 / n
    p.wait_evt("vector", s_evt)
    if war_evt == "auto":
        war_evt = ln.prev_store
    if war_evt is not None:
        p.wait_evt("vector", war_evt)
    p.inc("vector", lambda v: v.reduce_sum(ln.lsum[:], s_ln_ap, axis=AX.X), "dve", 1)
    p.inc("vector", lambda v: v.tensor_mul(junk_ap, s_ln_ap, s_ln_ap), "dve", 1)
    p.inc("vector", lambda v: v.reduce_sum(ln.lsq[:], junk_ap, axis=AX.X), "dve", 1)
    p.inc("vector", lambda v: v.tensor_scalar_mul(ln.lmean[:], ln.lsum[:], rn), "dve", 1)
    p.inc("vector", lambda v: v.tensor_mul(ln.lmz[:], ln.lmean[:], ln.lmean[:]), "dve", 1)
    p.inc("vector", lambda v: v.tensor_scalar(ln.lvar[:], ln.lsq[:], rn, eps,
                                              ALU.mult, ALU.add), "dve", 1)
    p.inc("vector", lambda v: v.tensor_sub(ln.lvar[:], ln.lvar[:], ln.lmz[:]), "dve", 1,
          evt=f"{tag}_LC")
    p.wait_evt("scalar", f"{tag}_LC")
    p.inc("scalar", lambda s: s.activation(ln.lu[:], ln.lvar[:], AF.Ln), "act", 1)
    p.self_wait("scalar", "act")
    p.inc("scalar", lambda s: s.activation(ln.linv[:], ln.lu[:], AF.Exp, scale=-0.5),
          "act", 1, evt=f"{tag}_LF")
    p.wait_evt("vector", f"{tag}_LF")
    p.inc("vector", lambda v: v.scalar_tensor_tensor(
        junk_ap, s_ln_ap, ln.lmean[:, 0:1],
        ln.linv[:, 0:1].broadcast_to([128, n]), ALU.subtract, ALU.mult),
        "dve", 1, evt=f"{tag}_norm")
    p.wait_evt("sync", f"{tag}_norm")
    for dst in store_list:
        p.inc("sync", lambda e, dst=dst: e.dma_start(out=dst, in_=junk_ap), "st", 16,
              evt=f"{tag}_store")
    ln.prev_store = f"{tag}_store"


def emit_k1(nc, p, x_in, x1_out, W, ident):
    a = nc.alloc_sbuf_tensor
    ap_ = nc.alloc_psum_tensor

    id_t = a("id_t", [128, 128], BF16)
    wT = {n: a(f"{n}Ts", [128, 8 * 1024], BF16) for n in ("Wq", "Wk", "Wv", "Wo")}
    brow = {n: a(f"{n}_row", [1, 1024], BF16) for n in ("bq", "bk", "bv", "bo")}
    onesl = a("onesl", [1, 128], BF16)
    ones_col_bf = a("ones_col_bf", [128, 1], BF16)
    xres = a("xres", [128, NT * 1024], F32)
    phiQT = a("phiQT", [128, NT * 1024], BF16)
    xb0 = a("xb0", [128, 1024], BF16)
    xb = [xb0, xb0]
    xT = [a(f"xT{i}", [128, 1024], BF16) for i in range(2)]
    phiK = [a(f"phiK{i}", [128, 1024], BF16) for i in range(2)]
    Vt = [a(f"Vt{i}", [128, 1024], BF16) for i in range(2)]
    ctx = [a(f"ctx{i}", [128, 1024], BF16) for i in range(2)]
    ctxT0 = a("ctxT0", [128, 1024], BF16)
    ctxT = [ctxT0, ctxT0]
    t0 = a("t0", [128, 1024], F32)
    scr2 = a("scr2", [128, 1024], F32)
    s_ln = a("s_ln", [128, 1024], F32)
    junk0 = a("junk", [128, 1024], F32)
    junkb = [junk0, t0]
    k1_jstore = [None, None]
    S_bd = a("S_bd", [128, 8 * 128], BF16)
    z2 = a("z2", [128, 16], BF16)
    dnr = a("dnr", [128, 16], F32)
    dnr2 = a("dnr2", [128, 16], F32)
    qs_f = a("qs_f", [128, 16], F32)
    qs_h = a("qs_h", [128, 16], F32)
    ln = LNPack(nc)

    P_tp = ap_("P_tp", [128, 512], F32)
    P_kq = ap_("P_kq", [128, 1024], F32)
    P_v = ap_("P_v", [128, 1024], F32)
    P_s = ap_("P_s", [64, 1024], F32)
    P_z = ap_("P_z", [128, 512], F32)
    tp_bf = P_tp.ap().bitcast(BF16)

    p.inc("sync", lambda e: e.dma_start(out=id_t[:], in_=ident), "init", 16, evt="id")
    ln.init(p)
    p.on("vector", lambda v: v.memset(onesl[:], 1.0))
    p.on("vector", lambda v: v.memset(ones_col_bf[:], 1.0))
    p.on("vector", lambda v: v.memset(S_bd[:], 0.0))
    p.inc("vector", lambda v: v.memset(z2[:], 0.0), "dve", 1, evt="init_dve")
    for n in ("bq", "bk", "bv", "bo"):
        p.inc("gpsimd", lambda e, n=n: e.dma_start(out=brow[n][0:1, :], in_=W[n][None, :]), "bias", 16)
    for n in ("Wq", "Wk", "Wv", "Wo"):
        p.inc("gpsimd", lambda e, n=n: e.dma_start(out=wT[n][:], in_=W[n + "T"]), "wst", 16)
    p.wait_sem_now("tensor", "bias")
    p.wait_sem_now("tensor", "wst")
    p.wait_evt("tensor", "init_dve")
    p.wait_evt("tensor", "id")
    p.evt_here("pe", "prep_done")

    def tp_transpose(src_bf_ap, dst_bf_ap, nch, tag, pre_evac_wait=None):
        for g in range((nch + 3) // 4):
            njj = min(4, nch - g * 4)
            for jj in range(njj):
                j = g * 4 + jj
                p.inc("tensor",
                      lambda t, j=j, jj=jj: t.matmul(
                          tp_bf[:, jj * 128:(jj + 1) * 128],
                          src_bf_ap[:, j * 128:(j + 1) * 128], id_t[:],
                          start=(jj == 0), stop=True, is_transpose=True,
                          skip_group_check=True),
                      "pe", 1, evt=f"{tag}_tp{g}")
            p.wait_evt("vector", f"{tag}_tp{g}")
            if pre_evac_wait is not None and g == 0:
                p.wait_evt("vector", pre_evac_wait)
            p.inc("vector",
                  lambda v, g=g, njj=njj: v.tensor_copy(
                      dst_bf_ap[:, g * 4 * 128:(g * 4 + njj) * 128],
                      tp_bf[:, 0:njj * 128]),
                  "dve", 1, evt=f"{tag}_ev")
            p.wait_evt("tensor", f"{tag}_ev")

    def proj_matmuls(ps_ap, xT_ap, wTn, brow_n, tag):
        for h in range(2):
            for j in range(8):
                p.inc("tensor",
                      lambda t, h=h, j=j: t.matmul(
                          ps_ap[:, h * 512:(h + 1) * 512],
                          xT_ap[:, j * 128:(j + 1) * 128],
                          wT[wTn][:, j * 1024 + h * 512: j * 1024 + (h + 1) * 512],
                          start=(j == 0), stop=False, skip_group_check=True),
                      "pe", 1)
            p.inc("tensor",
                  lambda t, h=h: t.matmul(
                      ps_ap[:, h * 512:(h + 1) * 512],
                      onesl[:],
                      brow[brow_n][:, h * 512:(h + 1) * 512],
                      start=False, stop=True, skip_group_check=True),
                  "pe", 1, evt=f"{tag}_h{h}")

    def elu1(ps_ap, out_bf_ap, tag):
        p.inc("vector", lambda v: v.tensor_single_scalar(scr2[:], ps_ap, 0.0, ALU.min),
              "dve", 1, evt=f"{tag}_min")
        p.inc("vector", lambda v: v.tensor_single_scalar(t0[:], ps_ap, 0.0, ALU.max),
              "dve", 1, evt=f"{tag}_max")
        p.wait_evt("scalar", f"{tag}_min")
        p.inc("scalar", lambda s: s.activation(scr2[:], scr2[:], AF.Exp), "act", 1, evt=f"{tag}_exp")
        p.wait_evt("vector", f"{tag}_exp")
        p.wait_evt("vector", f"{tag}_max")
        p.inc("vector", lambda v: v.tensor_add(out_bf_ap, scr2[:], t0[:]), "dve", 1, evt=f"{tag}_out")

    nb = x_in.shape[0]
    for b in range(nb):
        for i in range(NT):
            ib = i % 2
            if b > 0 and i == 0:
                p.wait_evt("sync", f"k1_b{b - 1}_done")
            p.inc("sync", lambda e, b=b, i=i: e.dma_start(
                out=xres[:, i * 1024:(i + 1) * 1024], in_=x_in[b, i * 128:(i + 1) * 128, :]),
                f"xch{i}", 16, evt=f"x_{b}_{i}")
            p.wait_evt("vector", f"x_{b}_{i}")
            if i > 0:
                p.wait_evt("vector", f"k1xT_{b}_{i - 1}_ev")
            p.inc("vector", lambda v, ib=ib, i=i: v.tensor_copy(xb[ib][:], xres[:, i * 1024:(i + 1) * 1024]),
                  "dve", 1, evt=f"xb_{b}_{i}")
            if b == 0 and i == 0:
                p.wait_evt("tensor", "prep_done")
            p.wait_evt("tensor", f"xb_{b}_{i}")
            tp_transpose(xb[ib][:], xT[ib][:], 8, f"k1xT_{b}_{i}",
                         pre_evac_wait=(f"k1Q_{b}_{i - 2}_h1" if i >= 2 else None))
            if i > 0:
                p.wait_evt("tensor", f"k1Q_{b}_{i - 1}_min")
                p.wait_evt("tensor", f"k1Q_{b}_{i - 1}_max")
            proj_matmuls(P_kq.ap(), xT[ib][:], "Wk", "bk", f"k1K_{b}_{i}")
            p.wait_evt("vector", f"k1K_{b}_{i}_h1")
            if i >= 2:
                p.wait_evt("vector", f"k1SZ_{b}_{i - 2}")
            elu1(P_kq.ap(), phiK[ib][:], f"k1K_{b}_{i}")
            if i > 0:
                p.wait_evt("tensor", f"k1V_{b}_{i - 1}_out")
            proj_matmuls(P_v.ap(), xT[ib][:], "Wv", "bv", f"k1V_{b}_{i}")
            p.wait_evt("vector", f"k1V_{b}_{i}_h1")
            p.inc("vector", lambda v, ib=ib: v.tensor_copy(Vt[ib][:], P_v.ap()[:]),
                  "dve", 1, evt=f"k1V_{b}_{i}_out")
            p.wait_evt("tensor", f"k1K_{b}_{i}_out")
            p.wait_evt("tensor", f"k1V_{b}_{i}_out")
            if b > 0 and i == 0:
                p.wait_evt("tensor", f"k1Sevac_{b - 1}")
                p.wait_evt("tensor", f"k1ctx_{b - 1}_{NT - 1}")
            for h in range(16):
                p.inc("tensor", lambda t, h=h, i=i: t.matmul(
                    P_s.ap()[0:64, h * 64:(h + 1) * 64],
                    phiK[i % 2][:, h * 64:(h + 1) * 64],
                    Vt[i % 2][:, h * 64:(h + 1) * 64],
                    start=(i == 0 and h % 8 == 0), stop=(i == NT - 1),
                    skip_group_check=True),
                    "pe", 1)
            for j in range(8):
                p.inc("tensor", lambda t, j=j, i=i: t.matmul(
                    P_z.ap()[:, j:j + 1],
                    phiK[i % 2][:, j * 128:(j + 1) * 128],
                    ones_col_bf[:],
                    start=(i == 0 and j == 0), stop=(i == NT - 1),
                    skip_group_check=True),
                    "pe", 1, evt=(f"k1SZ_{b}_{i}" if j == 7 else None))
            p.wait_evt("tensor", f"k1K_{b}_{i}_min")
            p.wait_evt("tensor", f"k1K_{b}_{i}_max")
            proj_matmuls(P_kq.ap(), xT[ib][:], "Wq", "bq", f"k1Q_{b}_{i}")
            p.wait_evt("vector", f"k1Q_{b}_{i}_h1")
            elu1(P_kq.ap(), ctx[ib][:], f"k1Q_{b}_{i}")
            p.wait_evt("tensor", f"k1Q_{b}_{i}_out")
            if DEBUG_DUMPS and b == 0 and i == 0:
                p.dump("xT0", xT[0][:], f"k1xT_{b}_{i}_ev")
                p.dump("phiK0", phiK[0][:], f"k1K_{b}_{i}_out")
                p.dump("Vt0", Vt[0][:], f"k1V_{b}_{i}_out")
                p.dump("phiQ0", ctx[0][:], f"k1Q_{b}_{i}_out")
            tp_transpose(ctx[ib][:], phiQT.ap()[:, i * 1024:(i + 1) * 1024], 8, f"k1qT_{b}_{i}")
        p.wait_evt("vector", f"k1SZ_{b}_{NT - 1}")
        for hp in range(8):
            p.inc("vector", lambda v, hp=hp: v.tensor_copy(
                S_bd[0:64, hp * 128:hp * 128 + 64], P_s.ap()[0:64, (2 * hp) * 64:(2 * hp + 1) * 64]),
                "dve", 1)
            p.inc("vector", lambda v, hp=hp: v.tensor_copy(
                S_bd[64:128, hp * 128 + 64:hp * 128 + 128],
                P_s.ap()[0:64, (2 * hp + 1) * 64:(2 * hp + 2) * 64]),
                "dve", 1)
            p.inc("vector", lambda v, hp=hp: v.tensor_copy(
                z2[0:64, 2 * hp:2 * hp + 1], P_z.ap()[0:64, hp:hp + 1]), "dve", 1)
            p.inc("vector", lambda v, hp=hp: v.tensor_copy(
                z2[64:128, 2 * hp + 1:2 * hp + 2], P_z.ap()[64:128, hp:hp + 1]), "dve", 1,
                evt=(f"k1Sevac_{b}v" if hp == 7 else None))
        p.wait_evt("tensor", f"k1Sevac_{b}v")
        p.inc("tensor", lambda t: t.nop(), "pe", 1, evt=f"k1Sevac_{b}")
        if DEBUG_DUMPS and b == 0:
            p.dump("S_bd", S_bd[:], f"k1Sevac_{b}v")
            p.dump("z2", z2[:], f"k1Sevac_{b}v")
            p.dump("phiQT", phiQT[:], f"k1Sevac_{b}v")
        for i in range(NT):
            ib = i % 2
            if i > 0:
                p.wait_evt("tensor", f"k1ctx_{b}_{i - 1}")
            for hp in range(8):
                p.inc("tensor", lambda t, hp=hp, i=i: t.matmul(
                    P_kq.ap()[:, hp * 128:(hp + 1) * 128],
                    phiQT.ap()[:, i * 1024 + hp * 128: i * 1024 + (hp + 1) * 128],
                    S_bd[:, hp * 128:(hp + 1) * 128],
                    start=(hp % 4 == 0), stop=True, skip_group_check=True),
                    "pe", 1)
            for hp in range(8):
                p.inc("tensor", lambda t, hp=hp, i=i: t.matmul(
                    P_z.ap()[:, 16 + hp * 2:16 + (hp + 1) * 2],
                    phiQT.ap()[:, i * 1024 + hp * 128: i * 1024 + (hp + 1) * 128],
                    z2[:, hp * 2:(hp + 1) * 2],
                    start=(hp == 0), stop=True, skip_group_check=True),
                    "pe", 1, evt=(f"k1nd_{b}_{i}" if hp == 7 else None))
            p.wait_evt("vector", f"k1nd_{b}_{i}")
            p.inc("vector", lambda v: v.tensor_single_scalar(dnr[:], P_z.ap()[:, 16:32], 1e-6, ALU.add),
                  "dve", 1, evt=f"k1dn_{b}_{i}a")
            p.wait_evt("scalar", f"k1dn_{b}_{i}a")
            p.inc("scalar", lambda s: s.copy(qs_f[:], dnr[:]), "act", 1, evt=f"k1dn_{b}_{i}b")
            p.wait_evt("vector", f"k1dn_{b}_{i}b")
            p.inc("vector", lambda v: v.reciprocal(dnr2[:], qs_f[:]), "dve", 1, evt=f"k1dn_{b}_{i}c")
            p.wait_evt("scalar", f"k1dn_{b}_{i}c")
            p.inc("scalar", lambda s: s.copy(qs_h[:], dnr2[:]), "act", 1, evt=f"k1dn_{b}_{i}d")
            p.wait_evt("vector", f"k1dn_{b}_{i}d")
            p.inc("vector", lambda v, ib=ib: v.tensor_tensor(
                ctx[ib].ap().rearrange("q (h j) -> q h j", h=16),
                P_kq.ap().rearrange("q (h j) -> q h j", h=16),
                qs_h.ap().rearrange("q (h o) -> q h o", h=16).broadcast_to([128, 16, 64]),
                ALU.mult),
                "dve", 1, evt=f"k1ctx_{b}_{i}v")
            p.wait_evt("tensor", f"k1ctx_{b}_{i}v")
            p.inc("tensor", lambda t: t.nop(), "pe", 1, evt=f"k1ctx_{b}_{i}")
            if DEBUG_DUMPS and b == 0 and i == 0:
                p.dump("ctx0", ctx[0][:], f"k1ctx_{b}_{i}v")
                p.dump("dnr0", qs_h[:], f"k1ctx_{b}_{i}v")
            tp_transpose(ctx[ib][:], ctxT[ib][:], 8, f"k1cT_{b}_{i}",
                         pre_evac_wait=(f"k1o_{b}_{i - 1}_h1" if i >= 1 else None))
            if i > 0:
                p.wait_evt("tensor", f"k1s_{b}_{i - 1}")
            proj_matmuls(P_v.ap(), ctxT[ib][:], "Wo", "bo", f"k1o_{b}_{i}")
            p.wait_evt("vector", f"k1o_{b}_{i}_h1")
            p.inc("vector", lambda v, i=i: v.tensor_add(s_ln[:], P_v.ap()[:],
                                                        xres[:, i * 1024:(i + 1) * 1024]),
                  "dve", 1, evt=f"k1s_{b}_{i}v")
            p.wait_evt("tensor", f"k1s_{b}_{i}v")
            p.inc("tensor", lambda t: t.nop(), "pe", 1, evt=f"k1s_{b}_{i}")
            if DEBUG_DUMPS and b == 0 and i == 0:
                p.dump("sln0", s_ln[:], f"k1s_{b}_{i}v")
            emit_ln_store(p, ln, s_ln[:], junkb[i % 2][:],
                          [x1_out[b, i * 128:(i + 1) * 128, :]],
                          f"k1s_{b}_{i}v", f"k1ln_{b}_{i}", war_evt=k1_jstore[i % 2])
            k1_jstore[i % 2] = f"k1ln_{b}_{i}_store"
        p.evt_here("st", f"k1_b{b}_done")
    p.wait_sem_now("sync", "st")
    p.finish_dumps()


def build_k1():
    nc = bass.Bass("TRN2", target_bir_lowering=False)
    x_in = nc.dram_tensor("x0", [B_PER_CORE, L, D], F32, kind="ExternalInput").ap()
    x1_out = nc.dram_tensor("x1", [B_PER_CORE, L, D], F32, kind="ExternalOutput").ap()
    W = {}
    for n in ("Wq", "Wk", "Wv", "Wo"):
        W[n + "T"] = nc.dram_tensor(n + "T", [128, 8192], F32, kind="ExternalInput").ap()
    for n in ("bq", "bk", "bv", "bo"):
        W[n] = nc.dram_tensor(n, [1024], F32, kind="ExternalInput").ap()
    ident = nc.dram_tensor("ident", [128, 128], BF16, kind="ExternalInput").ap()
    p = Prog(nc)
    emit_k1(nc, p, x_in, x1_out, W, ident)
    p.run()
    return nc


def _ident_bf16():
    import ml_dtypes
    return np.eye(128, dtype=ml_dtypes.bfloat16)


def k1_inputs(inputs, core):
    x = np.ascontiguousarray(inputs["L0_attn_query"][core * B_PER_CORE:(core + 1) * B_PER_CORE])
    d = {"x0": x, "ident": _ident_bf16()}
    for n, key in (("Wq", "L0_attn_Wq"), ("Wk", "L0_attn_Wk"), ("Wv", "L0_attn_Wv"), ("Wo", "L0_attn_Wo")):
        w = np.asarray(inputs[key], dtype=np.float32)
        d[n + "T"] = np.ascontiguousarray(w.reshape(1024, 8, 128).transpose(2, 1, 0).reshape(128, 8192))
    for n, key in (("bq", "L0_attn_bq"), ("bk", "L0_attn_bk"), ("bv", "L0_attn_bv"), ("bo", "L0_attn_bo")):
        d[n] = np.ascontiguousarray(np.asarray(inputs[key], dtype=np.float32))
    return d




def emit_k2(nc, p, x1_in, x2_out, W2, ident):
    a = nc.alloc_sbuf_tensor
    ap_ = nc.alloc_psum_tensor
    NTT = B_PER_CORE * NT
    NP = NTT // 2

    id_t = a("k2_id", [128, 128], BF16)
    id_f = a("k2_idf", [128, 128], F32)
    keys_bf = a("k2_keys", [128, 16 * 8 * 256], BF16)
    vals_bf = a("k2_vals", [128, 16 * 2 * 1024], BF16)
    selT = a("k2_selT", [128, 128], F32)
    rs_col = a("k2_rs", [128, 1], F32)
    comb = a("k2_comb", [128, NTT * 16], F32)
    xres1 = [a(f"k2_xres{i}", [128, 2048], F32) for i in range(2)]
    xb = a("k2_xb", [128, 1024], BF16)
    xTf = a("k2_xTf", [128, 1024], F32)
    x1T = [a(f"k2_x1T{i}", [128, 8 * 256], BF16) for i in range(2)]
    scT = [a(f"k2_scT{i}", [128, 512], BF16) for i in range(2)]
    acc = [a(f"k2_acc{i}", [128, 2048], F32) for i in range(2)]
    s_ln = a("k2_sln", [128, 1024], F32)
    junk = a("k2_junk", [128, 1024], F32)
    junk2 = a("k2_junk2", [128, 1024], F32)
    junkb = [junk, junk2]
    k2_jstore = [None, None]
    probs = a("k2_probs", [128, 16], F32)
    mx = a("k2_mx", [128, 8], F32)
    mxb = a("k2_mxb", [128, 8], F32)
    wsel = a("k2_wsel", [128, 16], F32)
    wselb = a("k2_wselb", [128, 16], F32)
    grc2 = a("k2_grc2", [128, 1], F32)
    grcb2 = a("k2_grcb2", [128, 1], F32)
    gsum = a("k2_gsum", [128, 1], F32)
    gsumb = a("k2_gsumb", [128, 1], F32)
    grc = a("k2_grc", [128, 1], F32)
    grcb = a("k2_grcb", [128, 1], F32)
    ln = LNPack(nc, "k2_")

    P_tp = ap_("k2P_tp", [128, 512], F32)
    P_sc = [ap_("k2P_sc0", [128, 512], F32), ap_("k2P_sc1", [128, 512], F32)]
    P_vo = [ap_(f"k2P_vo{i}", [128, 1024], F32) for i in range(2)]
    P_sel = ap_("k2P_sel", [128, 512], F32)
    tp_bf = P_tp.ap().bitcast(BF16)

    p.inc("sync", lambda e: e.dma_start(out=id_t[:], in_=ident), "init", 16, evt="k2id")
    p.inc("sync", lambda e: e.dma_start(out=id_f[:], in_=W2["ident_f"]), "init2", 16, evt="k2idf")
    p.inc("gpsimd", lambda e: e.dma_start(out=rs_col[:], in_=W2["rs_col"]), "wrs", 16)
    p.inc("gpsimd", lambda e: e.dma_start(out=selT[:], in_=W2["selT"]), "wsel", 16,
          evt="k2seln")
    for ee in range(16):
        p.inc("gpsimd", lambda e, ee=ee: e.dma_start(
            out=keys_bf.ap()[:, ee * 8 * 256:(ee + 1) * 8 * 256].rearrange("q (j c) -> q j c", j=8),
            in_=W2["keys"][ee].rearrange("(j q) h -> q j h", q=128)), "wkeys", 16, evt="k2keys")
    p.inc("gpsimd", lambda e: e.dma_start(
        out=vals_bf.ap().rearrange("q (e c f) -> q e c f", e=16, c=2),
        in_=W2["values"].rearrange("e (c q) f -> q e c f", q=128)), "wvals", 16, evt="k2vals")
    p.wait_evt("tensor", "k2seln")
    p.wait_evt("tensor", "k2idf")
    p.wait_evt("tensor", "k2id")

    def tile_xy(ti):
        b, i = divmod(ti, NT)
        return x1_in[b, i * 128:(i + 1) * 128, :], x2_out[b, i * 128:(i + 1) * 128, :]

    p.wait_sem_now("scalar", "wrs")
    for ti in range(NTT):
        src, _ = tile_xy(ti)
        if ti > 0:
            p.wait_evt("sync", f"k2g_{ti - 1}_tp1")
        p.inc("sync", lambda e, src=src: e.dma_start(out=junk[:], in_=src), f"k2g{ti % 2}", 16,
              evt=f"k2g_{ti}_ld")
        for g in range(2):
            p.wait_evt("tensor", f"k2g_{ti}_ld")
            if ti > 0 and g == 0:
                p.wait_evt("tensor", f"k2g_{ti - 1}_xT")
            for jj in range(4):
                j = g * 4 + jj
                p.inc("tensor", lambda t, j=j, jj=jj: t.matmul(
                    P_tp.ap()[:, jj * 128:(jj + 1) * 128], junk[:, j * 128:(j + 1) * 128], id_f[:],
                    start=(jj == 0), stop=True, is_transpose=True, skip_group_check=True),
                    "pe", 1, evt=f"k2g_{ti}_tp{g}")
            p.wait_evt("vector", f"k2g_{ti}_tp{g}")
            if ti > 0 and g == 0:
                p.wait_evt("vector", f"k2g_{ti - 1}_mm")
            p.inc("vector", lambda v, g=g: v.tensor_copy(
                xTf[:, g * 512:(g + 1) * 512], P_tp.ap()[:]),
                "dve", 1, evt=f"k2g_{ti}_xT")
            p.wait_evt("tensor", f"k2g_{ti}_xT")
        if ti > 0:
            p.wait_evt("tensor", f"k2g_{ti - 1}_sig")
        for j in range(8):
            p.inc("tensor", lambda t, j=j: t.matmul(
                P_sel.ap()[:, 0:16],
                xTf[:, j * 128:(j + 1) * 128],
                selT[:, j * 16:(j + 1) * 16],
                start=(j == 0), stop=(j == 7), skip_group_check=True),
                "pe", 1, evt=f"k2g_{ti}_mm")
        p.wait_evt("scalar", f"k2g_{ti}_mm")
        p.inc("scalar", lambda s: s.activation(probs[:], P_sel.ap()[:, 0:16], AF.Sigmoid),
              "act", 1, evt=f"k2g_{ti}_sig")
        p.wait_evt("vector", f"k2g_{ti}_sig")
        p.inc("vector", lambda v: v.max(mx[:], probs[:]), "dve", 1, evt=f"k2g_{ti}_mx")
        p.wait_evt("scalar", f"k2g_{ti}_mx")
        p.inc("scalar", lambda s: s.copy(mxb[:], mx[:]), "act", 1, evt=f"k2g_{ti}_mxb")
        p.wait_evt("vector", f"k2g_{ti}_mxb")
        p.inc("vector", lambda v: v.scalar_tensor_tensor(
            wsel[:], probs[:], mxb[:, 3:4], probs[:], ALU.is_ge, ALU.mult),
            "dve", 1, evt=f"k2g_{ti}_ws")
        p.wait_evt("scalar", f"k2g_{ti}_ws")
        p.inc("scalar", lambda s: s.copy(wselb[:], wsel[:]), "act", 1, evt=f"k2g_{ti}_wsb")
        p.wait_evt("vector", f"k2g_{ti}_wsb")
        p.inc("vector", lambda v: v.reduce_sum(gsum[:], wselb[:], axis=AX.X), "dve", 1,
              evt=f"k2g_{ti}_gs")
        p.wait_evt("scalar", f"k2g_{ti}_gs")
        p.inc("scalar", lambda s: s.activation(gsumb[:], gsum[:], AF.Relu), "act", 1,
              evt=f"k2g_{ti}_gsb")
        p.wait_evt("vector", f"k2g_{ti}_gsb")
        p.inc("vector", lambda v: v.tensor_scalar_max(grc[:], gsumb[:], 1e-9), "dve", 1,
              evt=f"k2g_{ti}_cl")
        p.wait_evt("scalar", f"k2g_{ti}_cl")
        p.inc("scalar", lambda s: s.copy(grcb[:], grc[:]), "act", 1, evt=f"k2g_{ti}_clb")
        p.wait_evt("vector", f"k2g_{ti}_clb")
        p.inc("vector", lambda v: v.reciprocal(grc2[:], grcb[:]), "dve", 1, evt=f"k2g_{ti}_rc")
        p.wait_evt("scalar", f"k2g_{ti}_rc")
        p.inc("scalar", lambda s: s.mul(grcb2[:], grc2[:], rs_col[:, 0:1]), "act", 1,
              evt=f"k2g_{ti}_rcb")
        p.wait_evt("vector", f"k2g_{ti}_rcb")
        p.inc("vector", lambda v, ti=ti: v.tensor_tensor(
            comb[:, ti * 16:(ti + 1) * 16], wsel[:],
            grcb2[:, 0:1].broadcast_to([128, 16]), ALU.mult),
            "dve", 1, evt=f"k2g_{ti}_done")
        if DEBUG_DUMPS and ti == 0:
            p.dump("k2_probs0", probs[:], f"k2g_{ti}_sig")
            p.dump("k2_mx0", mx[:], f"k2g_{ti}_mx")
            p.dump("k2_wsel0", wsel[:], f"k2g_{ti}_gs")
    p.wait_sem_now("tensor", "wkeys")
    p.wait_sem_now("tensor", "wvals")
    p.evt_here("dve", "k2gates_done")
    if DEBUG_DUMPS:
        p.dump("k2_comb", comb[:, 0:NTT * 16], "k2gates_done")

    for pr in range(NP):
        pb = pr % 2
        tiles = (2 * pr, 2 * pr + 1)
        for k, ti in enumerate(tiles):
            src, _ = tile_xy(ti)
            if pr >= 2:
                p.wait_evt("sync", f"k2ln_{2 * (pr - 2) + k}_store")
            p.inc("sync", lambda e, src=src, pb=pb, k=k: e.dma_start(
                out=xres1[pb][:, k * 1024:(k + 1) * 1024], in_=src), f"k2x{pb}{k}", 16,
                evt=f"k2x_{ti}")
        for k, ti in enumerate(tiles):
            p.wait_evt("vector", f"k2x_{ti}")
            if pr >= 1:
                p.wait_evt("vector", f"k2scm_{pr - 1}_15")
            p.inc("vector", lambda v, pb=pb, k=k: v.tensor_copy(xb[:], xres1[pb][:, k * 1024:(k + 1) * 1024]),
                  "dve", 1, evt=f"k2cast_{ti}")
            if pr == 0 and k == 0:
                p.wait_evt("tensor", "k2gates_done")
            p.wait_evt("tensor", f"k2cast_{ti}")
            for j in range(8):
                p.inc("tensor", lambda t, j=j: t.matmul(
                    tp_bf[:, (j % 4) * 128:((j % 4) + 1) * 128],
                    xb[:, j * 128:(j + 1) * 128], id_t[:],
                    start=(j % 4 == 0), stop=True, is_transpose=True, skip_group_check=True),
                    "pe", 1, evt=f"k2tp_{ti}_{j}")
                if j % 4 == 3:
                    p.wait_evt("vector", f"k2tp_{ti}_{j}")
                    if pr >= 1 and j == 3 and k == 0:
                        p.wait_evt("vector", f"k2vo_{pr - 1}_done")
                    p.inc("vector", lambda v, pb=pb, k=k, j4=(j // 4) * 4: v.tensor_copy(
                        x1T[pb].ap().rearrange("q (j c) -> q j c", j=8)[:, j4:j4 + 4, k * 128:(k + 1) * 128],
                        tp_bf[:, 0:512].rearrange("q (j c) -> q j c", j=4)),
                        "dve", 1, evt=f"k2xT_{ti}_ev")
                    p.wait_evt("tensor", f"k2xT_{ti}_ev")
        for e in range(16):
            eb2 = e % 2
            esc = f"k2sc_{pr}_{e - 2}" if e > 1 else (f"k2sc_{pr - 1}_{14 + e}" if pr > 0 else None)
            if esc is not None:
                p.wait_evt("tensor", esc)
            for h in range(2):
                for j in range(8):
                    p.inc("tensor", lambda t, e=e, h=h, j=j, pb=pb, eb2=eb2: t.matmul(
                        P_sc[eb2].ap()[:, h * 256:(h + 1) * 256],
                        keys_bf[:, (e * 8 + j) * 256 + h * 128:(e * 8 + j) * 256 + (h + 1) * 128],
                        x1T[pb][:, j * 256:(j + 1) * 256],
                        start=(j == 0 and h == 0), stop=(j == 7), skip_group_check=True),
                        "pe", 1, evt=(f"k2scm_{pr}_{e}" if h == 1 and j == 7 else None))
            p.wait_evt("scalar", f"k2scm_{pr}_{e}")
            if esc is not None:
                p.wait_evt("scalar", f"k2vo_{pr}_{e - 2}_1" if e > 1 else f"k2vo_{pr - 1}_{14 + e}_1")
            p.inc("scalar", lambda s, eb2=eb2: s.activation(scT[eb2][:], P_sc[eb2].ap()[:], AF.Relu),
                  "act", 1, evt=f"k2sc_{pr}_{e}")
            p.wait_evt("tensor", f"k2sc_{pr}_{e}")
            if DEBUG_DUMPS and pr == 0 and e in (0, 1):
                p.dump(f"k2_scT{e}", scT[:], f"k2sc_{pr}_{e}")
                if e == 0:
                    p.dump("k2_x1T0", x1T[0][:], f"k2sc_{pr}_{e}")
                    p.dump("k2_keys0", keys_bf[:, 0:2048], f"k2sc_{pr}_{e}")
            for k, ti in enumerate(tiles):
                if e > 0:
                    p.wait_evt("tensor", f"k2acc_{pr}_{e - 1}_{k}")
                elif pr > 0:
                    p.wait_evt("tensor", f"k2acc_{pr - 1}_15_{k}")
                for c in range(2):
                    for nh in range(2):
                        p.inc("tensor", lambda t, e=e, k=k, c=c, nh=nh, eb2=eb2: t.matmul(
                            P_vo[k].ap()[:, nh * 512:(nh + 1) * 512],
                            scT[eb2][:, c * 256 + k * 128: c * 256 + (k + 1) * 128],
                            vals_bf[:, (e * 2 + c) * 1024 + nh * 512:(e * 2 + c) * 1024 + (nh + 1) * 512],
                            start=(c == 0), stop=(c == 1), skip_group_check=True),
                            "pe", 1, evt=(f"k2vo_{pr}_{e}_{k}" if c == 1 and nh == 1 else None))
                p.wait_evt("vector", f"k2vo_{pr}_{e}_{k}")
                ccol = comb[:, tiles[k] * 16 + e: tiles[k] * 16 + e + 1]
                if e == 0:
                    p.inc("vector", lambda v, k=k, pb=pb, ccol=ccol: v.tensor_scalar(
                        acc[pb][:, k * 1024:(k + 1) * 1024], P_vo[k].ap()[:], ccol, None, ALU.mult),
                        "dve", 1, evt=f"k2acc_{pr}_{e}_{k}")
                else:
                    p.inc("vector", lambda v, k=k, pb=pb, ccol=ccol: v.scalar_tensor_tensor(
                        acc[pb][:, k * 1024:(k + 1) * 1024], P_vo[k].ap()[:], ccol,
                        acc[pb][:, k * 1024:(k + 1) * 1024], ALU.mult, ALU.add),
                        "dve", 1, evt=f"k2acc_{pr}_{e}_{k}")
        p.evt_here("dve", f"k2vo_{pr}_done")
        if DEBUG_DUMPS and pr == 0:
            p.dump("k2_acc0", acc[0][:, 0:1024], f"k2acc_{pr}_15_0")
        for k, ti in enumerate(tiles):
            _, dst = tile_xy(ti)
            p.wait_evt("vector", f"k2acc_{pr}_15_{k}")
            p.inc("vector", lambda v, pb=pb, k=k: v.tensor_add(
                s_ln[:], acc[pb][:, k * 1024:(k + 1) * 1024], xres1[pb][:, k * 1024:(k + 1) * 1024]),
                "dve", 1, evt=f"k2s_{ti}")
            emit_ln_store(p, ln, s_ln[:], junkb[ti % 2][:], [dst], f"k2s_{ti}", f"k2ln_{ti}",
                          war_evt=k2_jstore[ti % 2])
            k2_jstore[ti % 2] = f"k2ln_{ti}_store"
    p.wait_sem_now("sync", "st")


def build_k2():
    nc = bass.Bass("TRN2", target_bir_lowering=False)
    x1_in = nc.dram_tensor("x1", [B_PER_CORE, L, D], F32, kind="ExternalInput").ap()
    x2_out = nc.dram_tensor("x2", [B_PER_CORE, L, D], F32, kind="ExternalOutput").ap()
    W2 = {
        "selT": nc.dram_tensor("selT", [128, 128], F32, kind="ExternalInput").ap(),
        "keys": nc.dram_tensor("keys", [16, 1024, 256], F32, kind="ExternalInput").ap(),
        "values": nc.dram_tensor("values", [16, 256, 1024], F32, kind="ExternalInput").ap(),
        "rs_col": nc.dram_tensor("rs_col", [128, 1], F32, kind="ExternalInput").ap(),
        "ident_f": nc.dram_tensor("ident_f", [128, 128], F32, kind="ExternalInput").ap(),
    }
    ident = nc.dram_tensor("ident", [128, 128], BF16, kind="ExternalInput").ap()
    p = Prog(nc)
    emit_k2(nc, p, x1_in, x2_out, W2, ident)
    p.run()
    return nc


def k2_inputs(inputs, core, x1):
    d = {
        "x1": (np.ascontiguousarray(x1[core * B_PER_CORE:(core + 1) * B_PER_CORE])
               if x1 is not None else None),
        "selT": np.ascontiguousarray(np.asarray(inputs["L0_ffn_expert_sel"], np.float32)
                                     .reshape(16, 8, 128).transpose(2, 1, 0).reshape(128, 128)),
        "keys": np.ascontiguousarray(inputs["L0_ffn_keys"]),
        "values": np.ascontiguousarray(inputs["L0_ffn_values"]),
        "rs_col": np.ascontiguousarray(np.broadcast_to(
            np.asarray(inputs["L0_ffn_route_scale"], np.float32).reshape(1, 1), (128, 1))),
        "ident": _ident_bf16(),
        "ident_f": np.eye(128, dtype=np.float32),
    }
    return d




TC = 256
NFT = 16
G3 = 8
NG3 = NFT // G3


def emit_k3(nc, p, x2_in, x3_out, W3):
    a = nc.alloc_sbuf_tensor
    ap_ = nc.alloc_psum_tensor
    NCH = L // TC
    TPC = TC // 128

    id_t = a("k3_id", [128, 128], BF16)
    wip = a("k3_wip", [128, 8 * 4128], BF16)
    wout = a("k3_wout", [128, 16 * 1024], BF16)
    Wbd = {n: a(f"k3_Wbd{n}", [128, 128], BF16) for n in ("re", "im")}
    Wstage = a("k3_Wstage", [64, 64], F32)
    ns_col = {n: a(f"k3_ns{n}", [64, 1], F32) for n in ("re", "im")}
    theta_cols = a("k3_theta", [128, 16], F32)
    patft = a("k3_patft", [32, 16 * 128], F32)
    bp16 = a("k3_bp16", [128, 16 * 32], BF16)
    distb = [a(f"k3_distb{i}", [32, TC], F32) for i in range(2)]
    sc_col = a("k3_sc", [32, 1], F32)
    convk_s = a("k3_cks", [32, 4], F32)
    convb_s = a("k3_cbs", [32, 1], F32)
    convk_cols = a("k3_ckc", [128, 4 * 32], F32)
    convb_cols = a("k3_cbc", [128, 32], F32)
    slopes = a("k3_slopes", [1, 32], F32)
    srow = [a(f"k3_srow{i}", [1, 32], F32) for i in range(3)]
    ones11 = a("k3_ones11", [1, 1], F32)
    sp_col = a("k3_spcol", [32, 1], F32)
    onesl = a("k3_onesl", [1, 128], F32)
    wcols = a("k3_wcols", [128, 4], F32)
    sqs = a("k3_sqs", [128, 1], F32)
    sqb2 = a("k3_sqb2", [128, 1], F32)
    sqa = a("k3_sqa", [128, 1], F32)
    sqt = a("k3_sqt", [128, 1], F32)
    sqt2 = a("k3_sqt2", [128, 1], F32)
    hp_col = a("k3_hp", [128, 1], F32)
    zeros1 = a("k3_zeros1", [128, 1], F32)
    zseg = a("k3_zseg", [128, G3, 3 + TC], F32)
    zcb = a("k3_zcb", [128, G3, TC], F32)
    Bt = a("k3_Bt", [128, G3, TC], F32)
    Ct = a("k3_Ct", [128, G3, TC], F32)
    St = a("k3_St", [128, G3, TC], F32)
    numbuf = [a(f"k3_numb{i}", [128, NFT * TC], BF16) for i in range(2)]
    numfb = [a(f"k3_numfb{i}", [128, G3 * TC], BF16) for i in range(2)]
    sqbig = numfb[0].ap()[:, 0:G3 * TC]
    ygbuf = a("k3_ygbuf", [128, NFT * TC], BF16)
    x2T = a("k3_x2T", [128, 8 * TC], BF16)
    xbst = a("k3_xbst", [128, 1024], BF16)
    hist = a("k3_hist", [128, 32 * 3], F32)
    carry = [a(f"k3_carry{i}", [128, NFT], F32) for i in range(2)]
    red = a("k3_red", [128, G3], F32)
    dcarry = a("k3_dcarry", [32, 1], F32)
    zs = a("k3_zs", [32, 3 + TC], F32)
    pw = a("k3_pw", [32, TC], F32)
    tw = a("k3_tw", [32, TC], F32)
    den = a("k3_den", [32, TC], F32)
    invd = a("k3_invd", [32, TC], F32)
    rsb = a("k3_rsb", [32, TC], F32)
    s_ln = a("k3_sln", [128, 1024], F32)
    x2res = [a(f"k3_x2res{i}", [128, 1024], F32) for i in range(2)]
    ln = LNPack(nc, "k3_")

    gateb = Ct.ap().rearrange("q g c -> q (g c)").bitcast(BF16)[:, 0:G3 * TC]
    Bf = Bt.ap().rearrange("q g c -> q (g c)")
    Cf = Ct.ap().rearrange("q g c -> q (g c)")
    Sf = St.ap().rearrange("q g c -> q (g c)")
    Zf = zcb.ap().rearrange("q g c -> q (g c)")
    hist3 = hist.ap().rearrange("q (ci t) -> q ci t", ci=32)
    ckv4 = convk_cols.ap().rearrange("q (j c o) -> q j c o", j=4, c=32)
    cbv = convb_cols.ap().rearrange("q (c o) -> q c o", c=32)
    thv = theta_cols.ap().rearrange("q (f o) -> q f o", f=16)

    P_z = [ap_(f"k3P_z{i}", [128, 512], F32) for i in range(2)]
    P_bc = [ap_(f"k3P_bc{i}", [128, 512], F32) for i in range(2)]
    P_sm = ap_("k3P_sm", [128, 512], F32)
    P_out = ap_("k3P_out", [128, 1024], F32)
    P_tp = ap_("k3P_tp", [128, 512], F32)
    tp_bf = P_tp.ap().bitcast(BF16)

    TWO_PI = float(2 * np.pi)
    INV2PI = float(1.0 / (2 * np.pi))
    HALF_PI = float(np.pi / 2)
    nb = x2_in.shape[0]

    p.inc("sync", lambda e: e.dma_start(out=id_t[:], in_=W3["ident"]), "init", 16, evt="k3id")
    p.inc("gpsimd", lambda e: e.dma_start(
        out=wip.ap().rearrange("q (dj c) -> q dj c", dj=8),
        in_=W3["in_proj"].rearrange("(dj q) c -> q dj c", q=128)), "k3wip", 16)
    p.inc("gpsimd", lambda e: e.dma_start(
        out=wout.ap().rearrange("q (ft e) -> q ft e", ft=16),
        in_=W3["out_proj"].rearrange("(ft q) e -> q ft e", q=128)), "k3wout", 16)
    smalls = {"theta_cols": theta_cols, "patft": patft, "bp16": bp16,
              "sc_col": sc_col, "convk_s": convk_s, "convb_s": convb_s,
              "convk_cols": convk_cols, "convb_cols": convb_cols}
    for key, t in smalls.items():
        p.inc("gpsimd", lambda e, t=t, key=key: e.dma_start(out=t[:], in_=W3[key]), "k3small", 16)
    p.inc("gpsimd", lambda e: e.dma_start(out=slopes[0:1, 0:28], in_=W3["decay"]), "k3small", 16)
    p.inc("gpsimd", lambda e: e.dma_start(out=slopes[0:1, 28:32], in_=W3["anchor"]), "k3small", 16)
    p.inc("gpsimd", lambda e: e.dma_start(out=srow[2][0:1, 0:3], in_=W3["deriv"]), "k3small", 16)
    p.inc("gpsimd", lambda e: e.dma_start(out=ns_col["re"][:], in_=W3["ns_re"]), "k3small", 16)
    p.inc("gpsimd", lambda e: e.dma_start(out=ns_col["im"][:], in_=W3["ns_im"]), "k3small", 16)
    p.on("vector", lambda v: v.memset(ones11[:], 1.0))
    p.on("vector", lambda v: v.memset(onesl[:], 1.0))
    p.on("vector", lambda v: v.memset(zeros1[:], 0.0))
    p.on("vector", lambda v: v.memset(hp_col[:], HALF_PI))
    p.wait_sem_now("vector", "k3small")
    p.on("vector", lambda v: v.tensor_scalar_mul(srow[0][:], slopes[:], -1.0))
    p.inc("vector", lambda v: v.tensor_max(srow[0][:], srow[0][:], slopes[:]),
          "dve", 1, evt="k3sp_a")
    p.wait_evt("scalar", "k3sp_a")
    p.inc("scalar", lambda s: s.activation(srow[1][:], srow[0][:], AF.Exp, scale=-1.0), "act", 1,
          evt="k3sp_b")
    p.self_wait("scalar", "act")
    p.inc("scalar", lambda s: s.activation(srow[1][:], srow[1][:], AF.Ln, bias=ones11[0:1, 0:1]),
          "act", 1, evt="k3sp_c")
    p.wait_evt("vector", "k3sp_c")
    p.inc("vector", lambda v: v.tensor_single_scalar(srow[0][:], slopes[:], 0.0, ALU.max), "dve", 1)
    p.self_wait("vector", "dve")
    p.inc("vector", lambda v: v.tensor_add(srow[0][:], srow[0][:], srow[1][:]), "dve", 1,
          evt="k3sp_d")
    p.wait_evt("tensor", "k3sp_d")
    p.wait_evt("tensor", "k3id")
    p.inc("tensor", lambda t: t.matmul(P_sm.ap()[0:32, 0:1], srow[0][0:1, 0:32],
                                       ones11[0:1, 0:1], start=True, stop=True,
                                       is_transpose=True, skip_group_check=True),
          "pe", 1, evt="k3sp_t")
    p.wait_evt("vector", "k3sp_t")
    p.inc("vector", lambda v: v.tensor_copy(sp_col[:], P_sm.ap()[0:32, 0:1]), "dve", 1,
          evt="k3sp_e")
    p.wait_evt("scalar", "k3sp_e")
    p.wait_sem_now("scalar", "k3small")
    p.evt_here("act", "k3tw")
    p.inc("vector", lambda v: v.reduce_max(srow[0][0:1, 0:1], srow[2][0:1, 0:3], axis=AX.X),
          "dve", 1, evt="k3dv_a")
    p.wait_evt("scalar", "k3dv_a")
    p.inc("scalar", lambda s: s.copy(srow[0][0:1, 1:2], srow[0][0:1, 0:1]), "act", 1, evt="k3dv_b")
    p.wait_evt("vector", "k3dv_b")
    p.inc("vector", lambda v: v.tensor_tensor(srow[1][0:1, 0:3], srow[2][0:1, 0:3],
                                              srow[0][0:1, 1:2].broadcast_to([1, 3]), ALU.subtract),
          "dve", 1, evt="k3dv_c")
    p.wait_evt("scalar", "k3dv_c")
    p.inc("scalar", lambda s: s.activation(srow[1][0:1, 0:3], srow[1][0:1, 0:3], AF.Exp), "act", 1,
          evt="k3dv_d")
    p.wait_evt("vector", "k3dv_d")
    p.inc("vector", lambda v: v.reduce_sum(srow[0][0:1, 2:3], srow[1][0:1, 0:3], axis=AX.X),
          "dve", 1, evt="k3dv_e")
    p.wait_evt("scalar", "k3dv_e")
    p.inc("scalar", lambda s: s.copy(srow[2][0:1, 1:2], srow[0][0:1, 2:3]), "act", 1, evt="k3dv_f")
    p.wait_evt("vector", "k3dv_f")
    p.inc("vector", lambda v: v.reciprocal(srow[2][0:1, 0:1], srow[2][0:1, 1:2]), "dve", 1,
          evt="k3dv_g")
    p.wait_evt("scalar", "k3dv_g")
    p.inc("scalar", lambda s: s.copy(srow[0][0:1, 0:1], srow[2][0:1, 0:1]), "act", 1, evt="k3dv_h")
    p.wait_evt("vector", "k3dv_h")
    p.inc("vector", lambda v: v.tensor_tensor(srow[1][0:1, 0:3], srow[1][0:1, 0:3],
                                              srow[0][0:1, 0:1].broadcast_to([1, 3]), ALU.mult),
          "dve", 1, evt="k3dv_i")
    p.wait_evt("tensor", "k3dv_i")
    p.wait_evt("tensor", "k3sp_e")
    p.inc("tensor", lambda t: t.matmul(P_sm.ap()[:, 0:3], onesl[:], srow[1][0:1, 0:3],
                                       start=True, stop=True, skip_group_check=True),
          "pe", 1, evt="k3dv_j")
    p.wait_evt("vector", "k3dv_j")
    p.inc("vector", lambda v: v.tensor_copy(wcols[:, 0:3], P_sm.ap()[:, 0:3]), "dve", 1,
          evt="k3wcols")
    p.wait_evt("scalar", "k3wcols")
    p.inc("scalar", lambda s: s.activation(sqs[:], wcols[:, 2:3], AF.Sqrt), "act", 1,
          evt="k3sq_a")
    p.wait_evt("vector", "k3sq_a")
    p.inc("vector", lambda v: v.tensor_scalar_mul(sqt[:], sqs[:], 2.0), "dve", 1)
    p.self_wait("vector", "dve")
    p.inc("vector", lambda v: v.reciprocal(sqt2[:], sqt[:]), "dve", 1)
    p.self_wait("vector", "dve")
    p.inc("vector", lambda v: v.tensor_scalar_mul(sqt2[:], sqt2[:], -1.0), "dve", 1)
    p.self_wait("vector", "dve")
    p.inc("vector", lambda v: v.tensor_scalar(sqb2[:], wcols[:, 1:2], sqt2[:, 0:1], None,
                                              ALU.mult), "dve", 1, evt="k3sq_b")
    p.wait_evt("scalar", "k3sq_b")
    p.inc("scalar", lambda s: s.activation(sqt[:], sqb2[:], AF.Square), "act", 1, evt="k3sq_c")
    p.wait_evt("vector", "k3sq_c")
    p.inc("vector", lambda v: v.tensor_add(sqa[:], wcols[:, 0:1], sqt[:]), "dve", 1,
          evt="k3sq_d")
    for nm in ("re", "im"):
        if nm == "im":
            p.wait_evt("gpsimd", "k3Wre_done")
        p.inc("gpsimd", lambda e, nm=nm: e.dma_start(out=Wstage[:], in_=W3[f"W_{nm}"]),
              f"k3w{nm}", 16, evt=f"k3W{nm}_ld")
        p.wait_evt("vector", f"k3W{nm}_ld")
        p.inc("vector", lambda v, nm=nm: v.tensor_scalar(
            Wstage[:], Wstage[:], ns_col[nm][0:64, 0:1], None, ALU.mult), "dve", 1)
        p.self_wait("vector", "dve")
        p.inc("vector", lambda v: v.tensor_scalar_mul(Wstage[:], Wstage[:], -1.0), "dve", 1)
        p.self_wait("vector", "dve")
        p.inc("vector", lambda v, nm=nm: v.tensor_copy(Wbd[nm][0:64, 0:64], Wstage[:]), "dve", 1)
        p.inc("vector", lambda v, nm=nm: v.memset(Wbd[nm][0:64, 64:128], 0.0), "dve", 1)
        p.inc("vector", lambda v, nm=nm: v.memset(Wbd[nm][64:128, 0:64], 0.0), "dve", 1)
        p.inc("vector", lambda v, nm=nm: v.tensor_copy(Wbd[nm][64:128, 64:128], Wstage[:]),
              "dve", 1, evt=f"k3W{nm}_done")
    p.wait_sem_now("tensor", "k3wip")
    p.wait_sem_now("tensor", "k3wout")
    p.wait_sem_now("tensor", "k3small")
    p.wait_evt("tensor", "k3Wim_done")
    p.wait_evt("tensor", "k3sq_d")
    p.wait_evt("tensor", "k3tw")
    p.evt_here("pe", "k3prep")
    p.evt_here("act", "k3prep_act")
    p.evt_here("dve", "k3prep_dve")

    st = {
        "zseg_rd": None,
        "xbst_rd": None,
        "x2T_rd": None,
        "x2T_wr": None,
        "pz_free": [None, None],
        "pbc_free": [None, None],
        "psm_free": None,
        "tw_rd": [None, None],
        "numbuf_sq": [[None, None], [None, None]],
        "numfb_rd": None,
        "yg_last": None,
        "out_rd": None,
        "pout_free": None,
        "ln_store2": [None, None],
        "pw_evt": None,
        "tot_evt": None,
        "silu": [None, None],
        "scan_done": None,
        "zsm_ok": None,
    }

    def build_x2T(b, c, tag):
        for tt in range(TPC):
            if st["xbst_rd"] is not None:
                p.wait_evt("gpsimd", st["xbst_rd"])
            p.inc("gpsimd", lambda e, b=b, c=c, tt=tt: e.dma_start(
                out=xbst[:], in_=x2_in[b, (c * TPC + tt) * 128:(c * TPC + tt + 1) * 128, :]),
                "k3xb", 16, evt=f"{tag}_ld{tt}")
            p.wait_evt("tensor", f"{tag}_ld{tt}")
            for g in range(2):
                if st["pbc_free"][g] is not None:
                    pass
                for jj in range(4):
                    dj = g * 4 + jj
                    p.inc("tensor", lambda t, dj=dj, jj=jj: t.matmul(
                        tp_bf[:, jj * 128:(jj + 1) * 128], xbst[:, dj * 128:(dj + 1) * 128],
                        id_t[:], start=(jj == 0), stop=True, is_transpose=True,
                        skip_group_check=True), "pe", 1, evt=f"{tag}_tp{tt}_{g}")
                p.wait_evt("vector", f"{tag}_tp{tt}_{g}")
                if tt == 0 and g == 0 and st["x2T_rd"] is not None:
                    p.wait_evt("vector", st["x2T_rd"])
                p.inc("vector", lambda v, g=g, tt=tt: v.tensor_copy(
                    x2T.ap().rearrange("q (dj t) -> q dj t", dj=8)[:, g * 4:(g + 1) * 4,
                                                                  tt * 128:(tt + 1) * 128],
                    tp_bf[:, 0:512].rearrange("q (dj t) -> q dj t", dj=4)),
                    "dve", 1, evt=f"{tag}_ev{tt}_{g}")
                p.wait_evt("tensor", f"{tag}_ev{tt}_{g}")
            st["xbst_rd"] = f"{tag}_tp{tt}_1"
        st["x2T_wr"] = f"{tag}_ev{TPC - 1}_1"

    def zmm(ci, pzb, tag):
        if st["pz_free"][pzb] is not None:
            p.wait_evt("tensor", st["pz_free"][pzb])
        for dj in range(8):
            p.inc("tensor", lambda t, ci=ci, dj=dj, pzb=pzb: t.matmul(
                P_z[pzb].ap()[:, 0:TC],
                wip[:, dj * 4128 + ci * 128: dj * 4128 + (ci + 1) * 128],
                x2T[:, dj * TC:(dj + 1) * TC],
                start=(dj == 0), stop=(dj == 7), skip_group_check=True),
                "pe", 1, evt=(tag if dj == 7 else None))

    for b in range(nb):
        if b > 0:
            p.wait_evt("vector", f"k3_b{b - 1}_done")
        p.inc("vector", lambda v: v.memset(carry[0][:], 0.0), "dve", 1)
        p.inc("vector", lambda v: v.memset(carry[1][:], 0.0), "dve", 1)
        p.inc("vector", lambda v: v.memset(dcarry[:], 0.0), "dve", 1)
        p.inc("vector", lambda v: v.memset(hist[:], 0.0), "dve", 1)
        p.inc("vector", lambda v: v.memset(zs[0:32, 0:3], 0.0), "dve", 1, evt=f"k3rst_{b}")
        if b == 0:
            p.wait_evt("tensor", "k3prep")
            p.wait_evt("vector", "k3prep_act")
            p.wait_evt("scalar", "k3prep_dve")
        build_x2T(b, 0, f"k3bx_{b}_0")
        for c in range(NCH):
            bc = f"{b}_{c}"
            par = c % 2
            if st["tw_rd"][par] is not None:
                p.wait_evt("sync", st["tw_rd"][par])
            p.inc("sync", lambda e, c=c, par=par: e.dma_start(
                out=distb[par][:], in_=W3["dist32n"][:, c * TC:(c + 1) * TC]),
                "k3dist", 16, evt=f"k3dl_{bc}")
            p.wait_evt("tensor", st["x2T_wr"])
            if st["psm_free"] is not None:
                p.wait_evt("tensor", st["psm_free"])
            for dj in range(8):
                p.inc("tensor", lambda t, dj=dj: t.matmul(
                    P_sm.ap()[0:32, 0:TC],
                    wip[:, dj * 4128 + 4096: dj * 4128 + 4128],
                    x2T[:, dj * TC:(dj + 1) * TC],
                    start=(dj == 0), stop=(dj == 7), skip_group_check=True),
                    "pe", 1, evt=(f"k3zsm_{bc}" if dj == 7 else None))
            p.wait_evt("scalar", f"k3zsm_{bc}")
            p.wait_evt("scalar", f"k3dl_{bc}")
            p.inc("scalar", lambda s, par=par: s.activation(tw[0:32, :], distb[par][:],
                                                            AF.Exp, scale=sp_col[0:32, 0:1]),
                  "act", 1, evt=f"k3twc_{bc}")
            st["tw_rd"][par] = f"k3twc_{bc}"
            p.wait_evt("vector", f"k3zsm_{bc}")
            if b == 0 and c == 0:
                p.wait_evt("vector", "k3rst_0")
            p.inc("vector", lambda v: v.tensor_copy(zs[0:32, 3:3 + TC], P_sm.ap()[0:32, 0:TC]),
                  "dve", 1, evt=f"k3zsev_{bc}")
            st["zsm_ok"] = f"k3zsev_{bc}"
            for j in range(4):
                if j == 0:
                    p.inc("vector", lambda v: v.tensor_scalar(
                        den[0:32, :], zs[0:32, 0:TC], convk_s[0:32, 0:1], None, ALU.mult), "dve", 1)
                else:
                    p.inc("vector", lambda v, j=j: v.scalar_tensor_tensor(
                        den[0:32, :], zs[0:32, j:j + TC], convk_s[0:32, j:j + 1], den[0:32, :],
                        ALU.mult, ALU.add), "dve", 1)
            p.inc("vector", lambda v: v.tensor_scalar(den[0:32, :], den[0:32, :],
                                                      convb_s[0:32, 0:1], None, ALU.add), "dve", 1)
            p.inc("vector", lambda v: v.tensor_copy(zs[0:32, 0:3], zs[0:32, TC:TC + 3]), "dve", 1)
            p.inc("vector", lambda v: v.tensor_scalar(den[0:32, :], den[0:32, :], sc_col[0:32, 0:1],
                                                      None, ALU.mult), "dve", 1)
            p.inc("vector", lambda v: v.tensor_single_scalar(den[0:32, :], den[0:32, :], 20.0,
                                                             ALU.min), "dve", 1)
            p.inc("vector", lambda v: v.tensor_single_scalar(den[0:32, :], den[0:32, :], -20.0,
                                                             ALU.max), "dve", 1, evt=f"k3p_{bc}a")
            p.wait_evt("scalar", f"k3p_{bc}a")
            p.inc("scalar", lambda s: s.activation(pw[0:32, :], den[0:32, :], AF.Exp), "act", 1,
                  evt=f"k3p_{bc}b")
            p.wait_evt("vector", f"k3p_{bc}b")
            p.wait_evt("vector", f"k3twc_{bc}")
            p.inc("vector", lambda v: v.tensor_mul(pw[0:32, :], pw[0:32, :], tw[0:32, :]),
                  "dve", 1, evt=f"k3pw_{bc}")
            st["pw_evt"] = f"k3pw_{bc}"
            p.inc("vector", lambda v: v.tensor_tensor_scan(
                den[0:32, :], pw[0:32, :], zeros1[0:32, 0:1].broadcast_to([32, TC]),
                dcarry[0:32, 0:1], ALU.add, ALU.add), "dve", 1)
            p.inc("vector", lambda v: v.tensor_copy(dcarry[:], den[0:32, TC - 1:TC]), "dve", 1)
            p.inc("vector", lambda v: v.tensor_single_scalar(den[0:32, :], den[0:32, :], 1e-4,
                                                             ALU.max), "dve", 1, evt=f"k3dn_{bc}a")
            p.wait_evt("scalar", f"k3dn_{bc}a")
            p.inc("scalar", lambda s: s.copy(tw[0:32, :], den[0:32, :]), "act", 1,
                  evt=f"k3dn_{bc}b")
            p.wait_evt("vector", f"k3dn_{bc}b")
            p.inc("vector", lambda v: v.reciprocal(invd[0:32, :], tw[0:32, :]), "dve", 1,
                  evt=f"k3invd_{bc}")

            ssq_pend = []
            for g in range(NG3):
                gk = f"{bc}_{g}"
                for fl in range(G3):
                    ft = g * G3 + fl
                    pzb = fl % 2
                    zmm(ft, pzb, f"k3z_{gk}_{fl}")
                    p.wait_evt("scalar", f"k3z_{gk}_{fl}")
                    if fl == 0 and st["zseg_rd"] is not None:
                        eng, evt = st["zseg_rd"]
                        p.wait_evt("scalar", evt)
                    p.inc("scalar", lambda s, fl=fl, pzb=pzb: s.copy(
                        zseg.ap()[:, fl, 3:3 + TC], P_z[pzb].ap()[:, 0:TC]), "act", 1,
                        evt=f"k3zev_{gk}_{fl}")
                    st["pz_free"][pzb] = f"k3zev_{gk}_{fl}"
                while ssq_pend:
                    ssq_pend.pop(0)()
                p.wait_evt("vector", f"k3zev_{gk}_{G3 - 1}")
                if b == 0 and c == 0 and g == 0:
                    p.wait_evt("vector", f"k3rst_{b}")
                p.inc("vector", lambda v, g=g: v.tensor_copy(
                    zseg.ap()[:, :, 0:3], hist3[:, g * G3:(g + 1) * G3, :]), "dve", 1)
                for j in range(4):
                    ckb = ckv4[:, j, g * G3:(g + 1) * G3, :].broadcast_to([128, G3, TC])
                    if j == 0:
                        p.inc("vector", lambda v, ckb=ckb: v.tensor_tensor(
                            zcb[:], zseg.ap()[:, :, 0:TC], ckb, ALU.mult), "dve", 1)
                    else:
                        p.inc("vector", lambda v, j=j, ckb=ckb: v.tensor_tensor(
                            St[:], zseg.ap()[:, :, j:j + TC], ckb, ALU.mult), "dve", 1)
                        p.inc("vector", lambda v: v.tensor_add(zcb[:], zcb[:], St[:]), "dve", 1)
                cbb = cbv[:, 16 * 0 + g * G3:16 * 0 + (g + 1) * G3, :].broadcast_to([128, G3, TC])
                p.inc("vector", lambda v, cbb=cbb: v.tensor_tensor(zcb[:], zcb[:], cbb, ALU.add),
                      "dve", 1)
                p.inc("vector", lambda v, g=g: v.tensor_copy(
                    hist3[:, g * G3:(g + 1) * G3, :], zseg.ap()[:, :, TC:TC + 3]), "dve", 1,
                    evt=f"k3cv_{gk}")
                st["zseg_rd"] = ("vector", f"k3cv_{gk}")
                thb = thv[:, g * G3:(g + 1) * G3, :].broadcast_to([128, G3, TC])
                p.inc("vector", lambda v, thb=thb: v.tensor_tensor(Ct[:], zcb[:], thb, ALU.mult),
                      "dve", 1, evt=f"k3ph_{gk}")
                p.inc("vector", lambda v: v.tensor_scalar(Bf, Cf, INV2PI, RND_MAGIC,
                                                          ALU.mult, ALU.add), "dve", 1)
                p.inc("vector", lambda v: v.tensor_scalar(Bf, Bf, RND_MAGIC, TWO_PI,
                                                          ALU.subtract, ALU.mult), "dve", 1)
                p.inc("vector", lambda v: v.tensor_sub(Sf, Cf, Bf), "dve", 1, evt=f"k3rs_{gk}")
                p.inc("vector", lambda v: v.tensor_scalar(Bf, Cf, INV2PI, 0.25,
                                                          ALU.mult, ALU.add), "dve", 1)
                p.inc("vector", lambda v: v.tensor_scalar(Bf, Bf, RND_MAGIC, RND_MAGIC,
                                                          ALU.add, ALU.subtract), "dve", 1)
                p.inc("vector", lambda v: v.tensor_scalar_mul(Bf, Bf, TWO_PI), "dve", 1)
                p.inc("vector", lambda v: v.tensor_sub(Cf, Cf, Bf), "dve", 1, evt=f"k3rc_{gk}")
                p.wait_evt("scalar", f"k3rs_{gk}")
                p.inc("scalar", lambda s: s.activation(Sf, Sf, AF.Sin), "act", 1,
                      evt=f"k3sin_{gk}")
                p.wait_evt("scalar", f"k3rc_{gk}")
                p.inc("scalar", lambda s: s.activation(Cf, Cf, AF.Sin, bias=hp_col[:, 0:1]),
                      "act", 1, evt=f"k3cos_{gk}")
                p.inc("scalar", lambda s: s.activation(Bf, Zf, AF.Square, scale=sqs[:, 0:1],
                                                       bias=sqb2[:, 0:1]), "act", 1,
                      evt=f"k3poly_{gk}")
                p.wait_evt("tensor", st["pw_evt"])
                p.wait_evt("tensor", f"k3poly_{gk}")
                for fl in range(G3):
                    ft = g * G3 + fl
                    bb = fl % 2
                    if st["pbc_free"][bb] is not None:
                        p.wait_evt("tensor", st["pbc_free"][bb])
                    p.inc("tensor", lambda t, ft=ft, bb=bb: t.matmul(
                        P_bc[bb].ap()[:, 0:TC], patft[:, ft * 128:(ft + 1) * 128], pw[0:32, 0:TC],
                        start=True, stop=True, skip_group_check=True), "pe", 1,
                        evt=f"k3pbm_{gk}_{fl}")
                    p.wait_evt("scalar", f"k3pbm_{gk}_{fl}")
                    if fl == 0:
                        p.wait_evt("scalar", f"k3ph_{gk}")
                    p.inc("scalar", lambda s, fl=fl, bb=bb: s.copy(
                        zcb.ap()[:, fl, :], P_bc[bb].ap()[:, 0:TC]), "act", 1,
                        evt=f"k3pbe_{gk}_{fl}")
                    st["pbc_free"][bb] = f"k3pbe_{gk}_{fl}"
                p.wait_evt("vector", f"k3poly_{gk}")
                p.wait_evt("vector", f"k3pbe_{gk}_{G3 - 1}")
                p.inc("vector", lambda v: v.scalar_tensor_tensor(
                    Bf, Bf, sqa[:, 0:1], Zf, ALU.subtract, ALU.mult), "dve", 1)
                p.wait_evt("vector", f"k3cos_{gk}")
                p.inc("vector", lambda v: v.tensor_mul(Cf, Cf, Bf), "dve", 1)
                p.wait_evt("vector", f"k3sin_{gk}")
                p.inc("vector", lambda v: v.tensor_mul(Sf, Sf, Bf), "dve", 1, evt=f"k3mod_{gk}")
                if st["numbuf_sq"][0][g] is not None:
                    p.wait_evt("vector", st["numbuf_sq"][0][g])
                    p.wait_evt("vector", st["numbuf_sq"][1][g])
                for ri, src in ((0, Ct), (1, St)):
                    for fl in range(G3):
                        ft = g * G3 + fl
                        p.inc("vector", lambda v, ri=ri, src=src, fl=fl, ft=ft: v.tensor_tensor_scan(
                            numbuf[ri][:, ft * TC:(ft + 1) * TC], src.ap()[:, fl, :],
                            zeros1[:, 0:1].broadcast_to([128, TC]),
                            carry[ri][:, ft:ft + 1], ALU.add, ALU.add), "dve", 1)
                for ri, src in ((0, Ct), (1, St)):
                    p.inc("vector", lambda v, src=src: v.reduce_sum(red[:], src[:], axis=AX.X),
                          "dve", 1)
                    p.inc("vector", lambda v, ri=ri, g=g: v.tensor_add(
                        carry[ri][:, g * G3:(g + 1) * G3], carry[ri][:, g * G3:(g + 1) * G3],
                        red[:]), "dve", 1, evt=(f"k3scan_{gk}" if ri == 1 else None))
                st["scan_done"] = f"k3scan_{gk}"
                def emit_sq(g=g, gk=gk):
                    for ri in range(2):
                        p.wait_evt("gpsimd", f"k3scan_{gk}")
                        if not (g == 0 and ri == 0):
                            p.wait_evt("gpsimd", f"k3ssq_{gk}_{ri - 1}" if ri == 1 else
                                       f"k3ssq_{bc}_{g - 1}_1")
                        p.onp(lambda e, ri=ri, g=g: e.tensor_mul(
                            sqbig, numbuf[ri][:, g * G3 * TC:(g + 1) * G3 * TC],
                            numbuf[ri][:, g * G3 * TC:(g + 1) * G3 * TC]),
                            evt=f"k3sq_{gk}_{ri}")
                        st["numbuf_sq"][ri][g] = f"k3sq_{gk}_{ri}"
                        p.wait_evt("tensor", f"k3sq_{gk}_{ri}")
                        if g == 0 and ri == 0:
                            p.wait_evt("tensor", st["zsm_ok"])
                        for fl in range(G3):
                            ft = g * G3 + fl
                            p.inc("tensor", lambda t, ft=ft, fl=fl, g=g, ri=ri: t.matmul(
                                P_sm.ap()[0:32, 0:TC], bp16[:, ft * 32:(ft + 1) * 32],
                                sqbig[:, fl * TC:(fl + 1) * TC],
                                start=(g == 0 and ri == 0 and fl == 0),
                                stop=(g == NG3 - 1 and ri == 1 and fl == G3 - 1),
                                skip_group_check=True), "pe", 1,
                                evt=(f"k3ssq_{gk}_{ri}" if fl == G3 - 1 else None))
                ssq_pend.append(emit_sq)

            while ssq_pend:
                ssq_pend.pop(0)()

            q1 = Bt.ap().rearrange("q g c -> q (g c)").bitcast(I32)[0:32, 0:TC]
            q2 = Cf[0:32, 0:TC]
            q3 = Sf[0:32, 0:TC]
            mgv = Zf.bitcast(I32)[0:32, 0:TC]
            m_ = tw[0:32, 0:TC]
            p.wait_evt("vector", f"k3ssq_{bc}_{NG3 - 1}_1")
            p.inc("vector", lambda v: v.tensor_scalar(m_, P_sm.ap()[0:32, 0:TC],
                                                      1.0 / 128.0, None, ALU.mult), "dve", 1)
            p.on("vector", lambda v: v.tensor_mul(m_, m_, invd[0:32, :]))
            p.on("vector", lambda v: v.tensor_mul(m_, m_, invd[0:32, :]))
            p.inc("vector", lambda v: v.tensor_single_scalar(m_, m_, 1e-5,
                                                             ALU.add), "dve", 1, evt=f"k3ms_{bc}v")
            p.wait_evt("tensor", f"k3ms_{bc}v")
            p.inc("tensor", lambda t: t.nop(), "pe", 1, evt=f"k3ms_{bc}")
            st["psm_free"] = f"k3ms_{bc}v"
            p.on("vector", lambda v: v.memset(mgv, RSQRT_MAGIC))
            p.on("vector", lambda v: v.tensor_single_scalar(
                q1, tw.ap().bitcast(I32)[0:32, 0:TC], 1, ALU.arith_shift_right))
            p.on("vector", lambda v: v.tensor_sub(rsb.ap().bitcast(I32)[0:32, 0:TC], mgv, q1))
            p.on("vector", lambda v: v.tensor_scalar_mul(q2, m_, 0.5))
            for _ in range(3):
                p.on("vector", lambda v: v.tensor_mul(q3, rsb[0:32, :], rsb[0:32, :]))
                p.on("vector", lambda v: v.tensor_mul(q3, q3, q2))
                p.on("vector", lambda v: v.tensor_scalar(q3, q3, -1.0, 1.5, ALU.mult, ALU.add))
                p.on("vector", lambda v: v.tensor_mul(rsb[0:32, :], rsb[0:32, :], q3))
            p.inc("vector", lambda v: v.tensor_mul(rsb[0:32, :], rsb[0:32, :], invd[0:32, :]),
                  "dve", 1, evt=f"k3tot_{bc}")
            st["tot_evt"] = f"k3tot_{bc}"

            for g in range(NG3):
                gk = f"{bc}_{g}"
                for fl in range(G3):
                    ci = 16 + g * G3 + fl
                    pzb = fl % 2
                    zmm(ci, pzb, f"k3zg_{gk}_{fl}")
                    p.wait_evt("scalar", f"k3zg_{gk}_{fl}")
                    if fl == 0:
                        eng, evt = st["zseg_rd"]
                        if eng == "vector":
                            p.wait_evt("scalar", evt)
                        else:
                            p.wait_evt("scalar", evt)
                    p.inc("scalar", lambda s, fl=fl, pzb=pzb: s.copy(
                        zseg.ap()[:, fl, 3:3 + TC], P_z[pzb].ap()[:, 0:TC]), "act", 1,
                        evt=f"k3zgev_{gk}_{fl}")
                    st["pz_free"][pzb] = f"k3zgev_{gk}_{fl}"
                p.wait_evt("vector", f"k3zgev_{gk}_{G3 - 1}")
                p.inc("vector", lambda v, g=g: v.tensor_copy(
                    zseg.ap()[:, :, 0:3], hist3[:, 16 + g * G3:16 + (g + 1) * G3, :]), "dve", 1)
                for j in range(4):
                    ckb = ckv4[:, j, 16 + g * G3:16 + (g + 1) * G3, :].broadcast_to([128, G3, TC])
                    if j == 0:
                        p.inc("vector", lambda v, ckb=ckb: v.tensor_tensor(
                            zcb[:], zseg.ap()[:, :, 0:TC], ckb, ALU.mult), "dve", 1)
                    else:
                        p.inc("vector", lambda v, j=j, ckb=ckb: v.tensor_tensor(
                            Bt[:], zseg.ap()[:, :, j:j + TC], ckb, ALU.mult), "dve", 1)
                        p.inc("vector", lambda v: v.tensor_add(zcb[:], zcb[:], Bt[:]), "dve", 1)
                cbb = cbv[:, 16 + g * G3:16 + (g + 1) * G3, :].broadcast_to([128, G3, TC])
                p.inc("vector", lambda v, cbb=cbb: v.tensor_tensor(zcb[:], zcb[:], cbb, ALU.add),
                      "dve", 1)
                p.inc("vector", lambda v, g=g: v.tensor_copy(
                    hist3[:, 16 + g * G3:16 + (g + 1) * G3, :], zseg.ap()[:, :, TC:TC + 3]),
                    "dve", 1, evt=f"k3cvg_{gk}")
                st["zseg_rd"] = ("vector", f"k3cvg_{gk}")
                p.wait_evt("scalar", f"k3cvg_{gk}")
                p.wait_evt("scalar", st["scan_done"])
                if st["yg_last"] is not None:
                    p.wait_evt("scalar", st["yg_last"])
                p.inc("scalar", lambda s: s.activation(gateb, Zf[:, 0:G3 * TC], AF.Silu),
                      "act", 1, evt=f"k3silu_{gk}")
                st["silu"][g] = f"k3silu_{gk}"
                p.wait_evt("tensor", st["tot_evt"])
                sf_rd = st.get("sf_rd")
                for fl in range(G3):
                    ft = g * G3 + fl
                    bb = fl % 2
                    if st["pbc_free"][bb] is not None:
                        p.wait_evt("tensor", st["pbc_free"][bb])
                    p.inc("tensor", lambda t, ft=ft, bb=bb: t.matmul(
                        P_bc[bb].ap()[:, 0:TC], patft[:, ft * 128:(ft + 1) * 128], rsb[0:32, 0:TC],
                        start=True, stop=True, skip_group_check=True), "pe", 1,
                        evt=f"k3tbm_{gk}_{fl}")
                    p.wait_evt("scalar", f"k3tbm_{gk}_{fl}")
                    if fl == 0:
                        p.wait_evt("scalar", st["scan_done"])
                        if sf_rd is not None:
                            p.wait_evt("scalar", sf_rd)
                    p.inc("scalar", lambda s, fl=fl, bb=bb: s.copy(
                        St.ap()[:, fl, :], P_bc[bb].ap()[:, 0:TC]), "act", 1,
                        evt=f"k3tbe_{gk}_{fl}")
                    st["pbc_free"][bb] = f"k3tbe_{gk}_{fl}"
                p.wait_evt("vector", f"k3tbe_{gk}_{G3 - 1}")
                if g == 0:
                    p.wait_evt("vector", f"k3ssq_{bc}_{NG3 - 1}_1")
                if st["numfb_rd"] is not None:
                    p.wait_evt("vector", st["numfb_rd"])
                for ri in range(2):
                    p.inc("vector", lambda v, ri=ri, g=g: v.tensor_tensor(
                        numfb[ri][:], numbuf[ri][:, g * G3 * TC:(g + 1) * G3 * TC],
                        Sf[:, 0:G3 * TC], ALU.mult), "dve", 1,
                        evt=(f"k3yop_{gk}" if ri == 1 else None))
                st["sf_rd"] = f"k3yop_{gk}"
                for fl in range(G3):
                    ft = g * G3 + fl
                    bb = fl % 2
                    if st["pbc_free"][bb] is not None:
                        p.wait_evt("tensor", st["pbc_free"][bb])
                    if fl == 0:
                        p.wait_evt("tensor", f"k3yop_{gk}")
                    for ri, nm in ((0, "re"), (1, "im")):
                        p.inc("tensor", lambda t, ri=ri, nm=nm, fl=fl, bb=bb: t.matmul(
                            P_bc[bb].ap()[:, 0:TC], Wbd[nm][:],
                            numfb[ri][:, fl * TC:(fl + 1) * TC],
                            start=(ri == 0), stop=(ri == 1), skip_group_check=True), "pe", 1,
                            evt=(f"k3y_{gk}_{fl}" if ri == 1 else None))
                    p.wait_evt("vector", f"k3y_{gk}_{fl}")
                    p.wait_evt("vector", f"k3silu_{gk}")
                    if g == 0 and fl == 0 and st["out_rd"] is not None:
                        p.wait_evt("vector", st["out_rd"])
                    p.inc("vector", lambda v, ft=ft, fl=fl, bb=bb: v.tensor_tensor(
                        ygbuf[:, ft * TC:(ft + 1) * TC], P_bc[bb].ap()[:, 0:TC],
                        gateb[:, fl * TC:(fl + 1) * TC], ALU.mult), "dve", 1,
                        evt=f"k3yg_{gk}_{fl}")
                    st["pbc_free"][bb] = f"k3yg_{gk}_{fl}"
                st["numfb_rd"] = f"k3y_{gk}_{G3 - 1}"
                st["yg_last"] = f"k3yg_{gk}_{G3 - 1}"

            st["x2T_rd"] = f"k3zg_{bc}_{NG3 - 1}_{G3 - 1}"
            if c + 1 < NCH:
                build_x2T(b, c + 1, f"k3bx_{b}_{c + 1}")
            elif b + 1 < nb:
                build_x2T(b + 1, 0, f"k3bx_{b + 1}_0")

            p.wait_evt("tensor", st["yg_last"])
            for tt in range(TPC):
                gtile = c * TPC + tt
                ir = gtile % 2
                if st["ln_store2"][ir] is not None:
                    p.wait_evt("sync", st["ln_store2"][ir])
                p.inc("sync", lambda e, b=b, gtile=gtile, ir=ir: e.dma_start(
                    out=x2res[ir][:], in_=x2_in[b, gtile * 128:(gtile + 1) * 128, :]),
                    "k3xr", 16, evt=f"k3xr_{b}_{gtile}")
                if st["pout_free"] is not None:
                    p.wait_evt("tensor", st["pout_free"])
                for ft in range(NFT):
                    for nh in range(2):
                        p.inc("tensor", lambda t, ft=ft, nh=nh, tt=tt: t.matmul(
                            P_out.ap()[:, nh * 512:(nh + 1) * 512],
                            ygbuf[:, ft * TC + tt * 128: ft * TC + (tt + 1) * 128],
                            wout[:, ft * 1024 + nh * 512: ft * 1024 + (nh + 1) * 512],
                            start=(ft == 0), stop=(ft == NFT - 1), skip_group_check=True),
                            "pe", 1, evt=(f"k3o_{b}_{gtile}" if ft == NFT - 1 and nh == 1 else None))
                p.wait_evt("vector", f"k3o_{b}_{gtile}")
                p.wait_evt("vector", f"k3xr_{b}_{gtile}")
                p.inc("vector", lambda v, ir=ir: v.tensor_add(s_ln[:], P_out.ap()[:], x2res[ir][:]),
                      "dve", 1, evt=f"k3s_{b}_{gtile}v")
                st["pout_free"] = f"k3s_{b}_{gtile}v"
                emit_ln_store(p, ln, s_ln[:], x2res[ir][:],
                              [x3_out[b, gtile * 128:(gtile + 1) * 128, :]],
                              f"k3s_{b}_{gtile}v", f"k3ln_{b}_{gtile}",
                              war_evt=st["ln_store2"][ir])
                st["ln_store2"][ir] = f"k3ln_{b}_{gtile}_store"
            st["out_rd"] = f"k3o_{b}_{c * TPC + TPC - 1}"
        p.evt_here("st", f"k3_b{b}_done")
    p.wait_sem_now("sync", "st")


def build_k3():
    nc = bass.Bass("TRN2", target_bir_lowering=False)
    x2_in = nc.dram_tensor("x2", [B_PER_CORE, L, D], F32, kind="ExternalInput").ap()
    x3_out = nc.dram_tensor("x3", [B_PER_CORE, L, D], F32, kind="ExternalOutput").ap()
    shapes = {
        "in_proj": [1024, 4128], "out_proj": [2048, 1024], "theta_cols": [128, 16],
        "dist32n": [32, 2048], "patft": [32, 16 * 128], "sc_col": [32, 1],
        "convk_s": [32, 4], "convb_s": [32, 1], "convk_cols": [128, 4 * 32],
        "convb_cols": [128, 32], "decay": [1, 28], "anchor": [1, 4], "deriv": [1, 3],
        "W_re": [64, 64], "W_im": [64, 64], "ns_re": [64, 1], "ns_im": [64, 1],
    }
    W3 = {k: nc.dram_tensor(k, v, F32, kind="ExternalInput").ap() for k, v in shapes.items()}
    W3["bp16"] = nc.dram_tensor("bp16", [128, 16 * 32], BF16, kind="ExternalInput").ap()
    W3["ident"] = nc.dram_tensor("ident", [128, 128], BF16, kind="ExternalInput").ap()
    p = Prog(nc)
    emit_k3(nc, p, x2_in, x3_out, W3)
    p.run()
    return nc


def k3_inputs(inputs, core, x2):
    import ml_dtypes
    f32 = lambda x: np.ascontiguousarray(np.asarray(x, dtype=np.float32))
    theta = f32(inputs["L1_attn_theta"]).reshape(2048)
    ns = f32(inputs["L1_attn_norm_scale"])
    patft = np.zeros((32, 16 * 128), np.float32)
    for ft in range(16):
        for m in range(128):
            patft[2 * ft + m // 64, ft * 128 + m] = 1.0
    bp16 = np.zeros((128, 16 * 32), np.float32)
    for ft in range(16):
        for pp in range(128):
            bp16[pp, ft * 32 + 2 * ft + pp // 64] = 1.0
    dist = np.zeros((32, 2048), np.float32)
    pos = np.arange(2048, dtype=np.float32)
    dist[:28, :] = -(2047.0 - pos)[None, :]
    dist[28:, :] = -pos[None, :]
    ck = f32(inputs["L1_attn_conv_k"])[:, 0, :]
    d = {
        "x2": (np.ascontiguousarray(x2[core * B_PER_CORE:(core + 1) * B_PER_CORE])
               if x2 is not None else None),
        "in_proj": f32(inputs["L1_attn_in_proj_w"]),
        "out_proj": f32(inputs["L1_attn_out_proj_w"]),
        "theta_cols": np.ascontiguousarray(theta.reshape(16, 128).T),
        "dist32n": dist,
        "patft": patft,
        "bp16": bp16.astype(ml_dtypes.bfloat16),
        "sc_col": f32(inputs["L1_attn_score_scale"]).reshape(32, 1),
        "convk_s": np.ascontiguousarray(ck[:, 4096:4128].T),
        "convb_s": f32(inputs["L1_attn_conv_b"])[4096:4128].reshape(32, 1),
        "convk_cols": np.ascontiguousarray(
            np.concatenate([ck[j, :4096].reshape(32, 128).T for j in range(4)], axis=1)),
        "convb_cols": np.ascontiguousarray(
            f32(inputs["L1_attn_conv_b"])[:4096].reshape(32, 128).T),
        "decay": f32(inputs["L1_attn_decay_slopes"]).reshape(1, 28),
        "anchor": f32(inputs["L1_attn_anchor_slopes"]).reshape(1, 4),
        "deriv": f32(inputs["L1_attn_deriv_logits"]).reshape(1, 3),
        "W_re": f32(inputs["L1_attn_W_re"]),
        "W_im": f32(inputs["L1_attn_W_im"]),
        "ns_re": ns[:64].reshape(64, 1),
        "ns_im": ns[64:].reshape(64, 1),
        "ident": np.eye(128, dtype=ml_dtypes.bfloat16),
    }
    return d




NE4 = 18


def emit_k4(nc, p, x3_in, x4_out, W4):
    a = nc.alloc_sbuf_tensor
    ap_ = nc.alloc_psum_tensor
    NTT = B_PER_CORE * NT
    wT_hbm = nc.dram_tensor("k4_wT", [NE4, 128, 12288], BF16, kind="Internal").ap()

    id_t = a("k4_id", [128, 128], BF16)
    id_f = a("k4_idf", [128, 128], F32)
    gateT = a("k4_gateT", [128, 128], F32)
    comb = a("k4_comb", [128, NTT * 16], F32)
    ones_col = a("k4_ones", [128, 1], F32)

    probs = a("k4_probs", [128, 16], F32)
    pb8 = a("k4_pb8", [128, 32], F32)
    mxb4 = a("k4_mxb4", [128, 32], F32)
    gsc8 = a("k4_gsc8", [128, 8], F32)
    gmx = a("k4_gmx", [128, 8], F32)
    gmxb = a("k4_gmxb", [128, 8], F32)
    alw = a("k4_alw", [128, 4], F32)
    msk = a("k4_msk", [128, 16], F32)
    mx4 = a("k4_mx4", [128, 8], F32)
    mx4b = a("k4_mx4b", [128, 8], F32)
    wsel = a("k4_wsel", [128, 16], F32)
    wselb = a("k4_wselb", [128, 16], F32)
    gsum = a("k4_gsum", [128, 1], F32)
    gsumb = a("k4_gsumb", [128, 1], F32)
    grc = a("k4_grc", [128, 1], F32)
    grc2 = a("k4_grc2", [128, 1], F32)
    NCH4 = L // 512
    x3T = a("k4_x3T", [128, 8 * L], BF16)
    xb = a("k4_xb", [128, 1024], BF16)
    wbuf = [a(f"k4_wbuf{i}", [128, 12288], BF16) for i in range(2)]
    sgT = [a(f"k4_sgT{i}", [128, 512], BF16) for i in range(2)]
    guT = [a(f"k4_guT{i}", [128, 4 * 512], BF16) for i in range(2)]
    acc = a("k4_acc", [128, NT * 1024], F32)
    x3res = [a(f"k4_x3res{i}", [128, 1024], F32) for i in range(2)]
    junkb = [a(f"k4_junk{i}", [128, 1024], F32) for i in range(2)]
    junk = junkb[0]
    lsum16 = a("k4_lsum16", [128, NT], F32)
    lsq16 = a("k4_lsq16", [128, NT], F32)
    lmean16 = a("k4_lmean16", [128, NT], F32)
    lmz16 = a("k4_lmz16", [128, NT], F32)
    lvar16 = a("k4_lvar16", [128, NT], F32)
    lrs16 = a("k4_lrs16", [128, NT], F32)
    xTf = acc.ap()[:, 0:1024]
    s_ln = a("k4_sln", [128, 1024], F32)
    ln = LNPack(nc, "k4_")

    P_gu = [ap_(f"k4P_gu{i}", [128, 512], F32) for i in range(4)]
    P_oo = [ap_(f"k4P_oo{i}", [128, 1024], F32) for i in range(2)]
    P_tp = P_gu[0]
    P_sel = P_gu[1]
    tp_bf = P_tp.ap().bitcast(BF16)

    nb = x3_in.shape[0]

    p.inc("sync", lambda e: e.dma_start(out=id_t[:], in_=W4["ident"]), "init", 16, evt="k4id")
    p.inc("sync", lambda e: e.dma_start(out=id_f[:], in_=W4["ident_f"]), "init2", 16, evt="k4idf")
    p.inc("gpsimd", lambda e: e.dma_start(out=gateT[:], in_=W4["gateT"]), "k4small", 16,
          evt="k4gnat")
    p.on("vector", lambda v: v.memset(ones_col[:], 1.0))
    p.on("vector", lambda v: v.memset(pb8[:], -1e30))
    p.inc("vector", lambda v: v.memset(gsc8[:], -1e30), "dve", 1, evt="k4init_v")
    p.wait_evt("tensor", "k4gnat")
    p.wait_evt("tensor", "k4idf")
    p.wait_evt("tensor", "k4id")
    for e in range(NE4):
        p.inc("gpsimd", lambda q, e=e: q.dma_start(out=wT_hbm[e], in_=W4["wTf"][e]),
              "k4wc", 16, evt=f"k4cast_{e}")
    p.evt_here("pe", "k4prep")

    for ti in range(NTT):
        xsrc = x3_in[ti // NT, (ti % NT) * 128:((ti % NT) + 1) * 128, :]
        if ti > 0:
            p.wait_evt("sync", f"k4g_{ti - 1}_tp1")
        p.inc("sync", lambda e, xsrc=xsrc: e.dma_start(out=junk[:], in_=xsrc), f"k4g{ti % 2}", 16,
              evt=f"k4g_{ti}_ld")
        for g in range(2):
            p.wait_evt("tensor", f"k4g_{ti}_ld")
            if ti > 0 and g == 0:
                p.wait_evt("tensor", f"k4g_{ti - 1}_xT")
            for jj in range(4):
                j = g * 4 + jj
                p.inc("tensor", lambda t, j=j, jj=jj: t.matmul(
                    P_tp.ap()[:, jj * 128:(jj + 1) * 128], junk[:, j * 128:(j + 1) * 128], id_f[:],
                    start=(jj == 0), stop=True, is_transpose=True, skip_group_check=True),
                    "pe", 1, evt=f"k4g_{ti}_tp{g}")
            p.wait_evt("vector", f"k4g_{ti}_tp{g}")
            if ti > 0 and g == 0:
                p.wait_evt("vector", f"k4g_{ti - 1}_mm")
            p.inc("vector", lambda v, g=g: v.tensor_copy(xTf[:, g * 512:(g + 1) * 512], P_tp.ap()[:, 0:512]),
                  "dve", 1, evt=f"k4g_{ti}_xT")
            p.wait_evt("tensor", f"k4g_{ti}_xT")
        if ti > 0:
            p.wait_evt("tensor", f"k4g_{ti - 1}_sig")
        for j in range(8):
            p.inc("tensor", lambda t, j=j: t.matmul(
                P_sel.ap()[:, 0:16], xTf[:, j * 128:(j + 1) * 128], gateT[:, j * 16:(j + 1) * 16],
                start=(j == 0), stop=(j == 7), skip_group_check=True),
                "pe", 1, evt=f"k4g_{ti}_mm")
        p.wait_evt("scalar", f"k4g_{ti}_mm")
        p.inc("scalar", lambda s: s.activation(probs[:], P_sel.ap()[:, 0:16], AF.Sigmoid),
              "act", 1, evt=f"k4g_{ti}_sig")
        p.wait_evt("vector", f"k4g_{ti}_sig")
        if ti == 0:
            p.wait_evt("vector", "k4init_v")
        p.inc("vector", lambda v: v.tensor_copy(
            pb8.ap().rearrange("q (g w) -> q g w", g=4)[:, :, 0:4],
            probs.ap().rearrange("q (g w) -> q g w", g=4)), "dve", 1)
        for g in range(4):
            p.inc("vector", lambda v, g=g: v.max(mxb4[:, g * 8:(g + 1) * 8],
                                                 pb8[:, g * 8:(g + 1) * 8]), "dve", 1,
                  evt=f"k4g_{ti}_m8")
        p.wait_evt("scalar", f"k4g_{ti}_m8")
        p.inc("scalar", lambda s: s.copy(mx4b[:], mxb4[:, 0:8]), "act", 1,
              evt=f"k4g_{ti}_m8b")
        p.wait_evt("vector", f"k4g_{ti}_m8b")
        p.inc("vector", lambda v: v.tensor_add(
            gsc8[:, 0:4],
            mxb4.ap().rearrange("q (g w) -> q g w", g=4)[:, :, 0:1],
            mxb4.ap().rearrange("q (g w) -> q g w", g=4)[:, :, 1:2]), "dve", 1,
            evt=f"k4g_{ti}_gsc")
        p.inc("vector", lambda v: v.max(gmx[:], gsc8[:, 0:8]), "dve", 1, evt=f"k4g_{ti}_gmx")
        p.wait_evt("scalar", f"k4g_{ti}_gmx")
        p.inc("scalar", lambda s: s.copy(gmxb[:], gmx[:]), "act", 1, evt=f"k4g_{ti}_gmxb")
        p.wait_evt("vector", f"k4g_{ti}_gmxb")
        p.inc("vector", lambda v: v.tensor_tensor(
            alw[:], gsc8[:, 0:4], gmxb[:, 1:2].broadcast_to([128, 4]), ALU.is_ge), "dve", 1)
        p.inc("vector", lambda v: v.tensor_scalar(alw[:], alw[:], 1.0, 1e9,
                                                  ALU.subtract, ALU.mult), "dve", 1)
        p.inc("vector", lambda v: v.tensor_tensor(
            msk.ap().rearrange("q (g w) -> q g w", g=4),
            probs.ap().rearrange("q (g w) -> q g w", g=4),
            alw.ap().rearrange("q (g w) -> q g w", g=4).broadcast_to([128, 4, 4]),
            ALU.add), "dve", 1)
        p.inc("vector", lambda v: v.max(mx4[:], msk[:]), "dve", 1, evt=f"k4g_{ti}_mx4")
        p.wait_evt("scalar", f"k4g_{ti}_mx4")
        p.inc("scalar", lambda s: s.copy(mx4b[:], mx4[:]), "act", 1, evt=f"k4g_{ti}_mx4b")
        p.wait_evt("vector", f"k4g_{ti}_mx4b")
        p.inc("vector", lambda v: v.scalar_tensor_tensor(
            wsel[:], msk[:], mx4b[:, 3:4], probs[:], ALU.is_ge, ALU.mult), "dve", 1,
            evt=f"k4g_{ti}_ws")
        p.wait_evt("scalar", f"k4g_{ti}_ws")
        p.inc("scalar", lambda s: s.copy(wselb[:], wsel[:]), "act", 1, evt=f"k4g_{ti}_wsb")
        p.wait_evt("vector", f"k4g_{ti}_wsb")
        p.inc("vector", lambda v: v.reduce_sum(gsum[:], wselb[:], axis=AX.X), "dve", 1,
              evt=f"k4g_{ti}_gs")
        p.wait_evt("scalar", f"k4g_{ti}_gs")
        p.inc("scalar", lambda s: s.activation(gsumb[:], gsum[:], AF.Relu), "act", 1,
              evt=f"k4g_{ti}_gsb")
        p.wait_evt("vector", f"k4g_{ti}_gsb")
        p.inc("vector", lambda v: v.tensor_single_scalar(grc[:], gsumb[:], 1e-6, ALU.add),
              "dve", 1, evt=f"k4g_{ti}_cl")
        p.wait_evt("scalar", f"k4g_{ti}_cl")
        p.inc("scalar", lambda s: s.copy(grc2[:], grc[:]), "act", 1, evt=f"k4g_{ti}_clb")
        p.wait_evt("vector", f"k4g_{ti}_clb")
        p.inc("vector", lambda v: v.reciprocal(grc[:], grc2[:]), "dve", 1, evt=f"k4g_{ti}_rc")
        p.wait_evt("scalar", f"k4g_{ti}_rc")
        p.inc("scalar", lambda s: s.copy(grc2[:], grc[:]), "act", 1, evt=f"k4g_{ti}_rcb")
        p.wait_evt("vector", f"k4g_{ti}_rcb")
        p.inc("vector", lambda v, ti=ti: v.tensor_tensor(
            comb[:, ti * 16:(ti + 1) * 16], wsel[:],
            grc2[:, 0:1].broadcast_to([128, 16]), ALU.mult), "dve", 1, evt=f"k4g_{ti}_done")
    p.evt_here("dve", "k4gates")

    k4_jstore = [None, None]
    fhist_sg = []
    fhist_gu = []
    ohist = []
    for b in range(nb):
        for i in range(NT):
            if b > 0 and i == 0:
                p.wait_evt("gpsimd", f"k4xb_{b - 1}_{NT - 1}tp")
            if i > 0:
                p.wait_evt("gpsimd", f"k4xb_{b}_{i - 1}tp")
            p.inc("gpsimd", lambda g, b=b, i=i: g.dma_start(
                out=xb[:], in_=x3_in[b, i * 128:(i + 1) * 128, :]), f"k4xb{i % 2}", 16,
                evt=f"k4xb_{b}_{i}")
            p.wait_evt("tensor", f"k4xb_{b}_{i}")
            if b == 0 and i == 0:
                p.wait_evt("tensor", "k4gates")
                p.wait_evt("tensor", "k4prep")
            if b > 0 and i == 0:
                p.wait_evt("tensor", fhist_sg[-1])
                p.wait_evt("tensor", fhist_sg[-2])
            for g in range(2):
                for jj in range(4):
                    j = g * 4 + jj
                    p.inc("tensor", lambda t, j=j, jj=jj: t.matmul(
                        tp_bf[:, jj * 128:(jj + 1) * 128], xb[:, j * 128:(j + 1) * 128], id_t[:],
                        start=(jj == 0), stop=True, is_transpose=True, skip_group_check=True),
                        "pe", 1, evt=f"k4tp_{b}_{i}_{g}")
                p.wait_evt("vector", f"k4tp_{b}_{i}_{g}")
                if b > 0 and i == 0 and g == 0:
                    p.wait_evt("vector", f"k4mm_{b - 1}_done")
                p.inc("vector", lambda v, g=g, i=i: v.tensor_copy(
                    x3T.ap().rearrange("q (hc t) -> q hc t", hc=8)[:, g * 4:(g + 1) * 4,
                                                                   i * 128:(i + 1) * 128],
                    tp_bf[:, 0:512].rearrange("q (hc t) -> q hc t", hc=4)),
                    "dve", 1, evt=f"k4x3T_{b}_{i}_{g}")
                p.wait_evt("tensor", f"k4x3T_{b}_{i}_{g}")
            p.evt_here("pe", f"k4xb_{b}_{i}tp")
        p.evt_here("dve", f"k4x3T_{b}")
        w2_pend = [None]
        for e in range(NE4):
            eb = e % 2
            if b == 0 and e < 2:
                p.wait_evt("gpsimd", f"k4cast_{e}")
            if e < 2:
                if b > 0:
                    p.wait_evt("gpsimd", f"k4w_{b - 1}_{NE4 - 2 + e}_used")
            else:
                p.wait_evt("gpsimd", f"k4w_{b}_{e - 2}_used")
            p.inc("gpsimd", lambda g, e=e, eb=eb: g.dma_start(
                out=wbuf[eb][:], in_=wT_hbm[e]), f"k4w{eb}", 16, evt=f"k4w_{b}_{e}")
            for tc in range(NCH4):
                for ftile in range(4):
                    fseq = len(fhist_sg)
                    gb = P_gu[(2 * fseq) % 4]
                    ub = P_gu[(2 * fseq + 1) % 4]
                    sg = sgT[fseq % 2]
                    if e == 0 and tc == 0 and ftile == 0:
                        p.wait_evt("tensor", f"k4x3T_{b}")
                    if tc == 0 and ftile == 0:
                        p.wait_evt("tensor", f"k4w_{b}_{e}")
                    if fseq >= 2:
                        p.wait_evt("tensor", fhist_sg[fseq - 2])
                    for hc in range(8):
                        p.inc("tensor", lambda t, eb=eb, ftile=ftile, hc=hc, tc=tc, gb=gb: t.matmul(
                            gb.ap()[:, 0:512],
                            wbuf[eb][:, hc * 512 + ftile * 128: hc * 512 + (ftile + 1) * 128],
                            x3T[:, hc * L + tc * 512: hc * L + (tc + 1) * 512],
                            start=(hc == 0), stop=(hc == 7), skip_group_check=True),
                            "pe", 1, evt=(f"k4gm_{b}_{e}_{tc}_{ftile}" if hc == 7 else None))
                    if fseq >= 2:
                        p.wait_evt("tensor", fhist_gu[fseq - 2])
                    for hc in range(8):
                        p.inc("tensor", lambda t, eb=eb, ftile=ftile, hc=hc, tc=tc, ub=ub: t.matmul(
                            ub.ap()[:, 0:512],
                            wbuf[eb][:, 4096 + hc * 512 + ftile * 128: 4096 + hc * 512 + (ftile + 1) * 128],
                            x3T[:, hc * L + tc * 512: hc * L + (tc + 1) * 512],
                            start=(hc == 0), stop=(hc == 7), skip_group_check=True),
                            "pe", 1, evt=(f"k4um_{b}_{e}_{tc}_{ftile}" if hc == 7 else None))
                    p.wait_evt("scalar", f"k4gm_{b}_{e}_{tc}_{ftile}")
                    if fseq >= 2:
                        p.wait_evt("scalar", fhist_gu[fseq - 2])
                    p.inc("scalar", lambda s, sg=sg, gb=gb: s.activation(sg[:], gb.ap()[:], AF.Silu),
                          "act", 1, evt=f"k4sg_{b}_{e}_{tc}_{ftile}")
                    fhist_sg.append(f"k4sg_{b}_{e}_{tc}_{ftile}")
                    p.wait_evt("vector", f"k4sg_{b}_{e}_{tc}_{ftile}")
                    p.wait_evt("vector", f"k4um_{b}_{e}_{tc}_{ftile}")
                    p.inc("vector", lambda v, ftile=ftile, sg=sg, ub=ub, gub=guT[tc % 2]:
                          v.tensor_tensor(
                        gub[:, ftile * 512:(ftile + 1) * 512], sg[:], ub.ap()[:, 0:512],
                        ALU.mult), "dve", 1, evt=f"k4gu_{b}_{e}_{tc}_{ftile}v")
                    fhist_gu.append(f"k4gu_{b}_{e}_{tc}_{ftile}v")

                def w2_emit(e=e, eb=eb, tc=tc, b=b, gub=guT[tc % 2]):
                    p.wait_evt("tensor", f"k4gu_{b}_{e}_{tc}_3v")
                    if e == 0 and tc == 0 and b > 0:
                        p.wait_evt("vector", f"k4est_{b - 1}")
                    for tt in range(4):
                        gt = tc * 4 + tt
                        oseq = len(ohist)
                        ob = P_oo[oseq % 2]
                        if oseq >= 2:
                            p.wait_evt("tensor", ohist[oseq - 2])
                        for fc in range(4):
                            for nh in range(2):
                                p.inc("tensor", lambda t, fc=fc, nh=nh, tt=tt, ob=ob: t.matmul(
                                    ob.ap()[:, nh * 512:(nh + 1) * 512],
                                    gub[:, fc * 512 + tt * 128: fc * 512 + (tt + 1) * 128],
                                    wbuf[eb][:, 8192 + fc * 1024 + nh * 512: 8192 + fc * 1024 + (nh + 1) * 512],
                                    start=(fc == 0), stop=(fc == 3), skip_group_check=True),
                                    "pe", 1,
                                    evt=(f"k4o_{b}_{e}_{tc}_{tt}" if fc == 3 and nh == 1 else None))
                        p.wait_evt("vector", f"k4o_{b}_{e}_{tc}_{tt}")
                        ti = b * NT + tc * 4 + tt
                        if e < 16:
                            ccol = comb[:, ti * 16 + e: ti * 16 + e + 1]
                        else:
                            ccol = ones_col[:, 0:1]
                        if e == 0:
                            p.inc("vector", lambda v, gt=gt, ccol=ccol, ob=ob: v.tensor_scalar(
                                acc[:, gt * 1024:(gt + 1) * 1024], ob.ap()[:], ccol, None, ALU.mult),
                                "dve", 1, evt=f"k4acc_{b}_{e}_{tc}_{tt}v")
                        else:
                            p.inc("vector", lambda v, gt=gt, ccol=ccol, ob=ob: v.scalar_tensor_tensor(
                                acc[:, gt * 1024:(gt + 1) * 1024], ob.ap()[:], ccol,
                                acc[:, gt * 1024:(gt + 1) * 1024], ALU.mult, ALU.add),
                                "dve", 1, evt=f"k4acc_{b}_{e}_{tc}_{tt}v")
                        ohist.append(f"k4acc_{b}_{e}_{tc}_{tt}v")
                    if tc == NCH4 - 1:
                        p.evts[f"k4w_{b}_{e}_used"] = p.evts[f"k4o_{b}_{e}_{NCH4 - 1}_3"]
                if w2_pend[0] is not None:
                    w2_pend[0]()
                w2_pend[0] = w2_emit
        while w2_pend[0] is not None:
            fn, w2_pend[0] = w2_pend[0], None
            fn()
        p.evts[f"k4mm_{b}_done"] = p.evts[f"k4o_{b}_{NE4 - 1}_{NCH4 - 1}_3"]
        for i in range(NT):
            ir = i % 2
            prev2 = (b, i - 2) if i >= 2 else ((b - 1, NT - 2 + i) if b > 0 else None)
            if prev2 is not None:
                p.wait_evt("sync", f"k4sres_{prev2[0]}_{prev2[1]}")
            if b == 0 and i == 0:
                p.wait_evt("sync", "k4gates")
            p.inc("sync", lambda e, b=b, i=i, ir=ir: e.dma_start(
                out=x3res[ir][:], in_=x3_in[b, i * 128:(i + 1) * 128, :]), "k4xr", 16,
                evt=f"k4xr_{b}_{i}")
            tc, tt = divmod(i, 4)
            p.wait_evt("vector", f"k4acc_{b}_{NE4 - 1}_{tc}_{tt}v")
            p.wait_evt("vector", f"k4xr_{b}_{i}")
            p.inc("vector", lambda v, i=i, ir=ir: v.tensor_add(
                acc[:, i * 1024:(i + 1) * 1024], acc[:, i * 1024:(i + 1) * 1024],
                x3res[ir][:]), "dve", 1, evt=f"k4s_{b}_{i}")
            p.evts[f"k4sres_{b}_{i}"] = p.evts[f"k4s_{b}_{i}"]
        acc3 = acc.ap().rearrange("q (i t) -> q i t", i=NT)
        p.inc("vector", lambda v: v.reduce_sum(lsum16[:], acc3, axis=AX.X), "dve", 1,
              evt=f"k4es_{b}")
        for i in range(NT):
            p.wait_evt("scalar", f"k4s_{b}_{i}")
            if b > 0 and i == 0:
                p.wait_evt("scalar", f"k4est_{b - 1}")
            p.inc("scalar", lambda s, i=i: s.activation(
                junk[:], acc[:, i * 1024:(i + 1) * 1024], AF.Square,
                accum_out=lsq16[:, i:i + 1]), "act", 1,
                evt=(f"k4esq_{b}" if i == NT - 1 else None))
        p.inc("vector", lambda v: v.tensor_scalar_mul(lmean16[:], lsum16[:], 1.0 / 1024),
              "dve", 1)
        p.inc("vector", lambda v: v.tensor_mul(lmz16[:], lmean16[:], lmean16[:]), "dve", 1)
        p.wait_evt("vector", f"k4esq_{b}")
        p.inc("vector", lambda v: v.tensor_scalar(lvar16[:], lsq16[:], 1.0 / 1024, 1e-5,
                                                  ALU.mult, ALU.add), "dve", 1)
        p.inc("vector", lambda v: v.tensor_sub(lvar16[:], lvar16[:], lmz16[:]), "dve", 1,
              evt=f"k4ev_{b}")
        p.wait_evt("scalar", f"k4ev_{b}")
        p.inc("scalar", lambda s: s.activation(lvar16[:], lvar16[:], AF.Ln), "act", 1)
        p.self_wait("scalar", "act")
        p.inc("scalar", lambda s: s.activation(lrs16[:], lvar16[:], AF.Exp, scale=-0.5),
              "act", 1, evt=f"k4er_{b}")
        p.wait_evt("vector", f"k4er_{b}")
        p.inc("vector", lambda v: v.tensor_tensor(
            acc3, acc3, lmean16.ap().rearrange("q (i o) -> q i o", i=NT)
            .broadcast_to([128, NT, 1024]), ALU.subtract), "dve", 1)
        p.inc("vector", lambda v: v.tensor_tensor(
            acc3, acc3, lrs16.ap().rearrange("q (i o) -> q i o", i=NT)
            .broadcast_to([128, NT, 1024]), ALU.mult), "dve", 1, evt=f"k4en_{b}")
        p.wait_evt("sync", f"k4en_{b}")
        p.inc("sync", lambda e, b=b: e.dma_start(
            out=x4_out[b].rearrange("(i q) t -> q i t", q=128), in_=acc3), "st", 16,
            evt=f"k4est_{b}")
    p.wait_sem_now("sync", "st")


def tile_xy_k4(ti, x3_in, x4_out):
    b, i = divmod(ti, NT)
    return [(x3_in[b, i * 128:(i + 1) * 128, :], x4_out[b, i * 128:(i + 1) * 128, :])]


def build_k4():
    nc = bass.Bass("TRN2", target_bir_lowering=False)
    x3_in = nc.dram_tensor("x3", [B_PER_CORE, L, D], F32, kind="ExternalInput").ap()
    x4_out = nc.dram_tensor("x4", [B_PER_CORE, L, D], F32, kind="ExternalOutput").ap()
    W4 = {
        "gateT": nc.dram_tensor("gateT_h", [128, 128], F32, kind="ExternalInput").ap(),
        "wTf": nc.dram_tensor("wTf", [18, 128, 12288], F32, kind="ExternalInput").ap(),
        "ident": nc.dram_tensor("ident", [128, 128], BF16, kind="ExternalInput").ap(),
        "ident_f": nc.dram_tensor("ident_f", [128, 128], F32, kind="ExternalInput").ap(),
    }
    p = Prog(nc)
    emit_k4(nc, p, x3_in, x4_out, W4)
    p.run()
    return nc


def _k4_wTf(inputs):
    f32 = lambda x: np.asarray(x, dtype=np.float32)

    def pA(w):
        R, C = w.shape
        return w.reshape(R, C // 128, 128).transpose(2, 1, 0).reshape(128, R * C // 128)

    w1_e, w3_e = f32(inputs["L1_ffn_w1_e"]), f32(inputs["L1_ffn_w3_e"])
    w2_e = f32(inputs["L1_ffn_w2_e"])
    w1_s, w3_s = f32(inputs["L1_ffn_w1_s"]), f32(inputs["L1_ffn_w3_s"])
    w2_s = f32(inputs["L1_ffn_w2_s"])
    out = np.empty((18, 128, 12288), np.float32)
    for e in range(16):
        out[e, :, 0:4096] = pA(w1_e[e])
        out[e, :, 4096:8192] = pA(w3_e[e])
        out[e, :, 8192:12288] = pA(w2_e[e])
    w2sA = pA(w2_s)
    for s in range(2):
        out[16 + s, :, 0:4096] = pA(w1_s[s * 512:(s + 1) * 512])
        out[16 + s, :, 4096:8192] = pA(w3_s[s * 512:(s + 1) * 512])
        out[16 + s, :, 8192:12288] = w2sA[:, s * 4096:(s + 1) * 4096]
    return out


def k4_inputs(inputs, core, x3):
    f32 = lambda x: np.ascontiguousarray(np.asarray(x, dtype=np.float32))
    return {
        "x3": (np.ascontiguousarray(x3[core * B_PER_CORE:(core + 1) * B_PER_CORE])
               if x3 is not None else None),
        "gateT_h": np.ascontiguousarray(
            f32(inputs["L1_ffn_w_gate"]).reshape(16, 8, 128).transpose(2, 1, 0).reshape(128, 128)),
        "wTf": _k4_wTf(inputs),
        "ident": _ident_bf16(),
        "ident_f": np.eye(128, dtype=np.float32),
    }


def all_engine_fence(p):
    for eng in Prog.ENGS:
        if eng == "tensor":
            p.inc(eng, lambda t: t.nop(), "fence", 1)
        elif eng == "vector":
            p.inc(eng, lambda v: v.nop(), "fence", 1)
        elif eng == "scalar":
            p.inc(eng, lambda s: s.nop(), "fence", 1)
        elif eng == "sync":
            p.inc(eng, lambda e: e.nop(), "fence", 1)
        else:
            p.inc(eng, lambda g: g.nop(), "fence", 1)
    for eng in Prog.ENGS:
        p.wait_sem_now(eng, "fence")


def build_fused():
    nc = bass.Bass("TRN2", target_bir_lowering=False)
    x0 = nc.dram_tensor("x0", [B_PER_CORE, L, D], F32, kind="ExternalInput").ap()
    x4 = nc.dram_tensor("x4", [B_PER_CORE, L, D], F32, kind="ExternalOutput").ap()
    x1 = nc.dram_tensor("x1i", [B_PER_CORE, L, D], F32, kind="Internal").ap()
    x2 = nc.dram_tensor("x2i", [B_PER_CORE, L, D], F32, kind="Internal").ap()
    x3 = nc.dram_tensor("x3i", [B_PER_CORE, L, D], F32, kind="Internal").ap()
    ident = nc.dram_tensor("ident", [128, 128], BF16, kind="ExternalInput").ap()
    ident_f = nc.dram_tensor("ident_f", [128, 128], F32, kind="ExternalInput").ap()
    W1 = {}
    for n in ("Wq", "Wk", "Wv", "Wo"):
        W1[n + "T"] = nc.dram_tensor(n + "T", [128, 8192], F32, kind="ExternalInput").ap()
    for n in ("bq", "bk", "bv", "bo"):
        W1[n] = nc.dram_tensor(n, [1024], F32, kind="ExternalInput").ap()
    W2 = {
        "selT": nc.dram_tensor("selT", [128, 128], F32, kind="ExternalInput").ap(),
        "keys": nc.dram_tensor("keys", [16, 1024, 256], F32, kind="ExternalInput").ap(),
        "values": nc.dram_tensor("values", [16, 256, 1024], F32, kind="ExternalInput").ap(),
        "rs_col": nc.dram_tensor("rs_col", [128, 1], F32, kind="ExternalInput").ap(),
        "ident_f": ident_f,
    }
    shapes3 = {
        "in_proj": [1024, 4128], "out_proj": [2048, 1024], "theta_cols": [128, 16],
        "dist32n": [32, 2048], "patft": [32, 16 * 128], "sc_col": [32, 1],
        "convk_s": [32, 4], "convb_s": [32, 1], "convk_cols": [128, 4 * 32],
        "convb_cols": [128, 32], "decay": [1, 28], "anchor": [1, 4], "deriv": [1, 3],
        "W_re": [64, 64], "W_im": [64, 64], "ns_re": [64, 1], "ns_im": [64, 1],
    }
    W3 = {k: nc.dram_tensor(k, v, F32, kind="ExternalInput").ap() for k, v in shapes3.items()}
    W3["bp16"] = nc.dram_tensor("bp16", [128, 16 * 32], BF16, kind="ExternalInput").ap()
    W3["ident"] = ident
    W4 = {
        "gateT": nc.dram_tensor("gateT_h", [128, 128], F32, kind="ExternalInput").ap(),
        "wTf": nc.dram_tensor("wTf", [18, 128, 12288], F32, kind="ExternalInput").ap(),
        "ident": ident,
        "ident_f": ident_f,
    }
    p = Prog(nc)
    sb0, ps0 = nc.sbuf_base, nc.psum_base
    emit_k1(nc, p, x0, x1, W1, ident)
    all_engine_fence(p)
    nc.sbuf_base, nc.psum_base = sb0, ps0
    emit_k2(nc, p, x1, x2, W2, ident)
    all_engine_fence(p)
    nc.sbuf_base, nc.psum_base = sb0, ps0
    emit_k3(nc, p, x2, x3, W3)
    all_engine_fence(p)
    nc.sbuf_base, nc.psum_base = sb0, ps0
    emit_k4(nc, p, x3, x4, W4)
    p.run()
    return nc


def fused_inputs(inputs, core):
    d = k1_inputs(inputs, core)
    d["x0"] = np.ascontiguousarray(inputs["L0_attn_query"][core * B_PER_CORE:(core + 1) * B_PER_CORE])
    d2 = k2_inputs(inputs, core, None)
    d2.pop("x1")
    d3 = k3_inputs(inputs, core, None)
    d3.pop("x2")
    d4 = k4_inputs(inputs, core, None)
    d4.pop("x3")
    d.update(d2)
    d.update(d3)
    d.update(d4)
    return d


def kernel(**inputs):
    cores = list(range(NCORE))
    nc = build_fused()
    res = run_bass_kernel_spmd(nc, [fused_inputs(inputs, c) for c in cores], core_ids=cores)
    return np.concatenate([r["x4"] for r in res.results], axis=0)
